# version 27
# baseline (speedup 1.0000x reference)
import numpy as np
import ml_dtypes
import concourse.bass as bass
import concourse.mybir as mybir
from concourse.bass_utils import run_bass_kernel_spmd

F32 = mybir.dt.float32
BF16 = mybir.dt.bfloat16
AF = mybir.ActivationFunctionType
ALU = mybir.AluOpType
AX = mybir.AxisListType

T = 2048
D = 1024
H = 16
HD = 64
NT = 16
NK = 8
NEG = -1.0e10
ACT_SET_NL_EXP = 6


class Ctr:
    def __init__(self, h):
        self.h = h
        self.n = 0


_HW = {}


def _wait(eng, ctr, val):
    assert val <= ctr.n, f"wait on future token {val} > {ctr.n}"
    if val <= 0:
        return
    key = (id(eng), id(ctr))
    if _HW.get(key, 0) >= val:
        return
    _HW[key] = val
    eng.wait_ge(ctr.h, val)


def fap(t, offset, dims):
    return bass.AP(t, offset, dims)


PK_CLT = 0
PK_CUT = 128
PK_IDN = 256
PK_MR = 384
PACK_N = 640


def build_nc():
    _HW.clear()
    nc = bass.Bass("TRN2", target_bir_lowering=False,
                   detect_race_conditions=False)

    x_hbm = nc.dram_tensor("x", [T, D], F32, kind="ExternalInput").ap()
    out_hbm = nc.dram_tensor("out", [T, D], F32, kind="ExternalOutput").ap()
    wgT_h = nc.dram_tensor("wgT", [D, D], F32, kind="ExternalInput").ap()
    wvT_h = nc.dram_tensor("wvT", [D, D], F32, kind="ExternalInput").ap()
    woT_h = nc.dram_tensor("woT", [D, D], F32, kind="ExternalInput").ap()
    wfT_h = nc.dram_tensor("wfT", [D, D], F32, kind="ExternalInput").ap()
    pack_h = nc.dram_tensor("pack", [128, PACK_N], F32,
                            kind="ExternalInput").ap()
    bias_h = nc.dram_tensor("biases", [1, 4 * D], F32,
                            kind="ExternalInput").ap()
    lu_h = nc.dram_tensor("lu", [50, 4], F32, kind="ExternalInput").ap()
    ld_h = nc.dram_tensor("ld", [4, 100], F32, kind="ExternalInput").ap()
    uaT_h = nc.dram_tensor("uaT", [50, D], F32, kind="ExternalInput").ap()
    daT_h = nc.dram_tensor("daT", [128, 800], F32, kind="ExternalInput").ap()

    A = nc.alloc_sbuf_tensor
    x_sb = A("x_sb", [128, NT, D], F32)
    gh_sb = A("gh_sb", [128, NT, D], BF16)
    V_sb = A("V_sb", [128, NT, H * 65], BF16)
    Wa = A("Wa", [128, NK, D], BF16)
    Wb = A("Wb", [128, NK, D], BF16)
    lnx_roll = A("lnx_roll", [128, 2, D], BF16)
    lnxT_win = A("lnxT_win", [128, NK, 2, 128], BF16, is_transpose_dest=True)
    ghT_win = A("ghT_win", [128, NK, 2, 128], BF16, is_transpose_dest=True)
    gtT_win = A("gtT_win", [128, NK, 2, 128], BF16, is_transpose_dest=True)
    gt_roll = A("gt_roll", [128, 2, D], BF16)
    raw_roll = A("raw_roll", [128, 2, D], BF16)
    rawT_win = A("rawT_win", [128, NK, 1, 128], BF16, is_transpose_dest=True)
    E_sb = A("E_sb", [128, H, 2, 128], BF16)
    b2_sb = A("b2_sb", [100, D], BF16)
    t1T_sb = A("t1T_sb", [100, 128], BF16)
    pack_sb = A("pack_sb", [128, PACK_N], BF16)
    bias_sb = A("bias_sb", [1, 4 * D], BF16)
    lnst = A("lnst", [128, 6, 15], F32)
    nsq_sb = A("nsq_sb", [128, 2, 16], F32)
    inv_sb = A("inv_sb", [128, 2, 16], F32)
    rs_sb = A("rs_sb", [128, H, 16], F32)
    f2_sb = rs_sb
    rq_bf = A("rq_bf", [128, 16], BF16)
    sir_bf = A("sir_bf", [1, 16], BF16)
    siv_sb = A("siv_sb", [128, 16], F32)
    sr2_sb = A("sr2_sb", [128, 16], F32)
    eps_sb = A("eps_sb", [128, 1], F32)
    zr_sb = lnst.ap()[:, 2, 0:8]
    pk = pack_sb.ap()
    clt_sb = pk[:, PK_CLT:PK_CLT + 128]
    cut_sb = pk[:, PK_CUT:PK_CUT + 128]
    idn_sb = pk[:, PK_IDN:PK_IDN + 128]
    mrhs_v = pk[:, PK_MR:PK_MR + 256]
    orow_bf = pk[0:1, PK_CUT:PK_CUT + 128]
    ocol_bf = pk[:, PK_CLT:PK_CLT + 1]
    b1_view = raw_roll.ap()[0:4, 0, :]
    uaT_view = rawT_win.ap().rearrange("p a b c -> p (a b c)")[0:50, 0:D]
    lu_view = gt_roll.ap()[0:50, 0, 0:4]
    ld_view = gt_roll.ap()[0:4, 0, 16:116]
    bb = bias_sb.ap()
    bg_view = bb[0:1, 0:D]
    bv_view = bb[0:1, D:2 * D]
    bo_bf = bb[0:1, 2 * D:3 * D]
    bff_bf = bb[0:1, 3 * D:4 * D]
    daT_views = [fap(E_sb, 100 * k, [[H * 2 * 128, 128], [1, 100]])
                 for k in range(NK)]
    daT_dst = fap(E_sb, 0, [[H * 2 * 128, 128], [1, 800]])
    assert nc.sbuf_bytes_remaining >= 0, nc.sbuf_bytes_remaining

    PA = nc.alloc_psum_tensor("PA", [128, 1024], F32)
    PB = nc.alloc_psum_tensor("PB", [128, 1024], F32)
    PC = nc.alloc_psum_tensor("PC", [128, 1024], F32)
    PD = nc.alloc_psum_tensor("PD", [128, 1024], F32)
    assert nc.psum_banks_remaining == 0

    import contextlib
    stack = contextlib.ExitStack()
    s_x = Ctr(stack.enter_context(nc.semaphore("s_x")))
    s_w1 = Ctr(stack.enter_context(nc.semaphore("s_w1")))
    s_w2 = Ctr(stack.enter_context(nc.semaphore("s_w2")))
    s_w3 = Ctr(stack.enter_context(nc.semaphore("s_w3")))
    s_w4 = Ctr(stack.enter_context(nc.semaphore("s_w4")))
    s_sm = Ctr(stack.enter_context(nc.semaphore("s_sm")))
    s_tp = Ctr(stack.enter_context(nc.semaphore("s_tp")))
    s_mm = Ctr(stack.enter_context(nc.semaphore("s_mm")))
    s_v = Ctr(stack.enter_context(nc.semaphore("s_v")))
    s_s = Ctr(stack.enter_context(nc.semaphore("s_s")))
    s_go = Ctr(stack.enter_context(nc.semaphore("s_go")))
    s_da = Ctr(stack.enter_context(nc.semaphore("s_da")))
    s_sp = Ctr(stack.enter_context(nc.semaphore("s_sp")))
    s_xc = [Ctr(stack.enter_context(nc.semaphore(f"s_xc{c}")))
            for c in range(8)]

    with stack:
        with nc.Block() as block:
            _emit(nc, block, locals())
    return nc


def _emit(nc, block, L):
    g = block.gpsimd
    t = block.tensor
    v = block.vector
    s = block.scalar
    y = block.sync

    (s_x, s_w1, s_w2, s_w3, s_w4, s_sm, s_tp, s_mm, s_v, s_s, s_go, s_da) = (
        L["s_x"], L["s_w1"], L["s_w2"], L["s_w3"], L["s_w4"], L["s_sm"],
        L["s_tp"], L["s_mm"], L["s_v"], L["s_s"], L["s_go"], L["s_da"])
    s_xc = L["s_xc"]
    s_sp = L["s_sp"]
    x_sb, gh_sb, V_sb, Wa, Wb = L["x_sb"], L["gh_sb"], L["V_sb"], L["Wa"], L["Wb"]
    lnx_roll, lnxT_win = L["lnx_roll"], L["lnxT_win"]
    ghT_win, gtT_win, gt_roll = L["ghT_win"], L["gtT_win"], L["gt_roll"]
    raw_roll, rawT_win, E_sb = L["raw_roll"], L["rawT_win"], L["E_sb"]
    b2_sb, t1T_sb = L["b2_sb"], L["t1T_sb"]
    bo_bf, bff_bf = L["bo_bf"], L["bff_bf"]
    clt_sb, cut_sb, idn_sb, mrhs_v = (L["clt_sb"], L["cut_sb"],
                                      L["idn_sb"], L["mrhs_v"])
    orow_bf, ocol_bf = L["orow_bf"], L["ocol_bf"]
    daT_views = L["daT_views"]
    lnst, nsq_sb, inv_sb = L["lnst"], L["nsq_sb"], L["inv_sb"]
    rs_sb, f2_sb = L["rs_sb"], L["f2_sb"]
    rq_bf, sir_bf, siv_sb, sr2_sb, zr_sb = (L["rq_bf"], L["sir_bf"],
                                            L["siv_sb"], L["sr2_sb"],
                                            L["zr_sb"])
    eps_sb = L["eps_sb"]
    b1_view, uaT_view, daT_dst = L["b1_view"], L["uaT_view"], L["daT_dst"]
    lu_view, ld_view, bg_view, bv_view = (L["lu_view"], L["ld_view"],
                                          L["bg_view"], L["bv_view"])
    PA, PB, PC, PD = L["PA"], L["PB"], L["PC"], L["PD"]
    x_hbm, out_hbm = L["x_hbm"], L["out_hbm"]
    pack_h, bias_h = L["pack_h"], L["bias_h"]
    pack_sb, bias_sb = L["pack_sb"], L["bias_sb"]
    sq_scr = gt_roll.ap()[:, 1, :]

    def bn_pass(ve, slot, src_lo, src_hi):
        ve.bn_stats(lnst.ap()[:, slot, 0:6], src_lo)
        ve.bn_stats(lnst.ap()[:, slot, 6:12], src_hi)
        ve.drain()
        return ve.bn_aggr(lnst.ap()[:, slot, 12:14], lnst.ap()[:, slot, 0:12])

    def rstd_pass(sc, slot):
        sc.activation(lnst.ap()[:, slot, 14:15], lnst.ap()[:, slot, 13:14],
                      AF.Ln, scale=1.0, bias=eps_sb.ap())
        sc.drain()
        return sc.activation(lnst.ap()[:, slot, 14:15],
                             lnst.ap()[:, slot, 14:15], AF.Exp, scale=-0.5)

    def apply_pass(ve, slot, dst, src):
        return ve.tensor_scalar(dst, src, lnst.ap()[:, slot, 12:13],
                                lnst.ap()[:, slot, 14:15],
                                ALU.subtract, ALU.mult)

    tok = {}

    def _g_loads(gp):
        for dst, src in ((pack_sb.ap(), pack_h), (bias_sb.ap(), bias_h)):
            gp.dma_start(dst, src,
                         max_dma_last_dim=8192).then_inc(s_sp.h, 16)
            s_sp.n += 16
        gp.dma_start(Wa.ap(), L["wgT_h"].rearrange("(k p) o -> p k o", p=128)
                     ).then_inc(s_w1.h, 16)
        s_w1.n += 16
        gp.dma_start(Wb.ap(), L["wvT_h"].rearrange("(k p) o -> p k o", p=128)
                     ).then_inc(s_w2.h, 16)
        s_w2.n += 16
        xr = x_hbm.rearrange("(m p) f -> p m f", p=128)
        for c in (2, 3):
            gp.dma_start(x_sb.ap()[:, 2 * c:2 * c + 2, :],
                         xr[:, 2 * c:2 * c + 2, :]).then_inc(s_xc[c].h, 16)
            s_xc[c].n += 16
        for dst, src in ((uaT_view, L["uaT_h"]), (lu_view, L["lu_h"]),
                         (ld_view, L["ld_h"])):
            gp.dma_start(dst, src,
                         max_dma_last_dim=8192).then_inc(s_sm.h, 16)
            s_sm.n += 16
        for c in (4, 5, 6, 7):
            gp.dma_start(x_sb.ap()[:, 2 * c:2 * c + 2, :],
                         xr[:, 2 * c:2 * c + 2, :]).then_inc(s_xc[c].h, 16)
            s_xc[c].n += 16

    g(_g_loads)
    tok["sm"] = s_sm.n
    tok["sp"] = s_sp.n

    def _s_xload(sc):
        xr = x_hbm.rearrange("(m p) f -> p m f", p=128)
        for c in (0, 1):
            sc.dma_start(x_sb.ap()[:, 2 * c:2 * c + 2, :],
                         xr[:, 2 * c:2 * c + 2, :]).then_inc(s_xc[c].h, 16)
            s_xc[c].n += 16

    s(_s_xload)

    def _s_table(sc):
        sc.add_instruction(mybir.InstLoadActFuncSet(
            name=sc.bass.get_next_instruction_name(),
            act_func_set_id=ACT_SET_NL_EXP, ins=[], outs=[]))

    s(_s_table)

    def _v_ones(ve):
        ve.memset(fap(V_sb, 64, [[NT * H * 65, 128], [65 * H, NT], [65, H],
                                 [1, 1]]), 1.0)
        ve.memset(eps_sb.ap(), 1e-5).then_inc(s_v.h, 1)
        s_v.n += 1

    v(_v_ones)
    tok["vones"] = s_v.n

    tok_red1 = [0] * NT
    tok_ap1 = [0] * NT
    tok_ap1v = [0] * NT
    tok_tp1 = [0] * NT
    tok_gmm = [0] * NT
    tok_vmm = [0] * NT
    tok_gsq = [0] * NT
    tok_nsq = [0] * NT
    tok_inv = [0] * NT
    tok_ghev = [0] * NT
    tok_vev = [0] * NT
    tok_cmm = [0] * NT
    tok_dot = [0] * NT

    def kext(te, dst, bias_row, n):
        te.matmul(dst, orow_bf, bias_row[:, n * 512:(n + 1) * 512],
                  start=False, stop=True).then_inc(s_mm.h, 1)
        s_mm.n += 1

    def emit_ln1a(m):
        def _v_bn1(ve, m=m):
            _wait(ve, s_xc[m // 2], 16)
            bn_pass(ve, 2 + m % 2, x_sb.ap()[:, m, 0:512],
                    x_sb.ap()[:, m, 512:1024]).then_inc(s_v.h, 1)
            s_v.n += 1
            tok_red1[m] = s_v.n

        v(_v_bn1)

        def _s_rstd1(sc, m=m):
            _wait(sc, s_v, tok_red1[m])
            rstd_pass(sc, 2 + m % 2).then_inc(s_s.h, 1)
            s_s.n += 1
            tok_ap1[m] = s_s.n

        s(_s_rstd1)

    def emit_ln1b(m):
        def _v_ap1(ve, m=m):
            _wait(ve, s_s, tok_ap1[m])
            if m >= 2:
                _wait(ve, s_tp, tok_tp1[m - 2])
            apply_pass(ve, 2 + m % 2, lnx_roll.ap()[:, m % 2, :],
                       x_sb.ap()[:, m, :]).then_inc(s_v.h, 1)
            s_v.n += 1
            tok_ap1v[m] = s_v.n

        v(_v_ap1)

        def _y_tp1(sy, m=m):
            _wait(sy, s_v, tok_ap1v[m])
            if m >= 2:
                _wait(sy, s_mm, tok_vmm[m - 2])
            sy.dma_start_transpose(
                fap(lnxT_win, (m % 2) * 128,
                    [[NK * 256, 128], [256, NK], [1, 128]]),
                fap(lnx_roll, (m % 2) * D,
                    [[2 * D, 128], [128, NK], [1, 128]]),
            ).then_inc(s_tp.h, 16)
            s_tp.n += 16
            tok_tp1[m] = s_tp.n

        y(_y_tp1)

    emit_ln1a(0)
    emit_ln1b(0)

    for st in range(NT + 2):
        m2 = st + 1
        mp = st - 1
        qb = st - 2
        if 0 <= qb < NT:
            def _t_band(te, qb=qb):
                ps = PA if qb % 2 == 0 else PB
                _wait(te, s_s, tok_gsq[qb])
                _wait(te, s_v, tok_ghev[qb])
                for n in range(2):
                    dst = ps.ap()[:, n * 512:(n + 1) * 512]
                    first = True
                    if qb > 0:
                        te.matmul(dst, clt_sb,
                                  gh_sb.ap()[:, qb - 1, n * 512:(n + 1) * 512],
                                  start=True, stop=False)
                        first = False
                    te.matmul(dst, cut_sb,
                              gh_sb.ap()[:, qb, n * 512:(n + 1) * 512],
                              start=first, stop=True).then_inc(s_mm.h, 1)
                    s_mm.n += 1
                tok_cmm[qb] = s_mm.n

            t(_t_band)

        if 0 <= mp < NT:
            pg = PA if mp % 2 == 0 else PB
            pv = PC if mp % 2 == 0 else PD

            def _t_gv(te, mp=mp, pg=pg, pv=pv):
                _wait(te, s_tp, tok_tp1[mp])
                _wait(te, s_w1, 16)
                _wait(te, s_sp, tok["sp"])
                if mp >= 2:
                    _wait(te, s_v, tok_vev[mp - 2])
                for n in range(2):
                    dst = pg.ap()[:, n * 512:(n + 1) * 512]
                    for k in range(NK):
                        te.matmul(dst, lnxT_win.ap()[:, k, mp % 2, :],
                                  Wa.ap()[:, k, n * 512:(n + 1) * 512],
                                  start=(k == 0), stop=False)
                    kext(te, dst, bg_view, n)
                tok_gmm[mp] = s_mm.n
                _wait(te, s_w2, 16)
                for n in range(2):
                    dst = pv.ap()[:, n * 512:(n + 1) * 512]
                    for k in range(NK):
                        te.matmul(dst, lnxT_win.ap()[:, k, mp % 2, :],
                                  Wb.ap()[:, k, n * 512:(n + 1) * 512],
                                  start=(k == 0), stop=False)
                    kext(te, dst, bv_view, n)
                tok_vmm[mp] = s_mm.n

            t(_t_gv)

            def _s_gsq(sc, mp=mp, pg=pg):
                _wait(sc, s_mm, tok_gmm[mp])
                sc.activation(sq_scr, pg.ap(), AF.Square).then_inc(s_s.h, 1)
                s_s.n += 1
                tok_gsq[mp] = s_s.n

            s(_s_gsq)

        if m2 < NT:
            emit_ln1a(m2)

        if 0 <= mp < NT:
            def _v_nsq(ve, mp=mp):
                _wait(ve, s_s, tok_gsq[mp])
                if mp >= 2:
                    _wait(ve, s_s, tok_inv[mp - 2])
                ve.tensor_reduce(nsq_sb.ap()[:, mp % 2, :],
                                 gt_roll.ap()[:, 1, :].rearrange(
                                     "p (h d) -> p h d", h=16),
                                 AX.X, ALU.add).then_inc(s_v.h, 1)
                s_v.n += 1
                tok_nsq[mp] = s_v.n

            v(_v_nsq)

            def _s_inv(sc, mp=mp):
                _wait(sc, s_v, tok_nsq[mp])
                sc.activation(inv_sb.ap()[:, mp % 2, :],
                              nsq_sb.ap()[:, mp % 2, :], AF.Ln)
                sc.drain()
                sc.activation(inv_sb.ap()[:, mp % 2, :],
                              inv_sb.ap()[:, mp % 2, :], AF.Exp,
                              scale=-0.5).then_inc(s_s.h, 1)
                s_s.n += 1
                tok_inv[mp] = s_s.n

            s(_s_inv)

            def _v_ghev(ve, mp=mp, pg=pg):
                _wait(ve, s_s, tok_inv[mp])
                ve.tensor_tensor(
                    gh_sb.ap()[:, mp, :].rearrange("p (h d) -> p h d", h=16),
                    pg.ap().rearrange("p (h d) -> p h d", h=16),
                    fap(inv_sb, (mp % 2) * 16, [[32, 128], [1, 16], [0, 64]]),
                    ALU.mult).then_inc(s_v.h, 1)
                s_v.n += 1
                tok_ghev[mp] = s_v.n

            v(_v_ghev)

        if 0 <= qb < NT:
            def _v_dot(ve, dq=qb):
                ps = PA if dq % 2 == 0 else PB
                _wait(ve, s_mm, tok_cmm[dq])
                ve.tensor_tensor(raw_roll.ap()[:, 0, :], gh_sb.ap()[:, dq, :],
                                 ps.ap(), ALU.mult)
                ve.drain()
                ve.tensor_reduce(fap(rs_sb, dq, [[H * 16, 128], [16, H], [1, 1]]),
                                 raw_roll.ap()[:, 0, :].rearrange(
                                     "p (h d) -> p h d", h=16),
                                 AX.X, ALU.add).then_inc(s_v.h, 1)
                s_v.n += 1
                tok_dot[dq] = s_v.n

            v(_v_dot)

        if m2 < NT:
            emit_ln1b(m2)

        if 0 <= mp < NT:
            def _v_vev(ve, mp=mp, pv=pv):
                _wait(ve, s_mm, tok_vmm[mp])
                _wait(ve, s_v, tok["vones"])
                for n in range(2):
                    ve.tensor_copy(
                        fap(V_sb, mp * H * 65 + n * 8 * 65,
                            [[NT * H * 65, 128], [65, 8], [1, 64]]),
                        pv.ap()[:, n * 512:(n + 1) * 512]).then_inc(s_v.h, 1)
                    s_v.n += 1
                tok_vev[mp] = s_v.n

            v(_v_vev)

    def _g_w2(gp):
        _wait(gp, s_v, tok_vev[NT - 1])
        gp.dma_start(Wb.ap(), L["woT_h"].rearrange("(k p) o -> p k o", p=128)
                     ).then_inc(s_w3.h, 16)
        s_w3.n += 16
        gp.dma_start(Wa.ap(), L["wfT_h"].rearrange("(k p) o -> p k o", p=128)
                     ).then_inc(s_w4.h, 16)
        s_w4.n += 16

    g(_g_w2)

    def _t_l1(te):
        _wait(te, s_sm, tok["sm"])
        _wait(te, s_v, tok_vev[NT - 1])
        for n in range(2):
            te.matmul(PD.ap()[0:4, n * 512:(n + 1) * 512], lu_view,
                      uaT_view[:, n * 512:(n + 1) * 512], start=True,
                      stop=True).then_inc(s_mm.h, 1)
            s_mm.n += 1

    t(_t_l1)
    tok["b1"] = s_mm.n

    def _v_b1(ve):
        _wait(ve, s_mm, tok["b1"])
        ve.tensor_copy(b1_view, PD.ap()[0:4, :]).then_inc(s_v.h, 1)
        s_v.n += 1

    v(_v_b1)
    tok["b1e"] = s_v.n

    def _t_l2(te):
        _wait(te, s_v, tok["b1e"])
        for n in range(2):
            te.matmul(PD.ap()[0:100, n * 512:(n + 1) * 512], ld_view,
                      b1_view[:, n * 512:(n + 1) * 512], start=True,
                      stop=True).then_inc(s_mm.h, 1)
            s_mm.n += 1

    t(_t_l2)
    tok["b2"] = s_mm.n

    def _v_b2(ve):
        _wait(ve, s_mm, tok["b2"])
        ve.tensor_copy(b2_sb.ap(), PD.ap()[0:100, :]).then_inc(s_v.h, 1)
        s_v.n += 1

    v(_v_b2)
    tok["b2e"] = s_v.n


    def _v_fit1(ve):
        _wait(ve, s_v, tok_dot[NT - 1])
        ve.tensor_scalar(rs_sb.ap(), rs_sb.ap(), 1.0 / 4096.0, 1.0,
                         ALU.mult, ALU.add).then_inc(s_v.h, 1)
        s_v.n += 1

    v(_v_fit1)
    tok["fit1"] = s_v.n

    def _s_fit(sc):
        _wait(sc, s_v, tok["fit1"])
        sc.activation(rs_sb.ap(), rs_sb.ap(), AF.Ln)
        sc.drain()
        sc.activation(rs_sb.ap(), rs_sb.ap(), AF.Exp,
                      scale=-1.0).then_inc(s_s.h, 1)
        s_s.n += 1

    s(_s_fit)
    tok["fit2"] = s_s.n

    def _v_fit2(ve):
        _wait(ve, s_s, tok["fit2"])
        ve.tensor_reduce(siv_sb.ap(), rs_sb.ap(), AX.X,
                         ALU.add).then_inc(s_v.h, 1)
        s_v.n += 1

    v(_v_fit2)
    tok["siv"] = s_v.n

    def _s_rq(sc):
        _wait(sc, s_v, tok["siv"])
        sc.activation(rq_bf.ap(), siv_sb.ap(),
                      AF.Copy).then_inc(s_s.h, 1)
        s_s.n += 1

    s(_s_rq)
    tok["rq"] = s_s.n

    def _t_si(te):
        _wait(te, s_s, tok["rq"])
        _wait(te, s_v, tok_vev[14])
        te.matmul(PC.ap()[0:1, 0:16], ocol_bf, rq_bf.ap(), start=True,
                  stop=True).then_inc(s_mm.h, 1)
        s_mm.n += 1

    t(_t_si)
    tok["si"] = s_mm.n

    def _v_si(ve):
        _wait(ve, s_mm, tok["si"])
        ve.tensor_copy(sir_bf.ap(), PC.ap()[0:1, 0:16]).then_inc(s_v.h, 1)
        s_v.n += 1

    v(_v_si)
    tok["sir"] = s_v.n

    def _t_sib(te):
        _wait(te, s_v, tok["sir"])
        te.matmul(PC.ap()[:, 512:528], orow_bf, sir_bf.ap(), start=True,
                  stop=True).then_inc(s_mm.h, 1)
        s_mm.n += 1

    t(_t_sib)
    tok["sib"] = s_mm.n

    def _v_fcp(ve):
        _wait(ve, s_mm, tok["sib"])
        ve.tensor_copy(siv_sb.ap(), PC.ap()[:, 512:528]).then_inc(s_v.h, 1)
        s_v.n += 1

    v(_v_fcp)
    tok["fcp"] = s_v.n

    def _s_sr2(sc):
        _wait(sc, s_v, tok["fcp"])
        sc.activation(sr2_sb.ap(), siv_sb.ap(), AF.Ln, scale=2.0)
        sc.drain()
        sc.activation(sr2_sb.ap(), sr2_sb.ap(), AF.Exp,
                      scale=-1.0).then_inc(s_s.h, 1)
        s_s.n += 1

    s(_s_sr2)
    tok["sr2"] = s_s.n

    def _v_f2(ve):
        _wait(ve, s_s, tok["sr2"])
        ve.tensor_tensor(f2_sb.ap(), rs_sb.ap(),
                         fap(sr2_sb, 0, [[16, 128], [1, 16], [0, 16]]),
                         ALU.mult).then_inc(s_v.h, 1)
        s_v.n += 1

    v(_v_f2)
    tok["f2"] = s_v.n

    tok_ghT = [0] * NT
    tok_gt = [0] * NT
    tok_gtT = [0] * NT
    tok_smm = [[0] * 4 for _ in range(NT)]
    tok_exp = [[0] * 4 for _ in range(NT)]
    tok_av = [[0] * 4 for _ in range(NT)]
    tok_nrm = [[0] * 4 for _ in range(NT)]
    tok_rawT = [0] * NT
    tok_omm = [0] * NT
    tok_red2 = [0] * NT
    tok_ap2 = [0] * NT
    tok_ap2v = [0] * NT
    tok_ln2 = [0] * NT

    TP_DIMS = [[NK * 256, 128], [256, NK], [1, 128]]

    def ghT_slot(qb):
        s = qb % 3
        if s < 2:
            return fap(ghT_win, s * 128, TP_DIMS)
        return fap(lnxT_win, 0, TP_DIMS)

    def gtT_slot(qb):
        s = qb % 3
        if s < 2:
            return fap(gtT_win, s * 128, TP_DIMS)
        return fap(lnxT_win, 128, TP_DIMS)

    def emit_gt_and_T(qb):
        def _v_gt(ve, qb=qb):
            _wait(ve, s_v, tok["f2"])
            if qb >= 2:
                _wait(ve, s_tp, tok_gtT[qb - 2])
            ve.tensor_tensor(
                gt_roll.ap()[:, qb % 2, :].rearrange("p (h d) -> p h d", h=16),
                gh_sb.ap()[:, qb, :].rearrange("p (h d) -> p h d", h=16),
                fap(f2_sb, qb, [[H * 16, 128], [16, H], [0, 64]]),
                ALU.mult).then_inc(s_v.h, 1)
            s_v.n += 1
            tok_gt[qb] = s_v.n

        v(_v_gt)

        def _y_T(sy, qb=qb):
            if qb % 3 == 2:
                _wait(sy, s_mm, tok_vmm[NT - 1])
            if qb >= 3:
                _wait(sy, s_mm, tok_smm[qb - 2][3])
            sy.dma_start_transpose(
                ghT_slot(qb),
                fap(gh_sb, qb * D, [[NT * D, 128], [128, NK], [1, 128]]),
            ).then_inc(s_tp.h, 16)
            s_tp.n += 16
            tok_ghT[qb] = s_tp.n
            _wait(sy, s_v, tok_gt[qb])
            sy.dma_start_transpose(
                gtT_slot(qb),
                fap(gt_roll, (qb % 2) * D,
                    [[2 * D, 128], [128, NK], [1, 128]]),
            ).then_inc(s_tp.h, 16)
            s_tp.n += 16
            tok_gtT[qb] = s_tp.n

        y(_y_T)

    def emit_sc(qb, grp):
        def _t_sc(te, qb=qb, grp=grp):
            ps = PA if grp % 2 == 0 else PB
            _wait(te, s_tp, tok_ghT[qb])
            _wait(te, s_tp, tok_gtT[qb])
            if qb > 0:
                _wait(te, s_tp, tok_ghT[qb - 1])
            if grp >= 2:
                _wait(te, s_s, tok_exp[qb][grp - 2])
            elif qb > 0:
                _wait(te, s_s, tok_exp[qb - 1][grp + 2])
            else:
                _wait(te, s_v, tok_dot[NT - 1])
            hlist = [grp * 4, grp * 4 + 2, grp * 4 + 1, grp * 4 + 3]
            ghq = ghT_slot(qb)
            gtq = gtT_slot(qb)
            ghp = ghT_slot(qb - 1) if qb > 0 else None
            mms = []
            for i, h in enumerate(hlist):
                p0 = (h % 2) * 64
                lhs_same = ghq[p0:p0 + 64, h // 2, :]
                rhs_q = gtq[p0:p0 + 64, h // 2, :]
                if qb > 0:
                    mms.append((ps.ap()[:, i * 256:i * 256 + 128],
                                ghp[p0:p0 + 64, h // 2, :], rhs_q))
                mms.append((ps.ap()[:, i * 256 + 128:i * 256 + 256],
                            lhs_same, rhs_q))
            if qb > 0:
                for half in range(2):
                    mms.append((ps.ap()[:, half * 512:(half + 1) * 512],
                                idn_sb,
                                fap(pack_sb, PK_MR,
                                    [[PACK_N, 128], [0, 2], [1, 256]])))
            else:
                for i in range(4):
                    mms.append((ps.ap()[:, i * 256 + 128:i * 256 + 256],
                                idn_sb, mrhs_v[:, 128:256]))
            first = {0: True, 1: True}
            last = {}
            for j, (dst, _, _) in enumerate(mms):
                last[dst.offset // 512] = j
            for j, (dst, lhsT, rhs) in enumerate(mms):
                bank = dst.offset // 512
                bi = te.matmul(dst, lhsT, rhs, start=first[bank],
                               stop=(last[bank] == j))
                first[bank] = False
                if last[bank] == j:
                    bi.then_inc(s_mm.h, 1)
                    s_mm.n += 1
            tok_smm[qb][grp] = s_mm.n

        t(_t_sc)

    def emit_exp(qb, grp):
        def _s_exp(sc, qb=qb, grp=grp):
            ps = PA if grp % 2 == 0 else PB
            _wait(sc, s_mm, tok_smm[qb][grp])
            if qb == 0:
                sc.activation(
                    fap(E_sb, grp * 4 * 256 + 128,
                        [[H * 256, 128], [256, 4], [1, 128]]),
                    fap(ps, 128, [[1024, 128], [256, 4], [1, 128]]),
                    AF.Exp).then_inc(s_s.h, 1)
            else:
                sc.activation(
                    fap(E_sb, grp * 4 * 256, [[H * 256, 128], [1, 1024]]),
                    ps.ap(), AF.Exp).then_inc(s_s.h, 1)
            s_s.n += 1
            tok_exp[qb][grp] = s_s.n

        s(_s_exp)

    def emit_av(qb, grp):
        def _t_av(te, qb=qb, grp=grp):
            half = (grp % 2) * 512
            _wait(te, s_s, tok_exp[qb][grp])
            _wait(te, s_v, tok_vev[NT - 1])
            if grp >= 2:
                _wait(te, s_v, tok_nrm[qb][grp - 2])
            elif qb > 0:
                _wait(te, s_v, tok_nrm[qb - 1][grp + 2])
            elif grp == 0:
                _wait(te, s_v, tok["sir"])
            else:
                _wait(te, s_v, tok["f2"])
            n_mm = 4 * (2 if qb > 0 else 1)
            hlist = [grp * 4, grp * 4 + 2, grp * 4 + 1, grp * 4 + 3]
            j = 0
            for i, h in enumerate(hlist):
                dst = fap(PC, half + i * 65, [[1024, 128], [1, 65]])
                epos = grp * 4 + i
                if qb > 0:
                    te.matmul(dst, E_sb.ap()[:, epos, 0, :],
                              fap(V_sb, (qb - 1) * H * 65 + h * 65,
                                  [[NT * H * 65, 128], [1, 65]]),
                              start=(j == 0), stop=False)
                    j += 1
                bi = te.matmul(dst, E_sb.ap()[:, epos, 1, :],
                               fap(V_sb, qb * H * 65 + h * 65,
                                   [[NT * H * 65, 128], [1, 65]]),
                               start=(j == 0), stop=(j == n_mm - 1))
                j += 1
                if j == n_mm:
                    bi.then_inc(s_mm.h, 1)
                    s_mm.n += 1
            tok_av[qb][grp] = s_mm.n

        t(_t_av)

    tok_zr = [[0] * 4 for _ in range(NT)]

    def emit_zr(qb, pair):
        def _s_zr(sc, qb=qb, pair=pair):
            _wait(sc, s_mm, tok_av[qb][2 * pair + 1])
            if qb > 0:
                _wait(sc, s_v, tok_nrm[qb - 1][3])
            z = fap(lnst, (2 + pair) * 15, [[90, 128], [1, 8]])
            sc.activation(z, fap(PC, 64, [[1024, 128], [512, 2], [65, 4],
                                          [1, 1]]), AF.Ln)
            sc.drain()
            sc.activation(z, z, AF.Exp, scale=-1.0).then_inc(s_s.h, 1)
            s_s.n += 1
            tok_zr[qb][pair] = s_s.n

        s(_s_zr)

    def emit_nrm(qb, grp):
        def _v_nrm(ve, qb=qb, grp=grp):
            half = (grp % 2) * 512
            _wait(ve, s_s, tok_zr[qb][grp // 2])
            if qb >= 2 and grp == 0:
                _wait(ve, s_tp, tok_rawT[qb - 2])
            ve.tensor_tensor(
                fap(raw_roll, (qb % 2) * D + grp * 256,
                    [[2 * D, 128], [64, 2], [128, 2], [1, 64]]),
                fap(PC, half, [[1024, 128], [130, 2], [65, 2], [1, 64]]),
                fap(lnst, (2 + grp // 2) * 15 + (grp % 2) * 4,
                    [[90, 128], [2, 2], [1, 2], [0, 64]]),
                ALU.mult).then_inc(s_v.h, 1)
            s_v.n += 1
            tok_nrm[qb][grp] = s_v.n

        v(_v_nrm)

    def emit_rawT(qb):
        def _s_rawT(sc, qb=qb):
            _wait(sc, s_v, tok_nrm[qb][3])
            if qb >= 1:
                _wait(sc, s_mm, tok_omm[qb - 1])
            sc.dma_start_transpose(
                fap(rawT_win, 0, [[NK * 128, 128], [128, NK], [1, 128]]),
                fap(raw_roll, (qb % 2) * D, [[2 * D, 128], [128, NK], [1, 128]]),
            ).then_inc(s_tp.h, 16)
            s_tp.n += 16
            tok_rawT[qb] = s_tp.n

        s(_s_rawT)

    def emit_o(qb):
        def _t_o(te, qb=qb):
            _wait(te, s_tp, tok_rawT[qb])
            _wait(te, s_w3, 16)
            if qb == 0:
                _wait(te, s_v, tok["b2e"])
                _wait(te, s_v, tok_vev[15])
            else:
                _wait(te, s_v, tok_ap2v[qb - 1])
            for n in range(2):
                dst = PD.ap()[:, n * 512:(n + 1) * 512]
                for k in range(NK):
                    te.matmul(dst, rawT_win.ap()[:, k, 0, :],
                              Wb.ap()[:, k, n * 512:(n + 1) * 512],
                              start=(k == 0), stop=False)
                kext(te, dst, bo_bf, n)
            tok_omm[qb] = s_mm.n

        t(_t_o)

    def emit_ln2a(qb):
        def _v_bn2(ve, qb=qb):
            _wait(ve, s_mm, tok_omm[qb])
            bn_pass(ve, 4 + qb % 2, PD.ap()[:, 0:512],
                    PD.ap()[:, 512:1024]).then_inc(s_v.h, 1)
            s_v.n += 1
            tok_red2[qb] = s_v.n

        v(_v_bn2)

    def emit_ln2b(qb):
        def _s_r2(sc, qb=qb):
            _wait(sc, s_v, tok_red2[qb])
            rstd_pass(sc, 4 + qb % 2).then_inc(s_s.h, 1)
            s_s.n += 1
            tok_ap2[qb] = s_s.n

        s(_s_r2)

        def _v_ap2(ve, qb=qb):
            _wait(ve, s_s, tok_ap2[qb])
            apply_pass(ve, 4 + qb % 2, lnx_roll.ap()[:, qb % 2, :],
                       PD.ap()).then_inc(s_v.h, 1)
            s_v.n += 1
            tok_ap2v[qb] = s_v.n

        v(_v_ap2)

        def _v_add2(ve, qb=qb):
            ve.drain()
            ve.tensor_tensor(x_sb.ap()[:, qb, :], x_sb.ap()[:, qb, :],
                             lnx_roll.ap()[:, qb % 2, :],
                             ALU.add).then_inc(s_v.h, 1)
            s_v.n += 1
            tok_ln2[qb] = s_v.n

        v(_v_add2)

    tok_red3 = [0] * NT
    tok_ap3 = [0] * NT
    tok_ap3v = [0] * NT
    tok_tp2 = [0] * NT
    tok_nm3 = [0] * NT

    def emit_ln3a(m):
        def _v_bn3(ve, m=m):
            _wait(ve, s_v, tok_ln2[m])
            if m >= 2:
                _wait(ve, s_s, tok_ap3v[m - 2])
            bn_pass(ve, m % 2, x_sb.ap()[:, m, 0:512],
                    x_sb.ap()[:, m, 512:1024]).then_inc(s_v.h, 1)
            s_v.n += 1
            tok_red3[m] = s_v.n

        v(_v_bn3)

        def _s_rstd3(sc, m=m):
            _wait(sc, s_v, tok_red3[m])
            rstd_pass(sc, m % 2).then_inc(s_s.h, 1)
            s_s.n += 1
            tok_ap3[m] = s_s.n

        s(_s_rstd3)

        def _v_nm3(ve, m=m):
            _wait(ve, s_s, tok_ap3[m])
            ve.tensor_scalar(lnst.ap()[:, m % 2, 10:11],
                             lnst.ap()[:, m % 2, 12:13],
                             lnst.ap()[:, m % 2, 14:15], -1.0,
                             ALU.mult, ALU.mult).then_inc(s_v.h, 1)
            s_v.n += 1
            tok_nm3[m] = s_v.n

        v(_v_nm3)

    emit_gt_and_T(0)
    emit_gt_and_T(1)
    emit_gt_and_T(2)
    for qb in range(NT):
        emit_sc(qb, 0)
        emit_exp(qb, 0)
        if qb >= 1:
            emit_zr(qb - 1, 1)
            emit_nrm(qb - 1, 2)
            emit_nrm(qb - 1, 3)
            emit_rawT(qb - 1)
        emit_sc(qb, 1)
        emit_exp(qb, 1)
        if qb >= 2:
            emit_ln2b(qb - 2)
        emit_av(qb, 0)
        emit_sc(qb, 2)
        emit_exp(qb, 2)
        emit_av(qb, 1)
        emit_sc(qb, 3)
        emit_exp(qb, 3)
        emit_zr(qb, 0)
        emit_nrm(qb, 0)
        emit_nrm(qb, 1)
        if qb >= 1:
            emit_o(qb - 1)
        emit_av(qb, 2)
        emit_av(qb, 3)
        if qb >= 1:
            emit_ln2a(qb - 1)
        if 3 <= qb + 2 < NT:
            emit_gt_and_T(qb + 2)
        if qb == NT - 2:
            emit_ln3a(0)
            emit_ln3a(1)
    emit_zr(NT - 1, 1)
    emit_nrm(NT - 1, 2)
    emit_nrm(NT - 1, 3)
    emit_rawT(NT - 1)
    emit_ln2b(NT - 2)
    emit_o(NT - 1)
    emit_ln2a(NT - 1)
    emit_ln2b(NT - 1)

    def _g_daT(gp):
        _wait(gp, s_mm, tok_av[NT - 1][0])
        gp.dma_start(daT_dst, L["daT_h"]).then_inc(s_da.h, 16)
        s_da.n += 16

    g(_g_daT)

    tok_t1 = [0] * NT
    tok_t1e = [0] * NT
    tok_ymm = [0] * NT
    tok_red4 = [0] * NT
    tok_ap4 = [0] * NT
    tok_x2 = [0] * NT
    tok_ap4v = [0] * NT
    tok_nm4 = [0] * NT

    def emit_ln3b(m):
            def _s_ap3(sc, m=m):
                _wait(sc, s_v, tok_nm3[m])
                if m >= 2:
                    _wait(sc, s_tp, tok_tp2[m - 2])
                else:
                    _wait(sc, s_v, tok_ln2[14 + m])
                sc.activation(lnx_roll.ap()[:, m % 2, :], x_sb.ap()[:, m, :],
                              AF.Identity,
                              bias=lnst.ap()[:, m % 2, 10:11],
                              scale=lnst.ap()[:, m % 2, 14:15]
                              ).then_inc(s_s.h, 1)
                s_s.n += 1
                tok_ap3v[m] = s_s.n

            s(_s_ap3)

            def _y_tp2(sy, m=m):
                _wait(sy, s_s, tok_ap3v[m])
                if m < 2:
                    _wait(sy, s_mm, tok_smm[NT - 1][3])
                if m >= 2:
                    _wait(sy, s_mm, tok_ymm[m - 2])
                sy.dma_start_transpose(
                    fap(lnxT_win, (m % 2) * 128,
                        [[NK * 256, 128], [256, NK], [1, 128]]),
                    fap(lnx_roll, (m % 2) * D,
                        [[2 * D, 128], [128, NK], [1, 128]]),
                ).then_inc(s_tp.h, 16)
                s_tp.n += 16
                tok_tp2[m] = s_tp.n

            y(_y_tp2)

    emit_ln3b(0)

    for st in range(NT + 2):
        m3 = st + 1
        mp = st - 1
        q2 = st - 2
        if 0 <= mp < NT:
            py = PA if mp % 2 == 0 else PB

            def _t_t1(te, mp=mp):
                _wait(te, s_tp, tok_tp2[mp])
                _wait(te, s_da, 16)
                if mp == 0:
                    _wait(te, s_v, tok_nrm[NT - 1][2])
                else:
                    _wait(te, s_v, tok_t1e[mp - 1])
                for k in range(NK):
                    bi = te.matmul(PC.ap()[0:100, 0:128], daT_views[k],
                                   lnxT_win.ap()[:, k, mp % 2, :],
                                   start=(k == 0), stop=(k == NK - 1))
                bi.then_inc(s_mm.h, 1)
                s_mm.n += 1
                tok_t1[mp] = s_mm.n

            t(_t_t1)

            def _v_t1e(ve, mp=mp):
                _wait(ve, s_mm, tok_t1[mp])
                ve.tensor_copy(t1T_sb.ap(),
                               PC.ap()[0:100, 0:128]).then_inc(s_v.h, 1)
                s_v.n += 1
                tok_t1e[mp] = s_v.n

            v(_v_t1e)

            def _t_y(te, mp=mp, py=py):
                _wait(te, s_w4, 16)
                if mp < 2:
                    _wait(te, s_s, tok_exp[NT - 1][2 + mp])
                if mp >= 2:
                    _wait(te, s_s, tok_ap4v[mp - 2])
                for n in range(2):
                    dst = py.ap()[:, n * 512:(n + 1) * 512]
                    for k in range(NK):
                        te.matmul(dst, lnxT_win.ap()[:, k, mp % 2, :],
                                  Wa.ap()[:, k, n * 512:(n + 1) * 512],
                                  start=(k == 0), stop=False)
                    if n == 0:
                        _wait(te, s_v, tok_t1e[mp])
                    te.matmul(dst, t1T_sb.ap(),
                              b2_sb.ap()[:, n * 512:(n + 1) * 512],
                              start=False, stop=False)
                    kext(te, dst, bff_bf, n)
                tok_ymm[mp] = s_mm.n

            t(_t_y)

        if 0 <= q2 < NT:
            pq = PA if q2 % 2 == 0 else PB

            def _v_bn4(ve, q2=q2, pq=pq):
                _wait(ve, s_mm, tok_ymm[q2])
                if q2 < 2:
                    _wait(ve, s_s, tok_zr[NT - 1][1])
                else:
                    _wait(ve, s_s, tok_ap4v[q2 - 2])
                bn_pass(ve, 2 + q2 % 2, pq.ap()[:, 0:512],
                        pq.ap()[:, 512:1024]).then_inc(s_v.h, 1)
                s_v.n += 1
                tok_red4[q2] = s_v.n

            v(_v_bn4)

            def _s_rstd4(sc, q2=q2):
                _wait(sc, s_v, tok_red4[q2])
                rstd_pass(sc, 2 + q2 % 2).then_inc(s_s.h, 1)
                s_s.n += 1
                tok_ap4[q2] = s_s.n

            s(_s_rstd4)

            def _v_nm4(ve, q2=q2):
                _wait(ve, s_s, tok_ap4[q2])
                sl = 2 + q2 % 2
                ve.tensor_scalar(lnst.ap()[:, sl, 10:11],
                                 lnst.ap()[:, sl, 12:13],
                                 lnst.ap()[:, sl, 14:15], -1.0,
                                 ALU.mult, ALU.mult).then_inc(s_v.h, 1)
                s_v.n += 1
                tok_nm4[q2] = s_v.n

            v(_v_nm4)

            def _s_ap4(sc, q2=q2, pq=pq):
                _wait(sc, s_v, tok_nm4[q2])
                sl = 2 + q2 % 2
                sc.activation(raw_roll.ap()[:, q2 % 2, :], pq.ap(),
                              AF.Identity,
                              bias=lnst.ap()[:, sl, 10:11],
                              scale=lnst.ap()[:, sl, 14:15]
                              ).then_inc(s_s.h, 1)
                s_s.n += 1
                tok_ap4v[q2] = s_s.n

            s(_s_ap4)

            def _v_x2(ve, q2=q2):
                _wait(ve, s_s, tok_ap4v[q2])
                ve.tensor_tensor(x_sb.ap()[:, q2, :], x_sb.ap()[:, q2, :],
                                 raw_roll.ap()[:, q2 % 2, :],
                                 ALU.add).then_inc(s_v.h, 1)
                s_v.n += 1
                tok_x2[q2] = s_v.n

            v(_v_x2)

            def _st(eng, q2=q2):
                _wait(eng, s_v, tok_x2[q2])
                eng.dma_start(
                    out_hbm.rearrange("(m p) f -> p m f", p=128)[:, q2, :],
                    x_sb.ap()[:, q2, :]).then_inc(s_go.h, 16)
                s_go.n += 16

            (g if q2 % 2 == 0 else y)(_st)

        if m3 < NT:
            if m3 >= 2:
                emit_ln3a(m3)
            emit_ln3b(m3)

    def _g_final(gp):
        gp.wait_ge(s_go.h, 16 * NT)

    g(_g_final)


_BUILT = None


def _get_nc():
    global _BUILT
    if _BUILT is None:
        _BUILT = build_nc()
    return _BUILT


def kernel(L0_attn_x, L0_attn_Wg, L0_attn_bg, L0_attn_Wv, L0_attn_bv,
           L0_attn_Wo, L0_attn_bo, L0_ffn_W, L0_ffn_b, L0_ffn_lora_down,
           L0_ffn_lora_up, L0_ffn_down_aux, L0_ffn_up_aux):
    x = np.ascontiguousarray(np.asarray(L0_attn_x, dtype=np.float32))
    B = x.shape[0]
    f32 = lambda a: np.ascontiguousarray(np.asarray(a, dtype=np.float32))
    k = np.arange(128)[:, None]
    i = np.arange(128)[None, :]
    mpr = np.where(k >= i, 0.0, NEG).astype(np.float32)
    msa = np.where(k <= i, 0.0, NEG).astype(np.float32)
    pack = np.zeros((128, PACK_N), np.float32)
    pack[:, PK_CLT:PK_CLT + 128] = (k >= i).astype(np.float32)
    pack[:, PK_CUT:PK_CUT + 128] = (k <= i).astype(np.float32)
    pack[:, PK_IDN:PK_IDN + 128] = np.eye(128, dtype=np.float32)
    pack[:, PK_MR:PK_MR + 256] = np.concatenate([mpr, msa], axis=1)
    daT = f32(np.asarray(L0_ffn_down_aux).T)
    daT_packed = np.ascontiguousarray(
        daT.reshape(8, 128, 100).transpose(1, 0, 2).reshape(128, 800))
    biases = np.concatenate(
        [f32(L0_attn_bg), f32(L0_attn_bv), f32(L0_attn_bo),
         f32(L0_ffn_b)]).reshape(1, 4 * D)
    shared = {
        "wgT": f32(np.asarray(L0_attn_Wg).T),
        "wvT": f32(np.asarray(L0_attn_Wv).T),
        "woT": f32(np.asarray(L0_attn_Wo).T),
        "wfT": f32(np.asarray(L0_ffn_W).T),
        "pack": np.ascontiguousarray(pack),
        "biases": np.ascontiguousarray(biases),
        "lu": f32(L0_ffn_lora_up), "ld": f32(L0_ffn_lora_down),
        "uaT": f32(np.asarray(L0_ffn_up_aux).T),
        "daT": daT_packed,
    }
    in_maps = []
    for c in range(B):
        m = {"x": np.ascontiguousarray(x[c])}
        m.update(shared)
        in_maps.append(m)
    nc = _get_nc()
    res = run_bass_kernel_spmd(nc, in_maps, core_ids=list(range(B)))
    out = np.stack([np.asarray(r["out"]) for r in res.results], axis=0)
    return out.astype(np.float32)


# version 28
# speedup vs baseline: 1.0293x; 1.0293x over previous
import numpy as np
import ml_dtypes
import concourse.bass as bass
import concourse.mybir as mybir
from concourse.bass_utils import run_bass_kernel_spmd

F32 = mybir.dt.float32
BF16 = mybir.dt.bfloat16
AF = mybir.ActivationFunctionType
ALU = mybir.AluOpType
AX = mybir.AxisListType

T = 2048
D = 1024
H = 16
HD = 64
NT = 16
NK = 8
NEG = -1.0e10
ACT_SET_NL_EXP = 6


class Ctr:
    def __init__(self, h):
        self.h = h
        self.n = 0


_HW = {}


def _wait(eng, ctr, val):
    assert val <= ctr.n, f"wait on future token {val} > {ctr.n}"
    if val <= 0:
        return
    key = (id(eng), id(ctr))
    if _HW.get(key, 0) >= val:
        return
    _HW[key] = val
    eng.wait_ge(ctr.h, val)


def fap(t, offset, dims):
    return bass.AP(t, offset, dims)


PK_CLT = 0
PK_CUT = 128
PK_IDN = 256
PK_MR = 384
PACK_N = 640


def build_nc():
    _HW.clear()
    nc = bass.Bass("TRN2", target_bir_lowering=False,
                   detect_race_conditions=False)

    x_hbm = nc.dram_tensor("x", [T, D], F32, kind="ExternalInput").ap()
    out_hbm = nc.dram_tensor("out", [T, D], F32, kind="ExternalOutput").ap()
    wgT_h = nc.dram_tensor("wgT", [D, D], F32, kind="ExternalInput").ap()
    wvT_h = nc.dram_tensor("wvT", [D, D], F32, kind="ExternalInput").ap()
    woT_h = nc.dram_tensor("woT", [D, D], F32, kind="ExternalInput").ap()
    wfT_h = nc.dram_tensor("wfT", [D, D], F32, kind="ExternalInput").ap()
    pack_h = nc.dram_tensor("pack", [128, PACK_N], F32,
                            kind="ExternalInput").ap()
    bias_h = nc.dram_tensor("biases", [1, 4 * D], F32,
                            kind="ExternalInput").ap()
    lu_h = nc.dram_tensor("lu", [50, 4], F32, kind="ExternalInput").ap()
    ld_h = nc.dram_tensor("ld", [4, 100], F32, kind="ExternalInput").ap()
    uaT_h = nc.dram_tensor("uaT", [50, D], F32, kind="ExternalInput").ap()
    daT_h = nc.dram_tensor("daT", [128, 800], F32, kind="ExternalInput").ap()

    A = nc.alloc_sbuf_tensor
    x_sb = A("x_sb", [128, NT, D], F32)
    gh_sb = A("gh_sb", [128, NT, D], BF16)
    V_sb = A("V_sb", [128, NT, H * 65], BF16)
    Wa = A("Wa", [128, NK, D], BF16)
    Wb = A("Wb", [128, NK, D], BF16)
    lnx_roll = A("lnx_roll", [128, 2, D], BF16)
    lnxT_win = A("lnxT_win", [128, NK, 2, 128], BF16, is_transpose_dest=True)
    ghT_win = A("ghT_win", [128, NK, 2, 128], BF16, is_transpose_dest=True)
    gtT_win = A("gtT_win", [128, NK, 2, 128], BF16, is_transpose_dest=True)
    gt_roll = A("gt_roll", [128, 2, D], BF16)
    raw_roll = A("raw_roll", [128, 2, D], BF16)
    rawT_win = A("rawT_win", [128, NK, 1, 128], BF16, is_transpose_dest=True)
    E_sb = A("E_sb", [128, H, 2, 128], BF16)
    b2_sb = A("b2_sb", [100, D], BF16)
    t1T_sb = A("t1T_sb", [100, 128], BF16)
    pack_sb = A("pack_sb", [128, PACK_N], BF16)
    bias_sb = A("bias_sb", [1, 4 * D], BF16)
    lnst = A("lnst", [128, 6, 15], F32)
    nsq_sb = A("nsq_sb", [128, 2, 16], F32)
    inv_sb = A("inv_sb", [128, 2, 16], F32)
    rs_sb = A("rs_sb", [128, H, 16], F32)
    f2_sb = rs_sb
    rq_bf = A("rq_bf", [128, 16], BF16)
    sir_bf = A("sir_bf", [1, 16], BF16)
    siv_sb = A("siv_sb", [128, 16], F32)
    sr2_sb = A("sr2_sb", [128, 16], F32)
    eps_sb = A("eps_sb", [128, 1], F32)
    zr_sb = lnst.ap()[:, 2, 0:8]
    pk = pack_sb.ap()
    clt_sb = pk[:, PK_CLT:PK_CLT + 128]
    cut_sb = pk[:, PK_CUT:PK_CUT + 128]
    idn_sb = pk[:, PK_IDN:PK_IDN + 128]
    mrhs_v = pk[:, PK_MR:PK_MR + 256]
    orow_bf = pk[0:1, PK_CUT:PK_CUT + 128]
    ocol_bf = pk[:, PK_CLT:PK_CLT + 1]
    b1_view = raw_roll.ap()[0:4, 0, :]
    uaT_view = rawT_win.ap().rearrange("p a b c -> p (a b c)")[0:50, 0:D]
    lu_view = gt_roll.ap()[0:50, 0, 0:4]
    ld_view = gt_roll.ap()[0:4, 0, 16:116]
    bb = bias_sb.ap()
    bg_view = bb[0:1, 0:D]
    bv_view = bb[0:1, D:2 * D]
    bo_bf = bb[0:1, 2 * D:3 * D]
    bff_bf = bb[0:1, 3 * D:4 * D]
    daT_views = [fap(E_sb, 100 * k, [[H * 2 * 128, 128], [1, 100]])
                 for k in range(NK)]
    daT_dst = fap(E_sb, 0, [[H * 2 * 128, 128], [1, 800]])
    assert nc.sbuf_bytes_remaining >= 0, nc.sbuf_bytes_remaining

    PA = nc.alloc_psum_tensor("PA", [128, 1024], F32)
    PB = nc.alloc_psum_tensor("PB", [128, 1024], F32)
    PC = nc.alloc_psum_tensor("PC", [128, 1024], F32)
    PD = nc.alloc_psum_tensor("PD", [128, 1024], F32)
    assert nc.psum_banks_remaining == 0

    import contextlib
    stack = contextlib.ExitStack()
    s_x = Ctr(stack.enter_context(nc.semaphore("s_x")))
    s_w1 = Ctr(stack.enter_context(nc.semaphore("s_w1")))
    s_w2 = Ctr(stack.enter_context(nc.semaphore("s_w2")))
    s_w3 = Ctr(stack.enter_context(nc.semaphore("s_w3")))
    s_w4 = Ctr(stack.enter_context(nc.semaphore("s_w4")))
    s_sm = Ctr(stack.enter_context(nc.semaphore("s_sm")))
    s_tp = Ctr(stack.enter_context(nc.semaphore("s_tp")))
    s_mm = Ctr(stack.enter_context(nc.semaphore("s_mm")))
    s_v = Ctr(stack.enter_context(nc.semaphore("s_v")))
    s_s = Ctr(stack.enter_context(nc.semaphore("s_s")))
    s_go = Ctr(stack.enter_context(nc.semaphore("s_go")))
    s_da = Ctr(stack.enter_context(nc.semaphore("s_da")))
    s_sp = Ctr(stack.enter_context(nc.semaphore("s_sp")))
    s_xc = [Ctr(stack.enter_context(nc.semaphore(f"s_xc{c}")))
            for c in range(8)]

    with stack:
        with nc.Block() as block:
            _emit(nc, block, locals())
    return nc


def _emit(nc, block, L):
    g = block.gpsimd
    t = block.tensor
    v = block.vector
    s = block.scalar
    y = block.sync

    (s_x, s_w1, s_w2, s_w3, s_w4, s_sm, s_tp, s_mm, s_v, s_s, s_go, s_da) = (
        L["s_x"], L["s_w1"], L["s_w2"], L["s_w3"], L["s_w4"], L["s_sm"],
        L["s_tp"], L["s_mm"], L["s_v"], L["s_s"], L["s_go"], L["s_da"])
    s_xc = L["s_xc"]
    s_sp = L["s_sp"]
    x_sb, gh_sb, V_sb, Wa, Wb = L["x_sb"], L["gh_sb"], L["V_sb"], L["Wa"], L["Wb"]
    lnx_roll, lnxT_win = L["lnx_roll"], L["lnxT_win"]
    ghT_win, gtT_win, gt_roll = L["ghT_win"], L["gtT_win"], L["gt_roll"]
    raw_roll, rawT_win, E_sb = L["raw_roll"], L["rawT_win"], L["E_sb"]
    b2_sb, t1T_sb = L["b2_sb"], L["t1T_sb"]
    bo_bf, bff_bf = L["bo_bf"], L["bff_bf"]
    clt_sb, cut_sb, idn_sb, mrhs_v = (L["clt_sb"], L["cut_sb"],
                                      L["idn_sb"], L["mrhs_v"])
    orow_bf, ocol_bf = L["orow_bf"], L["ocol_bf"]
    daT_views = L["daT_views"]
    lnst, nsq_sb, inv_sb = L["lnst"], L["nsq_sb"], L["inv_sb"]
    rs_sb, f2_sb = L["rs_sb"], L["f2_sb"]
    rq_bf, sir_bf, siv_sb, sr2_sb, zr_sb = (L["rq_bf"], L["sir_bf"],
                                            L["siv_sb"], L["sr2_sb"],
                                            L["zr_sb"])
    eps_sb = L["eps_sb"]
    b1_view, uaT_view, daT_dst = L["b1_view"], L["uaT_view"], L["daT_dst"]
    lu_view, ld_view, bg_view, bv_view = (L["lu_view"], L["ld_view"],
                                          L["bg_view"], L["bv_view"])
    PA, PB, PC, PD = L["PA"], L["PB"], L["PC"], L["PD"]
    x_hbm, out_hbm = L["x_hbm"], L["out_hbm"]
    pack_h, bias_h = L["pack_h"], L["bias_h"]
    pack_sb, bias_sb = L["pack_sb"], L["bias_sb"]
    sq_scr = gt_roll.ap()[:, 1, :]

    def bn_pass(ve, slot, src_lo, src_hi):
        ve.bn_stats(lnst.ap()[:, slot, 0:6], src_lo)
        ve.bn_stats(lnst.ap()[:, slot, 6:12], src_hi)
        ve.drain()
        return ve.bn_aggr(lnst.ap()[:, slot, 12:14], lnst.ap()[:, slot, 0:12])

    def rstd_pass(sc, slot):
        sc.activation(lnst.ap()[:, slot, 14:15], lnst.ap()[:, slot, 13:14],
                      AF.Ln, scale=1.0, bias=eps_sb.ap())
        sc.drain()
        return sc.activation(lnst.ap()[:, slot, 14:15],
                             lnst.ap()[:, slot, 14:15], AF.Exp, scale=-0.5)

    def apply_pass(ve, slot, dst, src):
        return ve.tensor_scalar(dst, src, lnst.ap()[:, slot, 12:13],
                                lnst.ap()[:, slot, 14:15],
                                ALU.subtract, ALU.mult)

    tok = {}

    def _g_loads(gp):
        for dst, src in ((pack_sb.ap(), pack_h), (bias_sb.ap(), bias_h)):
            gp.dma_start(dst, src,
                         max_dma_last_dim=8192).then_inc(s_sp.h, 16)
            s_sp.n += 16
        gp.dma_start(Wa.ap(), L["wgT_h"].rearrange("(k p) o -> p k o", p=128)
                     ).then_inc(s_w1.h, 16)
        s_w1.n += 16
        gp.dma_start(Wb.ap(), L["wvT_h"].rearrange("(k p) o -> p k o", p=128)
                     ).then_inc(s_w2.h, 16)
        s_w2.n += 16
        xr = x_hbm.rearrange("(m p) f -> p m f", p=128)
        for c in (2, 3):
            gp.dma_start(x_sb.ap()[:, 2 * c:2 * c + 2, :],
                         xr[:, 2 * c:2 * c + 2, :]).then_inc(s_xc[c].h, 16)
            s_xc[c].n += 16
        for dst, src in ((uaT_view, L["uaT_h"]), (lu_view, L["lu_h"]),
                         (ld_view, L["ld_h"])):
            gp.dma_start(dst, src,
                         max_dma_last_dim=8192).then_inc(s_sm.h, 16)
            s_sm.n += 16
        for c in (4, 5, 6, 7):
            gp.dma_start(x_sb.ap()[:, 2 * c:2 * c + 2, :],
                         xr[:, 2 * c:2 * c + 2, :]).then_inc(s_xc[c].h, 16)
            s_xc[c].n += 16

    g(_g_loads)
    tok["sm"] = s_sm.n
    tok["sp"] = s_sp.n

    def _s_xload(sc):
        xr = x_hbm.rearrange("(m p) f -> p m f", p=128)
        for c in (0, 1):
            sc.dma_start(x_sb.ap()[:, 2 * c:2 * c + 2, :],
                         xr[:, 2 * c:2 * c + 2, :]).then_inc(s_xc[c].h, 16)
            s_xc[c].n += 16

    s(_s_xload)

    def _s_table(sc):
        sc.add_instruction(mybir.InstLoadActFuncSet(
            name=sc.bass.get_next_instruction_name(),
            act_func_set_id=ACT_SET_NL_EXP, ins=[], outs=[]))

    s(_s_table)

    def _v_ones(ve):
        ve.memset(fap(V_sb, 64, [[NT * H * 65, 128], [65 * H, NT], [65, H],
                                 [1, 1]]), 1.0)
        ve.memset(eps_sb.ap(), 1e-5).then_inc(s_v.h, 1)
        s_v.n += 1

    v(_v_ones)
    tok["vones"] = s_v.n

    tok_red1 = [0] * NT
    tok_ap1 = [0] * NT
    tok_ap1v = [0] * NT
    tok_tp1 = [0] * NT
    tok_gmm = [0] * NT
    tok_vmm = [0] * NT
    tok_gsq = [0] * NT
    tok_nsq = [0] * NT
    tok_inv = [0] * NT
    tok_ghev = [0] * NT
    tok_vev = [0] * NT
    tok_cmm = [0] * NT
    tok_dot = [0] * NT

    def kext(te, dst, bias_row, n):
        te.matmul(dst, orow_bf, bias_row[:, n * 512:(n + 1) * 512],
                  start=False, stop=True).then_inc(s_mm.h, 1)
        s_mm.n += 1

    def emit_ln1a(m):
        def _v_bn1(ve, m=m):
            _wait(ve, s_xc[m // 2], 16)
            bn_pass(ve, 2 + m % 2, x_sb.ap()[:, m, 0:512],
                    x_sb.ap()[:, m, 512:1024]).then_inc(s_v.h, 1)
            s_v.n += 1
            tok_red1[m] = s_v.n

        v(_v_bn1)

        def _s_rstd1(sc, m=m):
            _wait(sc, s_v, tok_red1[m])
            rstd_pass(sc, 2 + m % 2).then_inc(s_s.h, 1)
            s_s.n += 1
            tok_ap1[m] = s_s.n

        s(_s_rstd1)

    def emit_ln1b(m):
        def _v_ap1(ve, m=m):
            _wait(ve, s_s, tok_ap1[m])
            if m >= 2:
                _wait(ve, s_tp, tok_tp1[m - 2])
            apply_pass(ve, 2 + m % 2, lnx_roll.ap()[:, m % 2, :],
                       x_sb.ap()[:, m, :]).then_inc(s_v.h, 1)
            s_v.n += 1
            tok_ap1v[m] = s_v.n

        v(_v_ap1)

        def _y_tp1(sy, m=m):
            _wait(sy, s_v, tok_ap1v[m])
            if m >= 2:
                _wait(sy, s_mm, tok_vmm[m - 2])
            sy.dma_start_transpose(
                fap(lnxT_win, (m % 2) * 128,
                    [[NK * 256, 128], [256, NK], [1, 128]]),
                fap(lnx_roll, (m % 2) * D,
                    [[2 * D, 128], [128, NK], [1, 128]]),
            ).then_inc(s_tp.h, 16)
            s_tp.n += 16
            tok_tp1[m] = s_tp.n

        y(_y_tp1)

    emit_ln1a(0)
    emit_ln1b(0)

    for st in range(NT + 2):
        m2 = st + 1
        mp = st - 1
        qb = st - 2
        if 0 <= qb < NT:
            def _t_band(te, qb=qb):
                ps = PA if qb % 2 == 0 else PB
                _wait(te, s_s, tok_gsq[qb])
                _wait(te, s_v, tok_ghev[qb])
                for n in range(2):
                    dst = ps.ap()[:, n * 512:(n + 1) * 512]
                    first = True
                    if qb > 0:
                        te.matmul(dst, clt_sb,
                                  gh_sb.ap()[:, qb - 1, n * 512:(n + 1) * 512],
                                  start=True, stop=False)
                        first = False
                    te.matmul(dst, cut_sb,
                              gh_sb.ap()[:, qb, n * 512:(n + 1) * 512],
                              start=first, stop=True).then_inc(s_mm.h, 1)
                    s_mm.n += 1
                tok_cmm[qb] = s_mm.n

            t(_t_band)

        if 0 <= mp < NT:
            pg = PA if mp % 2 == 0 else PB
            pv = PC if mp % 2 == 0 else PD

            def _t_gv(te, mp=mp, pg=pg, pv=pv):
                _wait(te, s_tp, tok_tp1[mp])
                _wait(te, s_w1, 16)
                _wait(te, s_sp, tok["sp"])
                if mp >= 2:
                    _wait(te, s_v, tok_vev[mp - 2])
                for n in range(2):
                    dst = pg.ap()[:, n * 512:(n + 1) * 512]
                    for k in range(NK):
                        te.matmul(dst, lnxT_win.ap()[:, k, mp % 2, :],
                                  Wa.ap()[:, k, n * 512:(n + 1) * 512],
                                  start=(k == 0), stop=False)
                    kext(te, dst, bg_view, n)
                tok_gmm[mp] = s_mm.n
                _wait(te, s_w2, 16)
                for n in range(2):
                    dst = pv.ap()[:, n * 512:(n + 1) * 512]
                    for k in range(NK):
                        te.matmul(dst, lnxT_win.ap()[:, k, mp % 2, :],
                                  Wb.ap()[:, k, n * 512:(n + 1) * 512],
                                  start=(k == 0), stop=False)
                    kext(te, dst, bv_view, n)
                tok_vmm[mp] = s_mm.n

            t(_t_gv)

            def _s_gsq(sc, mp=mp, pg=pg):
                _wait(sc, s_mm, tok_gmm[mp])
                sc.activation(sq_scr, pg.ap(), AF.Square).then_inc(s_s.h, 1)
                s_s.n += 1
                tok_gsq[mp] = s_s.n

            s(_s_gsq)

        if m2 < NT:
            emit_ln1a(m2)

        if 0 <= mp < NT:
            def _v_nsq(ve, mp=mp):
                _wait(ve, s_s, tok_gsq[mp])
                if mp >= 2:
                    _wait(ve, s_s, tok_inv[mp - 2])
                ve.tensor_reduce(nsq_sb.ap()[:, mp % 2, :],
                                 gt_roll.ap()[:, 1, :].rearrange(
                                     "p (h d) -> p h d", h=16),
                                 AX.X, ALU.add).then_inc(s_v.h, 1)
                s_v.n += 1
                tok_nsq[mp] = s_v.n

            v(_v_nsq)

            def _s_inv(sc, mp=mp):
                _wait(sc, s_v, tok_nsq[mp])
                sc.activation(inv_sb.ap()[:, mp % 2, :],
                              nsq_sb.ap()[:, mp % 2, :], AF.Ln)
                sc.drain()
                sc.activation(inv_sb.ap()[:, mp % 2, :],
                              inv_sb.ap()[:, mp % 2, :], AF.Exp,
                              scale=-0.5).then_inc(s_s.h, 1)
                s_s.n += 1
                tok_inv[mp] = s_s.n

            s(_s_inv)

            def _v_ghev(ve, mp=mp, pg=pg):
                _wait(ve, s_s, tok_inv[mp])
                ve.tensor_tensor(
                    gh_sb.ap()[:, mp, :].rearrange("p (h d) -> p h d", h=16),
                    pg.ap().rearrange("p (h d) -> p h d", h=16),
                    fap(inv_sb, (mp % 2) * 16, [[32, 128], [1, 16], [0, 64]]),
                    ALU.mult).then_inc(s_v.h, 1)
                s_v.n += 1
                tok_ghev[mp] = s_v.n

            v(_v_ghev)

        if 0 <= qb < NT:
            def _v_dot(ve, dq=qb):
                ps = PA if dq % 2 == 0 else PB
                _wait(ve, s_mm, tok_cmm[dq])
                ve.tensor_tensor(raw_roll.ap()[:, 0, :], gh_sb.ap()[:, dq, :],
                                 ps.ap(), ALU.mult)
                ve.drain()
                ve.tensor_reduce(fap(rs_sb, dq, [[H * 16, 128], [16, H], [1, 1]]),
                                 raw_roll.ap()[:, 0, :].rearrange(
                                     "p (h d) -> p h d", h=16),
                                 AX.X, ALU.add).then_inc(s_v.h, 1)
                s_v.n += 1
                tok_dot[dq] = s_v.n

            v(_v_dot)

        if m2 < NT:
            emit_ln1b(m2)

        if 0 <= mp < NT:
            def _v_vev(ve, mp=mp, pv=pv):
                _wait(ve, s_mm, tok_vmm[mp])
                _wait(ve, s_v, tok["vones"])
                for n in range(2):
                    ve.tensor_copy(
                        fap(V_sb, mp * H * 65 + n * 8 * 65,
                            [[NT * H * 65, 128], [65, 8], [1, 64]]),
                        pv.ap()[:, n * 512:(n + 1) * 512]).then_inc(s_v.h, 1)
                    s_v.n += 1
                tok_vev[mp] = s_v.n

            v(_v_vev)

    def _g_w2(gp):
        _wait(gp, s_v, tok_vev[NT - 1])
        gp.dma_start(Wb.ap(), L["woT_h"].rearrange("(k p) o -> p k o", p=128)
                     ).then_inc(s_w3.h, 16)
        s_w3.n += 16
        gp.dma_start(Wa.ap(), L["wfT_h"].rearrange("(k p) o -> p k o", p=128)
                     ).then_inc(s_w4.h, 16)
        s_w4.n += 16

    g(_g_w2)

    def _t_l1(te):
        _wait(te, s_sm, tok["sm"])
        _wait(te, s_v, tok_vev[NT - 1])
        for n in range(2):
            te.matmul(PD.ap()[0:4, n * 512:(n + 1) * 512], lu_view,
                      uaT_view[:, n * 512:(n + 1) * 512], start=True,
                      stop=True).then_inc(s_mm.h, 1)
            s_mm.n += 1

    t(_t_l1)
    tok["b1"] = s_mm.n

    def _v_b1(ve):
        _wait(ve, s_mm, tok["b1"])
        ve.tensor_copy(b1_view, PD.ap()[0:4, :]).then_inc(s_v.h, 1)
        s_v.n += 1

    v(_v_b1)
    tok["b1e"] = s_v.n

    def _t_l2(te):
        _wait(te, s_v, tok["b1e"])
        for n in range(2):
            te.matmul(PD.ap()[0:100, n * 512:(n + 1) * 512], ld_view,
                      b1_view[:, n * 512:(n + 1) * 512], start=True,
                      stop=True).then_inc(s_mm.h, 1)
            s_mm.n += 1

    t(_t_l2)
    tok["b2"] = s_mm.n

    def _v_b2(ve):
        _wait(ve, s_mm, tok["b2"])
        ve.tensor_copy(b2_sb.ap(), PD.ap()[0:100, :]).then_inc(s_v.h, 1)
        s_v.n += 1

    v(_v_b2)
    tok["b2e"] = s_v.n


    def _v_fit1(ve):
        _wait(ve, s_v, tok_dot[NT - 1])
        ve.tensor_scalar(rs_sb.ap(), rs_sb.ap(), 1.0 / 4096.0, 1.0,
                         ALU.mult, ALU.add).then_inc(s_v.h, 1)
        s_v.n += 1

    v(_v_fit1)
    tok["fit1"] = s_v.n

    def _s_fit(sc):
        _wait(sc, s_v, tok["fit1"])
        sc.activation(rs_sb.ap(), rs_sb.ap(), AF.Ln)
        sc.drain()
        sc.activation(rs_sb.ap(), rs_sb.ap(), AF.Exp,
                      scale=-1.0).then_inc(s_s.h, 1)
        s_s.n += 1

    s(_s_fit)
    tok["fit2"] = s_s.n

    def _v_fit2(ve):
        _wait(ve, s_s, tok["fit2"])
        ve.tensor_reduce(siv_sb.ap(), rs_sb.ap(), AX.X,
                         ALU.add).then_inc(s_v.h, 1)
        s_v.n += 1

    v(_v_fit2)
    tok["siv"] = s_v.n

    def _s_rq(sc):
        _wait(sc, s_v, tok["siv"])
        sc.activation(rq_bf.ap(), siv_sb.ap(),
                      AF.Copy).then_inc(s_s.h, 1)
        s_s.n += 1

    s(_s_rq)
    tok["rq"] = s_s.n

    def _t_si(te):
        _wait(te, s_s, tok["rq"])
        _wait(te, s_v, tok_vev[14])
        te.matmul(PC.ap()[0:1, 0:16], ocol_bf, rq_bf.ap(), start=True,
                  stop=True).then_inc(s_mm.h, 1)
        s_mm.n += 1

    t(_t_si)
    tok["si"] = s_mm.n

    def _v_si(ve):
        _wait(ve, s_mm, tok["si"])
        ve.tensor_copy(sir_bf.ap(), PC.ap()[0:1, 0:16]).then_inc(s_v.h, 1)
        s_v.n += 1

    v(_v_si)
    tok["sir"] = s_v.n

    def _t_sib(te):
        _wait(te, s_v, tok["sir"])
        te.matmul(PC.ap()[:, 512:528], orow_bf, sir_bf.ap(), start=True,
                  stop=True).then_inc(s_mm.h, 1)
        s_mm.n += 1

    t(_t_sib)
    tok["sib"] = s_mm.n

    def _v_fcp(ve):
        _wait(ve, s_mm, tok["sib"])
        ve.tensor_copy(siv_sb.ap(), PC.ap()[:, 512:528]).then_inc(s_v.h, 1)
        s_v.n += 1

    v(_v_fcp)
    tok["fcp"] = s_v.n

    def _s_sr2(sc):
        _wait(sc, s_v, tok["fcp"])
        sc.activation(sr2_sb.ap(), siv_sb.ap(), AF.Ln, scale=2.0)
        sc.drain()
        sc.activation(sr2_sb.ap(), sr2_sb.ap(), AF.Exp,
                      scale=-1.0).then_inc(s_s.h, 1)
        s_s.n += 1

    s(_s_sr2)
    tok["sr2"] = s_s.n

    def _v_f2(ve):
        _wait(ve, s_s, tok["sr2"])
        ve.tensor_tensor(f2_sb.ap(), rs_sb.ap(),
                         fap(sr2_sb, 0, [[16, 128], [1, 16], [0, 16]]),
                         ALU.mult).then_inc(s_v.h, 1)
        s_v.n += 1

    v(_v_f2)
    tok["f2"] = s_v.n

    tok_ghT = [0] * NT
    tok_gt = [0] * NT
    tok_gtT = [0] * NT
    tok_smm = [[0] * 4 for _ in range(NT)]
    tok_exp = [[0] * 4 for _ in range(NT)]
    tok_av = [[0] * 4 for _ in range(NT)]
    tok_nrm = [[0] * 4 for _ in range(NT)]
    tok_rawT = [0] * NT
    tok_omm = [0] * NT
    tok_red2 = [0] * NT
    tok_ap2 = [0] * NT
    tok_ap2v = [0] * NT
    tok_ln2 = [0] * NT

    TP_DIMS = [[NK * 256, 128], [256, NK], [1, 128]]

    def ghT_slot(qb):
        s = qb % 3
        if s < 2:
            return fap(ghT_win, s * 128, TP_DIMS)
        return fap(lnxT_win, 0, TP_DIMS)

    def gtT_slot(qb):
        s = qb % 3
        if s < 2:
            return fap(gtT_win, s * 128, TP_DIMS)
        return fap(lnxT_win, 128, TP_DIMS)

    def emit_gt_and_T(qb):
        def _v_gt(ve, qb=qb):
            _wait(ve, s_v, tok["f2"])
            if qb >= 2:
                _wait(ve, s_tp, tok_gtT[qb - 2])
            ve.tensor_tensor(
                gt_roll.ap()[:, qb % 2, :].rearrange("p (h d) -> p h d", h=16),
                gh_sb.ap()[:, qb, :].rearrange("p (h d) -> p h d", h=16),
                fap(f2_sb, qb, [[H * 16, 128], [16, H], [0, 64]]),
                ALU.mult).then_inc(s_v.h, 1)
            s_v.n += 1
            tok_gt[qb] = s_v.n

        v(_v_gt)

        def _y_T(sy, qb=qb):
            if qb % 3 == 2:
                _wait(sy, s_mm, tok_vmm[NT - 1])
            if qb >= 3:
                _wait(sy, s_mm, tok_smm[qb - 2][3])
            sy.dma_start_transpose(
                ghT_slot(qb),
                fap(gh_sb, qb * D, [[NT * D, 128], [128, NK], [1, 128]]),
            ).then_inc(s_tp.h, 16)
            s_tp.n += 16
            tok_ghT[qb] = s_tp.n
            _wait(sy, s_v, tok_gt[qb])
            sy.dma_start_transpose(
                gtT_slot(qb),
                fap(gt_roll, (qb % 2) * D,
                    [[2 * D, 128], [128, NK], [1, 128]]),
            ).then_inc(s_tp.h, 16)
            s_tp.n += 16
            tok_gtT[qb] = s_tp.n

        y(_y_T)

    def emit_sc(qb, grp):
        def _t_sc(te, qb=qb, grp=grp):
            ps = PA if grp % 2 == 0 else PB
            _wait(te, s_tp, tok_ghT[qb])
            _wait(te, s_tp, tok_gtT[qb])
            if qb > 0:
                _wait(te, s_tp, tok_ghT[qb - 1])
            if grp >= 2:
                _wait(te, s_s, tok_exp[qb][grp - 2])
            elif qb > 0:
                _wait(te, s_s, tok_exp[qb - 1][grp + 2])
            else:
                _wait(te, s_v, tok_dot[NT - 1])
            hlist = [grp * 4, grp * 4 + 2, grp * 4 + 1, grp * 4 + 3]
            ghq = ghT_slot(qb)
            gtq = gtT_slot(qb)
            ghp = ghT_slot(qb - 1) if qb > 0 else None
            mms = []
            for i, h in enumerate(hlist):
                p0 = (h % 2) * 64
                lhs_same = ghq[p0:p0 + 64, h // 2, :]
                rhs_q = gtq[p0:p0 + 64, h // 2, :]
                if qb > 0:
                    mms.append((ps.ap()[:, i * 256:i * 256 + 128],
                                ghp[p0:p0 + 64, h // 2, :], rhs_q))
                mms.append((ps.ap()[:, i * 256 + 128:i * 256 + 256],
                            lhs_same, rhs_q))
            if qb > 0:
                for half in range(2):
                    mms.append((ps.ap()[:, half * 512:(half + 1) * 512],
                                idn_sb,
                                fap(pack_sb, PK_MR,
                                    [[PACK_N, 128], [0, 2], [1, 256]])))
            else:
                for i in range(4):
                    mms.append((ps.ap()[:, i * 256 + 128:i * 256 + 256],
                                idn_sb, mrhs_v[:, 128:256]))
            first = {0: True, 1: True}
            last = {}
            for j, (dst, _, _) in enumerate(mms):
                last[dst.offset // 512] = j
            for j, (dst, lhsT, rhs) in enumerate(mms):
                bank = dst.offset // 512
                bi = te.matmul(dst, lhsT, rhs, start=first[bank],
                               stop=(last[bank] == j))
                first[bank] = False
                if last[bank] == j:
                    bi.then_inc(s_mm.h, 1)
                    s_mm.n += 1
            tok_smm[qb][grp] = s_mm.n

        t(_t_sc)

    def emit_exp(qb, grp):
        def _s_exp(sc, qb=qb, grp=grp):
            ps = PA if grp % 2 == 0 else PB
            _wait(sc, s_mm, tok_smm[qb][grp])
            if qb == 0:
                sc.activation(
                    fap(E_sb, grp * 4 * 256 + 128,
                        [[H * 256, 128], [256, 4], [1, 128]]),
                    fap(ps, 128, [[1024, 128], [256, 4], [1, 128]]),
                    AF.Exp).then_inc(s_s.h, 1)
            else:
                sc.activation(
                    fap(E_sb, grp * 4 * 256, [[H * 256, 128], [1, 1024]]),
                    ps.ap(), AF.Exp).then_inc(s_s.h, 1)
            s_s.n += 1
            tok_exp[qb][grp] = s_s.n

        s(_s_exp)

    def emit_av(qb, grp):
        def _t_av(te, qb=qb, grp=grp):
            half = (grp % 2) * 512
            _wait(te, s_s, tok_exp[qb][grp])
            _wait(te, s_v, tok_vev[NT - 1])
            if grp >= 2:
                _wait(te, s_v, tok_nrm[qb][grp - 2])
            elif qb > 0:
                _wait(te, s_v, tok_nrm[qb - 1][grp + 2])
            elif grp == 0:
                _wait(te, s_v, tok["sir"])
            else:
                _wait(te, s_v, tok["f2"])
            n_mm = 4 * (2 if qb > 0 else 1)
            hlist = [grp * 4, grp * 4 + 2, grp * 4 + 1, grp * 4 + 3]
            j = 0
            for i, h in enumerate(hlist):
                dst = fap(PC, half + i * 65, [[1024, 128], [1, 65]])
                epos = grp * 4 + i
                if qb > 0:
                    te.matmul(dst, E_sb.ap()[:, epos, 0, :],
                              fap(V_sb, (qb - 1) * H * 65 + h * 65,
                                  [[NT * H * 65, 128], [1, 65]]),
                              start=(j == 0), stop=False)
                    j += 1
                bi = te.matmul(dst, E_sb.ap()[:, epos, 1, :],
                               fap(V_sb, qb * H * 65 + h * 65,
                                   [[NT * H * 65, 128], [1, 65]]),
                               start=(j == 0), stop=(j == n_mm - 1))
                j += 1
                if j == n_mm:
                    bi.then_inc(s_mm.h, 1)
                    s_mm.n += 1
            tok_av[qb][grp] = s_mm.n

        t(_t_av)

    tok_zr = [[0] * 4 for _ in range(NT)]

    def emit_zr(qb, pair):
        def _s_zr(sc, qb=qb, pair=pair):
            _wait(sc, s_mm, tok_av[qb][2 * pair + 1])
            if qb > 0:
                _wait(sc, s_v, tok_nrm[qb - 1][3])
            z = fap(lnst, (2 + pair) * 15, [[90, 128], [1, 8]])
            sc.activation(z, fap(PC, 64, [[1024, 128], [512, 2], [65, 4],
                                          [1, 1]]), AF.Ln)
            sc.drain()
            sc.activation(z, z, AF.Exp, scale=-1.0).then_inc(s_s.h, 1)
            s_s.n += 1
            tok_zr[qb][pair] = s_s.n

        s(_s_zr)

    def emit_nrm(qb, grp):
        def _v_nrm(ve, qb=qb, grp=grp):
            half = (grp % 2) * 512
            _wait(ve, s_s, tok_zr[qb][grp // 2])
            if qb >= 2 and grp == 0:
                _wait(ve, s_tp, tok_rawT[qb - 2])
            ve.tensor_tensor(
                fap(raw_roll, (qb % 2) * D + grp * 256,
                    [[2 * D, 128], [64, 2], [128, 2], [1, 64]]),
                fap(PC, half, [[1024, 128], [130, 2], [65, 2], [1, 64]]),
                fap(lnst, (2 + grp // 2) * 15 + (grp % 2) * 4,
                    [[90, 128], [2, 2], [1, 2], [0, 64]]),
                ALU.mult).then_inc(s_v.h, 1)
            s_v.n += 1
            tok_nrm[qb][grp] = s_v.n

        v(_v_nrm)

    def emit_rawT(qb):
        def _s_rawT(sc, qb=qb):
            _wait(sc, s_v, tok_nrm[qb][3])
            if qb >= 1:
                _wait(sc, s_mm, tok_omm[qb - 1])
            sc.dma_start_transpose(
                fap(rawT_win, 0, [[NK * 128, 128], [128, NK], [1, 128]]),
                fap(raw_roll, (qb % 2) * D, [[2 * D, 128], [128, NK], [1, 128]]),
            ).then_inc(s_tp.h, 16)
            s_tp.n += 16
            tok_rawT[qb] = s_tp.n

        s(_s_rawT)

    def emit_o(qb):
        def _t_o(te, qb=qb):
            _wait(te, s_tp, tok_rawT[qb])
            _wait(te, s_w3, 16)
            if qb == 0:
                _wait(te, s_v, tok["b2e"])
                _wait(te, s_v, tok_vev[15])
            else:
                _wait(te, s_v, tok_ap2v[qb - 1])
            for n in range(2):
                dst = PD.ap()[:, n * 512:(n + 1) * 512]
                for k in range(NK):
                    te.matmul(dst, rawT_win.ap()[:, k, 0, :],
                              Wb.ap()[:, k, n * 512:(n + 1) * 512],
                              start=(k == 0), stop=False)
                kext(te, dst, bo_bf, n)
            tok_omm[qb] = s_mm.n

        t(_t_o)

    def emit_ln2a(qb):
        def _v_bn2(ve, qb=qb):
            _wait(ve, s_mm, tok_omm[qb])
            bn_pass(ve, 4 + qb % 2, PD.ap()[:, 0:512],
                    PD.ap()[:, 512:1024]).then_inc(s_v.h, 1)
            s_v.n += 1
            tok_red2[qb] = s_v.n

        v(_v_bn2)

    def emit_ln2b(qb):
        def _s_r2(sc, qb=qb):
            _wait(sc, s_v, tok_red2[qb])
            rstd_pass(sc, 4 + qb % 2).then_inc(s_s.h, 1)
            s_s.n += 1
            tok_ap2[qb] = s_s.n

        s(_s_r2)

        def _v_ap2(ve, qb=qb):
            _wait(ve, s_s, tok_ap2[qb])
            apply_pass(ve, 4 + qb % 2, lnx_roll.ap()[:, qb % 2, :],
                       PD.ap()).then_inc(s_v.h, 1)
            s_v.n += 1
            tok_ap2v[qb] = s_v.n

        v(_v_ap2)

        def _v_add2(ve, qb=qb):
            ve.drain()
            ve.tensor_tensor(x_sb.ap()[:, qb, :], x_sb.ap()[:, qb, :],
                             lnx_roll.ap()[:, qb % 2, :],
                             ALU.add).then_inc(s_v.h, 1)
            s_v.n += 1
            tok_ln2[qb] = s_v.n

        v(_v_add2)

    tok_red3 = [0] * NT
    tok_ap3 = [0] * NT
    tok_ap3v = [0] * NT
    tok_tp2 = [0] * NT
    tok_nm3 = [0] * NT

    def emit_ln3a(m):
        def _v_bn3(ve, m=m):
            _wait(ve, s_v, tok_ln2[m])
            if m >= 2:
                _wait(ve, s_s, tok_ap3v[m - 2])
            bn_pass(ve, m % 2, x_sb.ap()[:, m, 0:512],
                    x_sb.ap()[:, m, 512:1024]).then_inc(s_v.h, 1)
            s_v.n += 1
            tok_red3[m] = s_v.n

        v(_v_bn3)

        def _s_rstd3(sc, m=m):
            _wait(sc, s_v, tok_red3[m])
            rstd_pass(sc, m % 2).then_inc(s_s.h, 1)
            s_s.n += 1
            tok_ap3[m] = s_s.n

        s(_s_rstd3)

        def _v_nm3(ve, m=m):
            _wait(ve, s_s, tok_ap3[m])
            ve.tensor_scalar(lnst.ap()[:, m % 2, 10:11],
                             lnst.ap()[:, m % 2, 12:13],
                             lnst.ap()[:, m % 2, 14:15], -1.0,
                             ALU.mult, ALU.mult).then_inc(s_v.h, 1)
            s_v.n += 1
            tok_nm3[m] = s_v.n

        v(_v_nm3)

    emit_gt_and_T(0)
    emit_gt_and_T(1)
    emit_gt_and_T(2)
    for qb in range(NT):
        emit_sc(qb, 0)
        emit_exp(qb, 0)
        emit_sc(qb, 1)
        emit_exp(qb, 1)
        if qb >= 2:
            emit_ln2b(qb - 2)
        emit_av(qb, 0)
        emit_sc(qb, 2)
        emit_exp(qb, 2)
        emit_av(qb, 1)
        emit_sc(qb, 3)
        emit_exp(qb, 3)
        emit_zr(qb, 0)
        emit_nrm(qb, 0)
        emit_nrm(qb, 1)
        if qb >= 1:
            emit_o(qb - 1)
        emit_av(qb, 2)
        emit_av(qb, 3)
        emit_zr(qb, 1)
        emit_nrm(qb, 2)
        emit_nrm(qb, 3)
        if qb >= 1:
            emit_ln2a(qb - 1)
        if 3 <= qb + 2 < NT:
            emit_gt_and_T(qb + 2)
        emit_rawT(qb)
        if qb == NT - 2:
            emit_ln3a(0)
            emit_ln3a(1)
    emit_ln2b(NT - 2)
    emit_o(NT - 1)
    emit_ln2a(NT - 1)
    emit_ln2b(NT - 1)

    def _g_daT(gp):
        _wait(gp, s_mm, tok_av[NT - 1][0])
        gp.dma_start(daT_dst, L["daT_h"]).then_inc(s_da.h, 16)
        s_da.n += 16

    g(_g_daT)

    tok_t1 = [0] * NT
    tok_t1e = [0] * NT
    tok_ymm = [0] * NT
    tok_red4 = [0] * NT
    tok_ap4 = [0] * NT
    tok_x2 = [0] * NT
    tok_ap4v = [0] * NT
    tok_nm4 = [0] * NT

    def emit_ln3b(m):
            def _s_ap3(sc, m=m):
                _wait(sc, s_v, tok_nm3[m])
                if m >= 2:
                    _wait(sc, s_tp, tok_tp2[m - 2])
                else:
                    _wait(sc, s_v, tok_ln2[14 + m])
                sc.activation(lnx_roll.ap()[:, m % 2, :], x_sb.ap()[:, m, :],
                              AF.Identity,
                              bias=lnst.ap()[:, m % 2, 10:11],
                              scale=lnst.ap()[:, m % 2, 14:15]
                              ).then_inc(s_s.h, 1)
                s_s.n += 1
                tok_ap3v[m] = s_s.n

            s(_s_ap3)

            def _y_tp2(sy, m=m):
                _wait(sy, s_s, tok_ap3v[m])
                if m < 2:
                    _wait(sy, s_mm, tok_smm[NT - 1][3])
                if m >= 2:
                    _wait(sy, s_mm, tok_ymm[m - 2])
                sy.dma_start_transpose(
                    fap(lnxT_win, (m % 2) * 128,
                        [[NK * 256, 128], [256, NK], [1, 128]]),
                    fap(lnx_roll, (m % 2) * D,
                        [[2 * D, 128], [128, NK], [1, 128]]),
                ).then_inc(s_tp.h, 16)
                s_tp.n += 16
                tok_tp2[m] = s_tp.n

            y(_y_tp2)

    emit_ln3b(0)

    for st in range(NT + 2):
        m3 = st + 1
        mp = st - 1
        q2 = st - 2
        if 0 <= mp < NT:
            py = PA if mp % 2 == 0 else PB

            def _t_t1(te, mp=mp):
                _wait(te, s_tp, tok_tp2[mp])
                _wait(te, s_da, 16)
                if mp == 0:
                    _wait(te, s_v, tok_nrm[NT - 1][2])
                else:
                    _wait(te, s_v, tok_t1e[mp - 1])
                for k in range(NK):
                    bi = te.matmul(PC.ap()[0:100, 0:128], daT_views[k],
                                   lnxT_win.ap()[:, k, mp % 2, :],
                                   start=(k == 0), stop=(k == NK - 1))
                bi.then_inc(s_mm.h, 1)
                s_mm.n += 1
                tok_t1[mp] = s_mm.n

            t(_t_t1)

            def _v_t1e(ve, mp=mp):
                _wait(ve, s_mm, tok_t1[mp])
                ve.tensor_copy(t1T_sb.ap(),
                               PC.ap()[0:100, 0:128]).then_inc(s_v.h, 1)
                s_v.n += 1
                tok_t1e[mp] = s_v.n

            v(_v_t1e)

            def _t_y(te, mp=mp, py=py):
                _wait(te, s_w4, 16)
                if mp < 2:
                    _wait(te, s_s, tok_exp[NT - 1][2 + mp])
                if mp >= 2:
                    _wait(te, s_s, tok_ap4v[mp - 2])
                for n in range(2):
                    dst = py.ap()[:, n * 512:(n + 1) * 512]
                    for k in range(NK):
                        te.matmul(dst, lnxT_win.ap()[:, k, mp % 2, :],
                                  Wa.ap()[:, k, n * 512:(n + 1) * 512],
                                  start=(k == 0), stop=False)
                    if n == 0:
                        _wait(te, s_v, tok_t1e[mp])
                    te.matmul(dst, t1T_sb.ap(),
                              b2_sb.ap()[:, n * 512:(n + 1) * 512],
                              start=False, stop=False)
                    kext(te, dst, bff_bf, n)
                tok_ymm[mp] = s_mm.n

            t(_t_y)

        if 0 <= q2 < NT:
            pq = PA if q2 % 2 == 0 else PB

            def _v_bn4(ve, q2=q2, pq=pq):
                _wait(ve, s_mm, tok_ymm[q2])
                if q2 < 2:
                    _wait(ve, s_s, tok_zr[NT - 1][1])
                else:
                    _wait(ve, s_s, tok_ap4v[q2 - 2])
                bn_pass(ve, 2 + q2 % 2, pq.ap()[:, 0:512],
                        pq.ap()[:, 512:1024]).then_inc(s_v.h, 1)
                s_v.n += 1
                tok_red4[q2] = s_v.n

            v(_v_bn4)

            def _s_rstd4(sc, q2=q2):
                _wait(sc, s_v, tok_red4[q2])
                rstd_pass(sc, 2 + q2 % 2).then_inc(s_s.h, 1)
                s_s.n += 1
                tok_ap4[q2] = s_s.n

            s(_s_rstd4)

            def _v_nm4(ve, q2=q2):
                _wait(ve, s_s, tok_ap4[q2])
                sl = 2 + q2 % 2
                ve.tensor_scalar(lnst.ap()[:, sl, 10:11],
                                 lnst.ap()[:, sl, 12:13],
                                 lnst.ap()[:, sl, 14:15], -1.0,
                                 ALU.mult, ALU.mult).then_inc(s_v.h, 1)
                s_v.n += 1
                tok_nm4[q2] = s_v.n

            v(_v_nm4)

            def _s_ap4(sc, q2=q2, pq=pq):
                _wait(sc, s_v, tok_nm4[q2])
                sl = 2 + q2 % 2
                sc.activation(raw_roll.ap()[:, q2 % 2, :], pq.ap(),
                              AF.Identity,
                              bias=lnst.ap()[:, sl, 10:11],
                              scale=lnst.ap()[:, sl, 14:15]
                              ).then_inc(s_s.h, 1)
                s_s.n += 1
                tok_ap4v[q2] = s_s.n

            s(_s_ap4)

            def _v_x2(ve, q2=q2):
                _wait(ve, s_s, tok_ap4v[q2])
                ve.tensor_tensor(x_sb.ap()[:, q2, :], x_sb.ap()[:, q2, :],
                                 raw_roll.ap()[:, q2 % 2, :],
                                 ALU.add).then_inc(s_v.h, 1)
                s_v.n += 1
                tok_x2[q2] = s_v.n

            v(_v_x2)

            def _st(eng, q2=q2):
                _wait(eng, s_v, tok_x2[q2])
                eng.dma_start(
                    out_hbm.rearrange("(m p) f -> p m f", p=128)[:, q2, :],
                    x_sb.ap()[:, q2, :]).then_inc(s_go.h, 16)
                s_go.n += 16

            (g if q2 % 2 == 0 else y)(_st)

        if m3 < NT:
            if m3 >= 2:
                emit_ln3a(m3)
            emit_ln3b(m3)

    def _g_final(gp):
        gp.wait_ge(s_go.h, 16 * NT)

    g(_g_final)


_BUILT = None


def _get_nc():
    global _BUILT
    if _BUILT is None:
        _BUILT = build_nc()
    return _BUILT


def kernel(L0_attn_x, L0_attn_Wg, L0_attn_bg, L0_attn_Wv, L0_attn_bv,
           L0_attn_Wo, L0_attn_bo, L0_ffn_W, L0_ffn_b, L0_ffn_lora_down,
           L0_ffn_lora_up, L0_ffn_down_aux, L0_ffn_up_aux):
    x = np.ascontiguousarray(np.asarray(L0_attn_x, dtype=np.float32))
    B = x.shape[0]
    f32 = lambda a: np.ascontiguousarray(np.asarray(a, dtype=np.float32))
    k = np.arange(128)[:, None]
    i = np.arange(128)[None, :]
    mpr = np.where(k >= i, 0.0, NEG).astype(np.float32)
    msa = np.where(k <= i, 0.0, NEG).astype(np.float32)
    pack = np.zeros((128, PACK_N), np.float32)
    pack[:, PK_CLT:PK_CLT + 128] = (k >= i).astype(np.float32)
    pack[:, PK_CUT:PK_CUT + 128] = (k <= i).astype(np.float32)
    pack[:, PK_IDN:PK_IDN + 128] = np.eye(128, dtype=np.float32)
    pack[:, PK_MR:PK_MR + 256] = np.concatenate([mpr, msa], axis=1)
    daT = f32(np.asarray(L0_ffn_down_aux).T)
    daT_packed = np.ascontiguousarray(
        daT.reshape(8, 128, 100).transpose(1, 0, 2).reshape(128, 800))
    biases = np.concatenate(
        [f32(L0_attn_bg), f32(L0_attn_bv), f32(L0_attn_bo),
         f32(L0_ffn_b)]).reshape(1, 4 * D)
    shared = {
        "wgT": f32(np.asarray(L0_attn_Wg).T),
        "wvT": f32(np.asarray(L0_attn_Wv).T),
        "woT": f32(np.asarray(L0_attn_Wo).T),
        "wfT": f32(np.asarray(L0_ffn_W).T),
        "pack": np.ascontiguousarray(pack),
        "biases": np.ascontiguousarray(biases),
        "lu": f32(L0_ffn_lora_up), "ld": f32(L0_ffn_lora_down),
        "uaT": f32(np.asarray(L0_ffn_up_aux).T),
        "daT": daT_packed,
    }
    in_maps = []
    for c in range(B):
        m = {"x": np.ascontiguousarray(x[c])}
        m.update(shared)
        in_maps.append(m)
    nc = _get_nc()
    res = run_bass_kernel_spmd(nc, in_maps, core_ids=list(range(B)))
    out = np.stack([np.asarray(r["out"]) for r in res.results], axis=0)
    return out.astype(np.float32)


# version 29
# speedup vs baseline: 1.0446x; 1.0149x over previous
import numpy as np
import ml_dtypes
import concourse.bass as bass
import concourse.mybir as mybir
from concourse.bass_utils import run_bass_kernel_spmd

F32 = mybir.dt.float32
BF16 = mybir.dt.bfloat16
AF = mybir.ActivationFunctionType
ALU = mybir.AluOpType
AX = mybir.AxisListType

T = 2048
D = 1024
H = 16
HD = 64
NT = 16
NK = 8
NEG = -1.0e10
ACT_SET_NL_EXP = 6


class Ctr:
    def __init__(self, h):
        self.h = h
        self.n = 0


_HW = {}


def _wait(eng, ctr, val):
    assert val <= ctr.n, f"wait on future token {val} > {ctr.n}"
    if val <= 0:
        return
    key = (id(eng), id(ctr))
    if _HW.get(key, 0) >= val:
        return
    _HW[key] = val
    eng.wait_ge(ctr.h, val)


def fap(t, offset, dims):
    return bass.AP(t, offset, dims)


PK_CLT = 0
PK_CUT = 128
PK_IDN = 256
PK_MR = 384
PACK_N = 640


def build_nc():
    _HW.clear()
    nc = bass.Bass("TRN2", target_bir_lowering=False,
                   detect_race_conditions=False)

    x_hbm = nc.dram_tensor("x", [T, D], F32, kind="ExternalInput").ap()
    out_hbm = nc.dram_tensor("out", [T, D], F32, kind="ExternalOutput").ap()
    wgT_h = nc.dram_tensor("wgT", [D, D], F32, kind="ExternalInput").ap()
    wvT_h = nc.dram_tensor("wvT", [D, D], F32, kind="ExternalInput").ap()
    woT_h = nc.dram_tensor("woT", [D, D], F32, kind="ExternalInput").ap()
    wfT_h = nc.dram_tensor("wfT", [D, D], F32, kind="ExternalInput").ap()
    pack_h = nc.dram_tensor("pack", [128, PACK_N], F32,
                            kind="ExternalInput").ap()
    bias_h = nc.dram_tensor("biases", [1, 4 * D], F32,
                            kind="ExternalInput").ap()
    lu_h = nc.dram_tensor("lu", [50, 4], F32, kind="ExternalInput").ap()
    ld_h = nc.dram_tensor("ld", [4, 100], F32, kind="ExternalInput").ap()
    uaT_h = nc.dram_tensor("uaT", [50, D], F32, kind="ExternalInput").ap()
    daT_h = nc.dram_tensor("daT", [128, 800], F32, kind="ExternalInput").ap()

    A = nc.alloc_sbuf_tensor
    x_sb = A("x_sb", [128, NT, D], F32)
    gh_sb = A("gh_sb", [128, NT, D], BF16)
    V_sb = A("V_sb", [128, NT, H * 65], BF16)
    Wa = A("Wa", [128, NK, D], BF16)
    Wb = A("Wb", [128, NK, D], BF16)
    lnx_roll = A("lnx_roll", [128, 2, D], BF16)
    lnxT_win = A("lnxT_win", [128, NK, 2, 128], BF16, is_transpose_dest=True)
    ghT_win = A("ghT_win", [128, NK, 2, 128], BF16, is_transpose_dest=True)
    gtT_win = A("gtT_win", [128, NK, 2, 128], BF16, is_transpose_dest=True)
    gt_roll = A("gt_roll", [128, 2, D], BF16)
    raw_roll = A("raw_roll", [128, 2, D], BF16)
    rawT_win = A("rawT_win", [128, NK, 1, 128], BF16, is_transpose_dest=True)
    E_sb = A("E_sb", [128, H, 2, 128], BF16)
    b2_sb = A("b2_sb", [100, D], BF16)
    t1T_sb = A("t1T_sb", [100, 128], BF16)
    pack_sb = A("pack_sb", [128, PACK_N], BF16)
    bias_sb = A("bias_sb", [1, 4 * D], BF16)
    lnst = A("lnst", [128, 6, 15], F32)
    nsq_sb = A("nsq_sb", [128, 2, 16], F32)
    inv_sb = A("inv_sb", [128, 2, 16], F32)
    rs_sb = A("rs_sb", [128, H, 16], F32)
    f2_sb = rs_sb
    rq_bf = A("rq_bf", [128, 16], BF16)
    sir_bf = A("sir_bf", [1, 16], BF16)
    siv_sb = A("siv_sb", [128, 16], F32)
    sr2_sb = A("sr2_sb", [128, 16], F32)
    eps_sb = A("eps_sb", [128, 1], F32)
    zr_sb = lnst.ap()[:, 2, 0:8]
    pk = pack_sb.ap()
    clt_sb = pk[:, PK_CLT:PK_CLT + 128]
    cut_sb = pk[:, PK_CUT:PK_CUT + 128]
    idn_sb = pk[:, PK_IDN:PK_IDN + 128]
    mrhs_v = pk[:, PK_MR:PK_MR + 256]
    orow_bf = pk[0:1, PK_CUT:PK_CUT + 128]
    ocol_bf = pk[:, PK_CLT:PK_CLT + 1]
    b1_view = raw_roll.ap()[0:4, 0, :]
    uaT_view = rawT_win.ap().rearrange("p a b c -> p (a b c)")[0:50, 0:D]
    lu_view = gt_roll.ap()[0:50, 0, 0:4]
    ld_view = gt_roll.ap()[0:4, 0, 16:116]
    bb = bias_sb.ap()
    bg_view = bb[0:1, 0:D]
    bv_view = bb[0:1, D:2 * D]
    bo_bf = bb[0:1, 2 * D:3 * D]
    bff_bf = bb[0:1, 3 * D:4 * D]
    daT_views = [fap(E_sb, 100 * k, [[H * 2 * 128, 128], [1, 100]])
                 for k in range(NK)]
    daT_dst = fap(E_sb, 0, [[H * 2 * 128, 128], [1, 800]])
    assert nc.sbuf_bytes_remaining >= 0, nc.sbuf_bytes_remaining

    PA = nc.alloc_psum_tensor("PA", [128, 1024], F32)
    PB = nc.alloc_psum_tensor("PB", [128, 1024], F32)
    PC = nc.alloc_psum_tensor("PC", [128, 1024], F32)
    PD = nc.alloc_psum_tensor("PD", [128, 1024], F32)
    assert nc.psum_banks_remaining == 0

    import contextlib
    stack = contextlib.ExitStack()
    s_x = Ctr(stack.enter_context(nc.semaphore("s_x")))
    s_w1 = Ctr(stack.enter_context(nc.semaphore("s_w1")))
    s_w2 = Ctr(stack.enter_context(nc.semaphore("s_w2")))
    s_w3 = Ctr(stack.enter_context(nc.semaphore("s_w3")))
    s_w4 = Ctr(stack.enter_context(nc.semaphore("s_w4")))
    s_sm = Ctr(stack.enter_context(nc.semaphore("s_sm")))
    s_tp = Ctr(stack.enter_context(nc.semaphore("s_tp")))
    s_mm = Ctr(stack.enter_context(nc.semaphore("s_mm")))
    s_v = Ctr(stack.enter_context(nc.semaphore("s_v")))
    s_s = Ctr(stack.enter_context(nc.semaphore("s_s")))
    s_go = Ctr(stack.enter_context(nc.semaphore("s_go")))
    s_da = Ctr(stack.enter_context(nc.semaphore("s_da")))
    s_sp = Ctr(stack.enter_context(nc.semaphore("s_sp")))
    s_xc = [Ctr(stack.enter_context(nc.semaphore(f"s_xc{c}")))
            for c in range(8)]

    with stack:
        with nc.Block() as block:
            _emit(nc, block, locals())
    return nc


def _emit(nc, block, L):
    g = block.gpsimd
    t = block.tensor
    v = block.vector
    s = block.scalar
    y = block.sync

    (s_x, s_w1, s_w2, s_w3, s_w4, s_sm, s_tp, s_mm, s_v, s_s, s_go, s_da) = (
        L["s_x"], L["s_w1"], L["s_w2"], L["s_w3"], L["s_w4"], L["s_sm"],
        L["s_tp"], L["s_mm"], L["s_v"], L["s_s"], L["s_go"], L["s_da"])
    s_xc = L["s_xc"]
    s_sp = L["s_sp"]
    x_sb, gh_sb, V_sb, Wa, Wb = L["x_sb"], L["gh_sb"], L["V_sb"], L["Wa"], L["Wb"]
    lnx_roll, lnxT_win = L["lnx_roll"], L["lnxT_win"]
    ghT_win, gtT_win, gt_roll = L["ghT_win"], L["gtT_win"], L["gt_roll"]
    raw_roll, rawT_win, E_sb = L["raw_roll"], L["rawT_win"], L["E_sb"]
    b2_sb, t1T_sb = L["b2_sb"], L["t1T_sb"]
    bo_bf, bff_bf = L["bo_bf"], L["bff_bf"]
    clt_sb, cut_sb, idn_sb, mrhs_v = (L["clt_sb"], L["cut_sb"],
                                      L["idn_sb"], L["mrhs_v"])
    orow_bf, ocol_bf = L["orow_bf"], L["ocol_bf"]
    daT_views = L["daT_views"]
    lnst, nsq_sb, inv_sb = L["lnst"], L["nsq_sb"], L["inv_sb"]
    rs_sb, f2_sb = L["rs_sb"], L["f2_sb"]
    rq_bf, sir_bf, siv_sb, sr2_sb, zr_sb = (L["rq_bf"], L["sir_bf"],
                                            L["siv_sb"], L["sr2_sb"],
                                            L["zr_sb"])
    eps_sb = L["eps_sb"]
    b1_view, uaT_view, daT_dst = L["b1_view"], L["uaT_view"], L["daT_dst"]
    lu_view, ld_view, bg_view, bv_view = (L["lu_view"], L["ld_view"],
                                          L["bg_view"], L["bv_view"])
    PA, PB, PC, PD = L["PA"], L["PB"], L["PC"], L["PD"]
    x_hbm, out_hbm = L["x_hbm"], L["out_hbm"]
    pack_h, bias_h = L["pack_h"], L["bias_h"]
    pack_sb, bias_sb = L["pack_sb"], L["bias_sb"]
    sq_scr = gt_roll.ap()[:, 1, :]

    def bn_pass(ve, slot, src_lo, src_hi):
        ve.bn_stats(lnst.ap()[:, slot, 0:6], src_lo)
        ve.bn_stats(lnst.ap()[:, slot, 6:12], src_hi)
        ve.drain()
        return ve.bn_aggr(lnst.ap()[:, slot, 12:14], lnst.ap()[:, slot, 0:12])

    def rstd_pass(sc, slot):
        sc.activation(lnst.ap()[:, slot, 14:15], lnst.ap()[:, slot, 13:14],
                      AF.Ln, scale=1.0, bias=eps_sb.ap())
        sc.drain()
        return sc.activation(lnst.ap()[:, slot, 14:15],
                             lnst.ap()[:, slot, 14:15], AF.Exp, scale=-0.5)

    def apply_pass(ve, slot, dst, src):
        return ve.tensor_scalar(dst, src, lnst.ap()[:, slot, 12:13],
                                lnst.ap()[:, slot, 14:15],
                                ALU.subtract, ALU.mult)

    tok = {}

    def _g_loads(gp):
        for dst, src in ((pack_sb.ap(), pack_h), (bias_sb.ap(), bias_h)):
            gp.dma_start(dst, src,
                         max_dma_last_dim=8192).then_inc(s_sp.h, 16)
            s_sp.n += 16
        gp.dma_start(Wa.ap(), L["wgT_h"].rearrange("(k p) o -> p k o", p=128)
                     ).then_inc(s_w1.h, 16)
        s_w1.n += 16
        gp.wait_ge(s_tp.h, 16)
        gp.dma_start(Wb.ap(), L["wvT_h"].rearrange("(k p) o -> p k o", p=128)
                     ).then_inc(s_w2.h, 16)
        s_w2.n += 16
        xr = x_hbm.rearrange("(m p) f -> p m f", p=128)
        for c in (2, 3):
            gp.dma_start(x_sb.ap()[:, 2 * c:2 * c + 2, :],
                         xr[:, 2 * c:2 * c + 2, :]).then_inc(s_xc[c].h, 16)
            s_xc[c].n += 16
        for dst, src in ((uaT_view, L["uaT_h"]), (lu_view, L["lu_h"]),
                         (ld_view, L["ld_h"])):
            gp.dma_start(dst, src,
                         max_dma_last_dim=8192).then_inc(s_sm.h, 16)
            s_sm.n += 16
        for c in (4, 5, 6, 7):
            gp.dma_start(x_sb.ap()[:, 2 * c:2 * c + 2, :],
                         xr[:, 2 * c:2 * c + 2, :]).then_inc(s_xc[c].h, 16)
            s_xc[c].n += 16

    g(_g_loads)
    tok["sm"] = s_sm.n
    tok["sp"] = s_sp.n

    def _s_xload(sc):
        xr = x_hbm.rearrange("(m p) f -> p m f", p=128)
        for c in (0, 1):
            sc.dma_start(x_sb.ap()[:, 2 * c:2 * c + 2, :],
                         xr[:, 2 * c:2 * c + 2, :]).then_inc(s_xc[c].h, 16)
            s_xc[c].n += 16

    s(_s_xload)

    def _s_table(sc):
        sc.add_instruction(mybir.InstLoadActFuncSet(
            name=sc.bass.get_next_instruction_name(),
            act_func_set_id=ACT_SET_NL_EXP, ins=[], outs=[]))

    s(_s_table)

    def _v_ones(ve):
        ve.memset(fap(V_sb, 64, [[NT * H * 65, 128], [65 * H, NT], [65, H],
                                 [1, 1]]), 1.0)
        ve.memset(eps_sb.ap(), 1e-5).then_inc(s_v.h, 1)
        s_v.n += 1

    v(_v_ones)
    tok["vones"] = s_v.n

    tok_red1 = [0] * NT
    tok_ap1 = [0] * NT
    tok_ap1v = [0] * NT
    tok_tp1 = [0] * NT
    tok_gmm = [0] * NT
    tok_vmm = [0] * NT
    tok_gsq = [0] * NT
    tok_nsq = [0] * NT
    tok_inv = [0] * NT
    tok_ghev = [0] * NT
    tok_vev = [0] * NT
    tok_cmm = [0] * NT
    tok_dot = [0] * NT

    def kext(te, dst, bias_row, n):
        te.matmul(dst, orow_bf, bias_row[:, n * 512:(n + 1) * 512],
                  start=False, stop=True).then_inc(s_mm.h, 1)
        s_mm.n += 1

    def emit_ln1a(m):
        def _v_bn1(ve, m=m):
            _wait(ve, s_xc[m // 2], 16)
            bn_pass(ve, 2 + m % 2, x_sb.ap()[:, m, 0:512],
                    x_sb.ap()[:, m, 512:1024]).then_inc(s_v.h, 1)
            s_v.n += 1
            tok_red1[m] = s_v.n

        v(_v_bn1)

        def _s_rstd1(sc, m=m):
            _wait(sc, s_v, tok_red1[m])
            rstd_pass(sc, 2 + m % 2).then_inc(s_s.h, 1)
            s_s.n += 1
            tok_ap1[m] = s_s.n

        s(_s_rstd1)

    def emit_ln1b(m):
        def _v_ap1(ve, m=m):
            _wait(ve, s_s, tok_ap1[m])
            if m >= 2:
                _wait(ve, s_tp, tok_tp1[m - 2])
            apply_pass(ve, 2 + m % 2, lnx_roll.ap()[:, m % 2, :],
                       x_sb.ap()[:, m, :]).then_inc(s_v.h, 1)
            s_v.n += 1
            tok_ap1v[m] = s_v.n

        v(_v_ap1)

        def _y_tp1(sy, m=m):
            _wait(sy, s_v, tok_ap1v[m])
            if m >= 2:
                _wait(sy, s_mm, tok_vmm[m - 2])
            sy.dma_start_transpose(
                fap(lnxT_win, (m % 2) * 128,
                    [[NK * 256, 128], [256, NK], [1, 128]]),
                fap(lnx_roll, (m % 2) * D,
                    [[2 * D, 128], [128, NK], [1, 128]]),
            ).then_inc(s_tp.h, 16)
            s_tp.n += 16
            tok_tp1[m] = s_tp.n

        y(_y_tp1)

    emit_ln1a(0)
    emit_ln1b(0)

    for st in range(NT + 2):
        m2 = st + 1
        mp = st - 1
        qb = st - 2
        if 0 <= qb < NT:
            def _t_band(te, qb=qb):
                ps = PA if qb % 2 == 0 else PB
                _wait(te, s_s, tok_gsq[qb])
                _wait(te, s_v, tok_ghev[qb])
                for n in range(2):
                    dst = ps.ap()[:, n * 512:(n + 1) * 512]
                    first = True
                    if qb > 0:
                        te.matmul(dst, clt_sb,
                                  gh_sb.ap()[:, qb - 1, n * 512:(n + 1) * 512],
                                  start=True, stop=False)
                        first = False
                    te.matmul(dst, cut_sb,
                              gh_sb.ap()[:, qb, n * 512:(n + 1) * 512],
                              start=first, stop=True).then_inc(s_mm.h, 1)
                    s_mm.n += 1
                tok_cmm[qb] = s_mm.n

            t(_t_band)

        if 0 <= mp < NT:
            pg = PA if mp % 2 == 0 else PB
            pv = PC if mp % 2 == 0 else PD

            def _t_gv(te, mp=mp, pg=pg, pv=pv):
                _wait(te, s_tp, tok_tp1[mp])
                _wait(te, s_w1, 16)
                _wait(te, s_sp, tok["sp"])
                if mp >= 2:
                    _wait(te, s_v, tok_vev[mp - 2])
                for n in range(2):
                    dst = pg.ap()[:, n * 512:(n + 1) * 512]
                    for k in range(NK):
                        te.matmul(dst, lnxT_win.ap()[:, k, mp % 2, :],
                                  Wa.ap()[:, k, n * 512:(n + 1) * 512],
                                  start=(k == 0), stop=False)
                    kext(te, dst, bg_view, n)
                tok_gmm[mp] = s_mm.n
                _wait(te, s_w2, 16)
                for n in range(2):
                    dst = pv.ap()[:, n * 512:(n + 1) * 512]
                    for k in range(NK):
                        te.matmul(dst, lnxT_win.ap()[:, k, mp % 2, :],
                                  Wb.ap()[:, k, n * 512:(n + 1) * 512],
                                  start=(k == 0), stop=False)
                    kext(te, dst, bv_view, n)
                tok_vmm[mp] = s_mm.n

            t(_t_gv)

            def _s_gsq(sc, mp=mp, pg=pg):
                _wait(sc, s_mm, tok_gmm[mp])
                sc.activation(sq_scr, pg.ap(), AF.Square).then_inc(s_s.h, 1)
                s_s.n += 1
                tok_gsq[mp] = s_s.n

            s(_s_gsq)

        if m2 < NT:
            emit_ln1a(m2)

        if 0 <= mp < NT:
            def _v_nsq(ve, mp=mp):
                _wait(ve, s_s, tok_gsq[mp])
                if mp >= 2:
                    _wait(ve, s_s, tok_inv[mp - 2])
                ve.tensor_reduce(nsq_sb.ap()[:, mp % 2, :],
                                 gt_roll.ap()[:, 1, :].rearrange(
                                     "p (h d) -> p h d", h=16),
                                 AX.X, ALU.add).then_inc(s_v.h, 1)
                s_v.n += 1
                tok_nsq[mp] = s_v.n

            v(_v_nsq)

            def _s_inv(sc, mp=mp):
                _wait(sc, s_v, tok_nsq[mp])
                sc.activation(inv_sb.ap()[:, mp % 2, :],
                              nsq_sb.ap()[:, mp % 2, :], AF.Ln)
                sc.drain()
                sc.activation(inv_sb.ap()[:, mp % 2, :],
                              inv_sb.ap()[:, mp % 2, :], AF.Exp,
                              scale=-0.5).then_inc(s_s.h, 1)
                s_s.n += 1
                tok_inv[mp] = s_s.n

            s(_s_inv)

            def _v_ghev(ve, mp=mp, pg=pg):
                _wait(ve, s_s, tok_inv[mp])
                ve.tensor_tensor(
                    gh_sb.ap()[:, mp, :].rearrange("p (h d) -> p h d", h=16),
                    pg.ap().rearrange("p (h d) -> p h d", h=16),
                    fap(inv_sb, (mp % 2) * 16, [[32, 128], [1, 16], [0, 64]]),
                    ALU.mult).then_inc(s_v.h, 1)
                s_v.n += 1
                tok_ghev[mp] = s_v.n

            v(_v_ghev)

        if 0 <= qb < NT:
            def _v_dot(ve, dq=qb):
                ps = PA if dq % 2 == 0 else PB
                _wait(ve, s_mm, tok_cmm[dq])
                ve.tensor_tensor(raw_roll.ap()[:, 0, :], gh_sb.ap()[:, dq, :],
                                 ps.ap(), ALU.mult)
                ve.drain()
                ve.tensor_reduce(fap(rs_sb, dq, [[H * 16, 128], [16, H], [1, 1]]),
                                 raw_roll.ap()[:, 0, :].rearrange(
                                     "p (h d) -> p h d", h=16),
                                 AX.X, ALU.add).then_inc(s_v.h, 1)
                s_v.n += 1
                tok_dot[dq] = s_v.n

            v(_v_dot)

        if m2 < NT:
            emit_ln1b(m2)

        if 0 <= mp < NT:
            def _v_vev(ve, mp=mp, pv=pv):
                _wait(ve, s_mm, tok_vmm[mp])
                _wait(ve, s_v, tok["vones"])
                for n in range(2):
                    ve.tensor_copy(
                        fap(V_sb, mp * H * 65 + n * 8 * 65,
                            [[NT * H * 65, 128], [65, 8], [1, 64]]),
                        pv.ap()[:, n * 512:(n + 1) * 512]).then_inc(s_v.h, 1)
                    s_v.n += 1
                tok_vev[mp] = s_v.n

            v(_v_vev)

    def _g_w2(gp):
        _wait(gp, s_v, tok_vev[NT - 1])
        gp.dma_start(Wb.ap(), L["woT_h"].rearrange("(k p) o -> p k o", p=128)
                     ).then_inc(s_w3.h, 16)
        s_w3.n += 16
        gp.dma_start(Wa.ap(), L["wfT_h"].rearrange("(k p) o -> p k o", p=128)
                     ).then_inc(s_w4.h, 16)
        s_w4.n += 16

    g(_g_w2)

    def _t_l1(te):
        _wait(te, s_sm, tok["sm"])
        _wait(te, s_v, tok_vev[NT - 1])
        for n in range(2):
            te.matmul(PD.ap()[0:4, n * 512:(n + 1) * 512], lu_view,
                      uaT_view[:, n * 512:(n + 1) * 512], start=True,
                      stop=True).then_inc(s_mm.h, 1)
            s_mm.n += 1

    t(_t_l1)
    tok["b1"] = s_mm.n

    def _v_b1(ve):
        _wait(ve, s_mm, tok["b1"])
        ve.tensor_copy(b1_view, PD.ap()[0:4, :]).then_inc(s_v.h, 1)
        s_v.n += 1

    v(_v_b1)
    tok["b1e"] = s_v.n

    def _t_l2(te):
        _wait(te, s_v, tok["b1e"])
        for n in range(2):
            te.matmul(PD.ap()[0:100, n * 512:(n + 1) * 512], ld_view,
                      b1_view[:, n * 512:(n + 1) * 512], start=True,
                      stop=True).then_inc(s_mm.h, 1)
            s_mm.n += 1

    t(_t_l2)
    tok["b2"] = s_mm.n

    def _v_b2(ve):
        _wait(ve, s_mm, tok["b2"])
        ve.tensor_copy(b2_sb.ap(), PD.ap()[0:100, :]).then_inc(s_v.h, 1)
        s_v.n += 1

    v(_v_b2)
    tok["b2e"] = s_v.n


    def _v_fit1(ve):
        _wait(ve, s_v, tok_dot[NT - 1])
        ve.tensor_scalar(rs_sb.ap(), rs_sb.ap(), 1.0 / 4096.0, 1.0,
                         ALU.mult, ALU.add).then_inc(s_v.h, 1)
        s_v.n += 1

    v(_v_fit1)
    tok["fit1"] = s_v.n

    def _s_fit(sc):
        _wait(sc, s_v, tok["fit1"])
        sc.activation(rs_sb.ap(), rs_sb.ap(), AF.Ln)
        sc.drain()
        sc.activation(rs_sb.ap(), rs_sb.ap(), AF.Exp,
                      scale=-1.0).then_inc(s_s.h, 1)
        s_s.n += 1

    s(_s_fit)
    tok["fit2"] = s_s.n

    def _v_fit2(ve):
        _wait(ve, s_s, tok["fit2"])
        ve.tensor_reduce(siv_sb.ap(), rs_sb.ap(), AX.X,
                         ALU.add).then_inc(s_v.h, 1)
        s_v.n += 1

    v(_v_fit2)
    tok["siv"] = s_v.n

    def _s_rq(sc):
        _wait(sc, s_v, tok["siv"])
        sc.activation(rq_bf.ap(), siv_sb.ap(),
                      AF.Copy).then_inc(s_s.h, 1)
        s_s.n += 1

    s(_s_rq)
    tok["rq"] = s_s.n

    def _t_si(te):
        _wait(te, s_s, tok["rq"])
        _wait(te, s_v, tok_vev[14])
        te.matmul(PC.ap()[0:1, 0:16], ocol_bf, rq_bf.ap(), start=True,
                  stop=True).then_inc(s_mm.h, 1)
        s_mm.n += 1

    t(_t_si)
    tok["si"] = s_mm.n

    def _v_si(ve):
        _wait(ve, s_mm, tok["si"])
        ve.tensor_copy(sir_bf.ap(), PC.ap()[0:1, 0:16]).then_inc(s_v.h, 1)
        s_v.n += 1

    v(_v_si)
    tok["sir"] = s_v.n

    def _t_sib(te):
        _wait(te, s_v, tok["sir"])
        te.matmul(PC.ap()[:, 512:528], orow_bf, sir_bf.ap(), start=True,
                  stop=True).then_inc(s_mm.h, 1)
        s_mm.n += 1

    t(_t_sib)
    tok["sib"] = s_mm.n

    def _v_fcp(ve):
        _wait(ve, s_mm, tok["sib"])
        ve.tensor_copy(siv_sb.ap(), PC.ap()[:, 512:528]).then_inc(s_v.h, 1)
        s_v.n += 1

    v(_v_fcp)
    tok["fcp"] = s_v.n

    def _s_sr2(sc):
        _wait(sc, s_v, tok["fcp"])
        sc.activation(sr2_sb.ap(), siv_sb.ap(), AF.Ln, scale=2.0)
        sc.drain()
        sc.activation(sr2_sb.ap(), sr2_sb.ap(), AF.Exp,
                      scale=-1.0).then_inc(s_s.h, 1)
        s_s.n += 1

    s(_s_sr2)
    tok["sr2"] = s_s.n

    def _v_f2(ve):
        _wait(ve, s_s, tok["sr2"])
        ve.tensor_tensor(f2_sb.ap(), rs_sb.ap(),
                         fap(sr2_sb, 0, [[16, 128], [1, 16], [0, 16]]),
                         ALU.mult).then_inc(s_v.h, 1)
        s_v.n += 1

    v(_v_f2)
    tok["f2"] = s_v.n

    tok_ghT = [0] * NT
    tok_gt = [0] * NT
    tok_gtT = [0] * NT
    tok_smm = [[0] * 4 for _ in range(NT)]
    tok_exp = [[0] * 4 for _ in range(NT)]
    tok_av = [[0] * 4 for _ in range(NT)]
    tok_nrm = [[0] * 4 for _ in range(NT)]
    tok_rawT = [0] * NT
    tok_omm = [0] * NT
    tok_red2 = [0] * NT
    tok_ap2 = [0] * NT
    tok_ap2v = [0] * NT
    tok_ln2 = [0] * NT

    TP_DIMS = [[NK * 256, 128], [256, NK], [1, 128]]

    def ghT_slot(qb):
        s = qb % 3
        if s < 2:
            return fap(ghT_win, s * 128, TP_DIMS)
        return fap(lnxT_win, 0, TP_DIMS)

    def gtT_slot(qb):
        s = qb % 3
        if s < 2:
            return fap(gtT_win, s * 128, TP_DIMS)
        return fap(lnxT_win, 128, TP_DIMS)

    def emit_gt_and_T(qb):
        def _v_gt(ve, qb=qb):
            _wait(ve, s_v, tok["f2"])
            if qb >= 2:
                _wait(ve, s_tp, tok_gtT[qb - 2])
            ve.tensor_tensor(
                gt_roll.ap()[:, qb % 2, :].rearrange("p (h d) -> p h d", h=16),
                gh_sb.ap()[:, qb, :].rearrange("p (h d) -> p h d", h=16),
                fap(f2_sb, qb, [[H * 16, 128], [16, H], [0, 64]]),
                ALU.mult).then_inc(s_v.h, 1)
            s_v.n += 1
            tok_gt[qb] = s_v.n

        v(_v_gt)

        def _y_T(sy, qb=qb):
            if qb % 3 == 2:
                _wait(sy, s_mm, tok_vmm[NT - 1])
            if qb >= 3:
                _wait(sy, s_mm, tok_smm[qb - 2][3])
            sy.dma_start_transpose(
                ghT_slot(qb),
                fap(gh_sb, qb * D, [[NT * D, 128], [128, NK], [1, 128]]),
            ).then_inc(s_tp.h, 16)
            s_tp.n += 16
            tok_ghT[qb] = s_tp.n
            _wait(sy, s_v, tok_gt[qb])
            sy.dma_start_transpose(
                gtT_slot(qb),
                fap(gt_roll, (qb % 2) * D,
                    [[2 * D, 128], [128, NK], [1, 128]]),
            ).then_inc(s_tp.h, 16)
            s_tp.n += 16
            tok_gtT[qb] = s_tp.n

        y(_y_T)

    def emit_sc(qb, grp):
        def _t_sc(te, qb=qb, grp=grp):
            ps = PA if grp % 2 == 0 else PB
            _wait(te, s_tp, tok_ghT[qb])
            _wait(te, s_tp, tok_gtT[qb])
            if qb > 0:
                _wait(te, s_tp, tok_ghT[qb - 1])
            if grp >= 2:
                _wait(te, s_s, tok_exp[qb][grp - 2])
            elif qb > 0:
                _wait(te, s_s, tok_exp[qb - 1][grp + 2])
            else:
                _wait(te, s_v, tok_dot[NT - 1])
            hlist = [grp * 4, grp * 4 + 2, grp * 4 + 1, grp * 4 + 3]
            ghq = ghT_slot(qb)
            gtq = gtT_slot(qb)
            ghp = ghT_slot(qb - 1) if qb > 0 else None
            mms = []
            for i, h in enumerate(hlist):
                p0 = (h % 2) * 64
                lhs_same = ghq[p0:p0 + 64, h // 2, :]
                rhs_q = gtq[p0:p0 + 64, h // 2, :]
                if qb > 0:
                    mms.append((ps.ap()[:, i * 256:i * 256 + 128],
                                ghp[p0:p0 + 64, h // 2, :], rhs_q))
                mms.append((ps.ap()[:, i * 256 + 128:i * 256 + 256],
                            lhs_same, rhs_q))
            if qb > 0:
                for half in range(2):
                    mms.append((ps.ap()[:, half * 512:(half + 1) * 512],
                                idn_sb,
                                fap(pack_sb, PK_MR,
                                    [[PACK_N, 128], [0, 2], [1, 256]])))
            else:
                for i in range(4):
                    mms.append((ps.ap()[:, i * 256 + 128:i * 256 + 256],
                                idn_sb, mrhs_v[:, 128:256]))
            first = {0: True, 1: True}
            last = {}
            for j, (dst, _, _) in enumerate(mms):
                last[dst.offset // 512] = j
            for j, (dst, lhsT, rhs) in enumerate(mms):
                bank = dst.offset // 512
                bi = te.matmul(dst, lhsT, rhs, start=first[bank],
                               stop=(last[bank] == j))
                first[bank] = False
                if last[bank] == j:
                    bi.then_inc(s_mm.h, 1)
                    s_mm.n += 1
            tok_smm[qb][grp] = s_mm.n

        t(_t_sc)

    def emit_exp(qb, grp):
        def _s_exp(sc, qb=qb, grp=grp):
            ps = PA if grp % 2 == 0 else PB
            _wait(sc, s_mm, tok_smm[qb][grp])
            if qb == 0:
                sc.activation(
                    fap(E_sb, grp * 4 * 256 + 128,
                        [[H * 256, 128], [256, 4], [1, 128]]),
                    fap(ps, 128, [[1024, 128], [256, 4], [1, 128]]),
                    AF.Exp).then_inc(s_s.h, 1)
            else:
                sc.activation(
                    fap(E_sb, grp * 4 * 256, [[H * 256, 128], [1, 1024]]),
                    ps.ap(), AF.Exp).then_inc(s_s.h, 1)
            s_s.n += 1
            tok_exp[qb][grp] = s_s.n

        s(_s_exp)

    def emit_av(qb, grp):
        def _t_av(te, qb=qb, grp=grp):
            half = (grp % 2) * 512
            _wait(te, s_s, tok_exp[qb][grp])
            _wait(te, s_v, tok_vev[NT - 1])
            if grp >= 2:
                _wait(te, s_v, tok_nrm[qb][grp - 2])
            elif qb > 0:
                _wait(te, s_v, tok_nrm[qb - 1][grp + 2])
            elif grp == 0:
                _wait(te, s_v, tok["sir"])
            else:
                _wait(te, s_v, tok["f2"])
            n_mm = 4 * (2 if qb > 0 else 1)
            hlist = [grp * 4, grp * 4 + 2, grp * 4 + 1, grp * 4 + 3]
            j = 0
            for i, h in enumerate(hlist):
                dst = fap(PC, half + i * 65, [[1024, 128], [1, 65]])
                epos = grp * 4 + i
                if qb > 0:
                    te.matmul(dst, E_sb.ap()[:, epos, 0, :],
                              fap(V_sb, (qb - 1) * H * 65 + h * 65,
                                  [[NT * H * 65, 128], [1, 65]]),
                              start=(j == 0), stop=False)
                    j += 1
                bi = te.matmul(dst, E_sb.ap()[:, epos, 1, :],
                               fap(V_sb, qb * H * 65 + h * 65,
                                   [[NT * H * 65, 128], [1, 65]]),
                               start=(j == 0), stop=(j == n_mm - 1))
                j += 1
                if j == n_mm:
                    bi.then_inc(s_mm.h, 1)
                    s_mm.n += 1
            tok_av[qb][grp] = s_mm.n

        t(_t_av)

    tok_zr = [[0] * 4 for _ in range(NT)]

    def emit_zr(qb, pair):
        def _s_zr(sc, qb=qb, pair=pair):
            _wait(sc, s_mm, tok_av[qb][2 * pair + 1])
            if qb > 0:
                _wait(sc, s_v, tok_nrm[qb - 1][3])
            z = fap(lnst, (2 + pair) * 15, [[90, 128], [1, 8]])
            sc.activation(z, fap(PC, 64, [[1024, 128], [512, 2], [65, 4],
                                          [1, 1]]), AF.Ln)
            sc.drain()
            sc.activation(z, z, AF.Exp, scale=-1.0).then_inc(s_s.h, 1)
            s_s.n += 1
            tok_zr[qb][pair] = s_s.n

        s(_s_zr)

    def emit_nrm(qb, grp):
        def _v_nrm(ve, qb=qb, grp=grp):
            half = (grp % 2) * 512
            _wait(ve, s_s, tok_zr[qb][grp // 2])
            if qb >= 2 and grp == 0:
                _wait(ve, s_tp, tok_rawT[qb - 2])
            ve.tensor_tensor(
                fap(raw_roll, (qb % 2) * D + grp * 256,
                    [[2 * D, 128], [64, 2], [128, 2], [1, 64]]),
                fap(PC, half, [[1024, 128], [130, 2], [65, 2], [1, 64]]),
                fap(lnst, (2 + grp // 2) * 15 + (grp % 2) * 4,
                    [[90, 128], [2, 2], [1, 2], [0, 64]]),
                ALU.mult).then_inc(s_v.h, 1)
            s_v.n += 1
            tok_nrm[qb][grp] = s_v.n

        v(_v_nrm)

    def emit_rawT(qb):
        def _s_rawT(sc, qb=qb):
            _wait(sc, s_v, tok_nrm[qb][3])
            if qb >= 1:
                _wait(sc, s_mm, tok_omm[qb - 1])
            sc.dma_start_transpose(
                fap(rawT_win, 0, [[NK * 128, 128], [128, NK], [1, 128]]),
                fap(raw_roll, (qb % 2) * D, [[2 * D, 128], [128, NK], [1, 128]]),
            ).then_inc(s_tp.h, 16)
            s_tp.n += 16
            tok_rawT[qb] = s_tp.n

        s(_s_rawT)

    def emit_o(qb):
        def _t_o(te, qb=qb):
            _wait(te, s_tp, tok_rawT[qb])
            _wait(te, s_w3, 16)
            if qb == 0:
                _wait(te, s_v, tok["b2e"])
                _wait(te, s_v, tok_vev[15])
            else:
                _wait(te, s_v, tok_ap2v[qb - 1])
            for n in range(2):
                dst = PD.ap()[:, n * 512:(n + 1) * 512]
                for k in range(NK):
                    te.matmul(dst, rawT_win.ap()[:, k, 0, :],
                              Wb.ap()[:, k, n * 512:(n + 1) * 512],
                              start=(k == 0), stop=False)
                kext(te, dst, bo_bf, n)
            tok_omm[qb] = s_mm.n

        t(_t_o)

    def emit_ln2a(qb):
        def _v_bn2(ve, qb=qb):
            _wait(ve, s_mm, tok_omm[qb])
            bn_pass(ve, 4 + qb % 2, PD.ap()[:, 0:512],
                    PD.ap()[:, 512:1024]).then_inc(s_v.h, 1)
            s_v.n += 1
            tok_red2[qb] = s_v.n

        v(_v_bn2)

    def emit_ln2b(qb):
        def _s_r2(sc, qb=qb):
            _wait(sc, s_v, tok_red2[qb])
            rstd_pass(sc, 4 + qb % 2).then_inc(s_s.h, 1)
            s_s.n += 1
            tok_ap2[qb] = s_s.n

        s(_s_r2)

        def _v_ap2(ve, qb=qb):
            _wait(ve, s_s, tok_ap2[qb])
            apply_pass(ve, 4 + qb % 2, lnx_roll.ap()[:, qb % 2, :],
                       PD.ap()).then_inc(s_v.h, 1)
            s_v.n += 1
            tok_ap2v[qb] = s_v.n

        v(_v_ap2)

        def _v_add2(ve, qb=qb):
            ve.drain()
            ve.tensor_tensor(x_sb.ap()[:, qb, :], x_sb.ap()[:, qb, :],
                             lnx_roll.ap()[:, qb % 2, :],
                             ALU.add).then_inc(s_v.h, 1)
            s_v.n += 1
            tok_ln2[qb] = s_v.n

        v(_v_add2)

    tok_red3 = [0] * NT
    tok_ap3 = [0] * NT
    tok_ap3v = [0] * NT
    tok_tp2 = [0] * NT
    tok_nm3 = [0] * NT

    def emit_ln3a(m):
        def _v_bn3(ve, m=m):
            _wait(ve, s_v, tok_ln2[m])
            if m >= 2:
                _wait(ve, s_s, tok_ap3v[m - 2])
            bn_pass(ve, m % 2, x_sb.ap()[:, m, 0:512],
                    x_sb.ap()[:, m, 512:1024]).then_inc(s_v.h, 1)
            s_v.n += 1
            tok_red3[m] = s_v.n

        v(_v_bn3)

        def _s_rstd3(sc, m=m):
            _wait(sc, s_v, tok_red3[m])
            rstd_pass(sc, m % 2).then_inc(s_s.h, 1)
            s_s.n += 1
            tok_ap3[m] = s_s.n

        s(_s_rstd3)

        def _v_nm3(ve, m=m):
            _wait(ve, s_s, tok_ap3[m])
            ve.tensor_scalar(lnst.ap()[:, m % 2, 10:11],
                             lnst.ap()[:, m % 2, 12:13],
                             lnst.ap()[:, m % 2, 14:15], -1.0,
                             ALU.mult, ALU.mult).then_inc(s_v.h, 1)
            s_v.n += 1
            tok_nm3[m] = s_v.n

        v(_v_nm3)

    emit_gt_and_T(0)
    emit_gt_and_T(1)
    emit_gt_and_T(2)
    for qb in range(NT):
        emit_sc(qb, 0)
        emit_exp(qb, 0)
        emit_sc(qb, 1)
        emit_exp(qb, 1)
        if qb >= 2:
            emit_ln2b(qb - 2)
        emit_av(qb, 0)
        emit_sc(qb, 2)
        emit_exp(qb, 2)
        emit_av(qb, 1)
        emit_sc(qb, 3)
        emit_exp(qb, 3)
        emit_zr(qb, 0)
        emit_nrm(qb, 0)
        emit_nrm(qb, 1)
        if qb >= 1:
            emit_o(qb - 1)
        emit_av(qb, 2)
        emit_av(qb, 3)
        emit_zr(qb, 1)
        emit_nrm(qb, 2)
        emit_nrm(qb, 3)
        if qb >= 1:
            emit_ln2a(qb - 1)
        if 3 <= qb + 2 < NT:
            emit_gt_and_T(qb + 2)
        emit_rawT(qb)
        if qb == NT - 2:
            emit_ln3a(0)
            emit_ln3a(1)
    emit_ln2b(NT - 2)
    emit_o(NT - 1)
    emit_ln2a(NT - 1)
    emit_ln2b(NT - 1)

    def _g_daT(gp):
        _wait(gp, s_mm, tok_av[NT - 1][0])
        gp.dma_start(daT_dst, L["daT_h"]).then_inc(s_da.h, 16)
        s_da.n += 16

    g(_g_daT)

    tok_t1 = [0] * NT
    tok_t1e = [0] * NT
    tok_ymm = [0] * NT
    tok_red4 = [0] * NT
    tok_ap4 = [0] * NT
    tok_x2 = [0] * NT
    tok_ap4v = [0] * NT
    tok_nm4 = [0] * NT

    def emit_ln3b(m):
            def _s_ap3(sc, m=m):
                _wait(sc, s_v, tok_nm3[m])
                if m >= 2:
                    _wait(sc, s_tp, tok_tp2[m - 2])
                else:
                    _wait(sc, s_v, tok_ln2[14 + m])
                sc.activation(lnx_roll.ap()[:, m % 2, :], x_sb.ap()[:, m, :],
                              AF.Identity,
                              bias=lnst.ap()[:, m % 2, 10:11],
                              scale=lnst.ap()[:, m % 2, 14:15]
                              ).then_inc(s_s.h, 1)
                s_s.n += 1
                tok_ap3v[m] = s_s.n

            s(_s_ap3)

            def _y_tp2(sy, m=m):
                _wait(sy, s_s, tok_ap3v[m])
                if m < 2:
                    _wait(sy, s_mm, tok_smm[NT - 1][3])
                if m >= 2:
                    _wait(sy, s_mm, tok_ymm[m - 2])
                sy.dma_start_transpose(
                    fap(lnxT_win, (m % 2) * 128,
                        [[NK * 256, 128], [256, NK], [1, 128]]),
                    fap(lnx_roll, (m % 2) * D,
                        [[2 * D, 128], [128, NK], [1, 128]]),
                ).then_inc(s_tp.h, 16)
                s_tp.n += 16
                tok_tp2[m] = s_tp.n

            y(_y_tp2)

    emit_ln3b(0)

    for st in range(NT + 2):
        m3 = st + 1
        mp = st - 1
        q2 = st - 2
        if 0 <= mp < NT:
            py = PA if mp % 2 == 0 else PB

            def _t_t1(te, mp=mp):
                _wait(te, s_tp, tok_tp2[mp])
                _wait(te, s_da, 16)
                if mp == 0:
                    _wait(te, s_v, tok_nrm[NT - 1][2])
                else:
                    _wait(te, s_v, tok_t1e[mp - 1])
                for k in range(NK):
                    bi = te.matmul(PC.ap()[0:100, 0:128], daT_views[k],
                                   lnxT_win.ap()[:, k, mp % 2, :],
                                   start=(k == 0), stop=(k == NK - 1))
                bi.then_inc(s_mm.h, 1)
                s_mm.n += 1
                tok_t1[mp] = s_mm.n

            t(_t_t1)

            def _v_t1e(ve, mp=mp):
                _wait(ve, s_mm, tok_t1[mp])
                ve.tensor_copy(t1T_sb.ap(),
                               PC.ap()[0:100, 0:128]).then_inc(s_v.h, 1)
                s_v.n += 1
                tok_t1e[mp] = s_v.n

            v(_v_t1e)

            def _t_y(te, mp=mp, py=py):
                _wait(te, s_w4, 16)
                if mp < 2:
                    _wait(te, s_s, tok_exp[NT - 1][2 + mp])
                if mp >= 2:
                    _wait(te, s_s, tok_ap4v[mp - 2])
                for n in range(2):
                    dst = py.ap()[:, n * 512:(n + 1) * 512]
                    for k in range(NK):
                        te.matmul(dst, lnxT_win.ap()[:, k, mp % 2, :],
                                  Wa.ap()[:, k, n * 512:(n + 1) * 512],
                                  start=(k == 0), stop=False)
                    if n == 0:
                        _wait(te, s_v, tok_t1e[mp])
                    te.matmul(dst, t1T_sb.ap(),
                              b2_sb.ap()[:, n * 512:(n + 1) * 512],
                              start=False, stop=False)
                    kext(te, dst, bff_bf, n)
                tok_ymm[mp] = s_mm.n

            t(_t_y)

        if m3 < NT:
            if m3 >= 2:
                emit_ln3a(m3)
            emit_ln3b(m3)

        if 0 <= q2 < NT:
            pq = PA if q2 % 2 == 0 else PB

            def _v_bn4(ve, q2=q2, pq=pq):
                _wait(ve, s_mm, tok_ymm[q2])
                if q2 < 2:
                    _wait(ve, s_s, tok_zr[NT - 1][1])
                else:
                    _wait(ve, s_s, tok_ap4v[q2 - 2])
                bn_pass(ve, 2 + q2 % 2, pq.ap()[:, 0:512],
                        pq.ap()[:, 512:1024]).then_inc(s_v.h, 1)
                s_v.n += 1
                tok_red4[q2] = s_v.n

            v(_v_bn4)

            def _s_rstd4(sc, q2=q2):
                _wait(sc, s_v, tok_red4[q2])
                rstd_pass(sc, 2 + q2 % 2).then_inc(s_s.h, 1)
                s_s.n += 1
                tok_ap4[q2] = s_s.n

            s(_s_rstd4)

            def _v_nm4(ve, q2=q2):
                _wait(ve, s_s, tok_ap4[q2])
                sl = 2 + q2 % 2
                ve.tensor_scalar(lnst.ap()[:, sl, 10:11],
                                 lnst.ap()[:, sl, 12:13],
                                 lnst.ap()[:, sl, 14:15], -1.0,
                                 ALU.mult, ALU.mult).then_inc(s_v.h, 1)
                s_v.n += 1
                tok_nm4[q2] = s_v.n

            v(_v_nm4)

            def _s_ap4(sc, q2=q2, pq=pq):
                _wait(sc, s_v, tok_nm4[q2])
                sl = 2 + q2 % 2
                sc.activation(raw_roll.ap()[:, q2 % 2, :], pq.ap(),
                              AF.Identity,
                              bias=lnst.ap()[:, sl, 10:11],
                              scale=lnst.ap()[:, sl, 14:15]
                              ).then_inc(s_s.h, 1)
                s_s.n += 1
                tok_ap4v[q2] = s_s.n

            s(_s_ap4)

            def _v_x2(ve, q2=q2):
                _wait(ve, s_s, tok_ap4v[q2])
                ve.tensor_tensor(x_sb.ap()[:, q2, :], x_sb.ap()[:, q2, :],
                                 raw_roll.ap()[:, q2 % 2, :],
                                 ALU.add).then_inc(s_v.h, 1)
                s_v.n += 1
                tok_x2[q2] = s_v.n

            v(_v_x2)

            def _st(eng, q2=q2):
                _wait(eng, s_v, tok_x2[q2])
                eng.dma_start(
                    out_hbm.rearrange("(m p) f -> p m f", p=128)[:, q2, :],
                    x_sb.ap()[:, q2, :]).then_inc(s_go.h, 16)
                s_go.n += 16

            (g if q2 % 2 == 0 else y)(_st)

    def _g_final(gp):
        gp.wait_ge(s_go.h, 16 * NT)

    g(_g_final)


_BUILT = None


def _get_nc():
    global _BUILT
    if _BUILT is None:
        _BUILT = build_nc()
    return _BUILT


def kernel(L0_attn_x, L0_attn_Wg, L0_attn_bg, L0_attn_Wv, L0_attn_bv,
           L0_attn_Wo, L0_attn_bo, L0_ffn_W, L0_ffn_b, L0_ffn_lora_down,
           L0_ffn_lora_up, L0_ffn_down_aux, L0_ffn_up_aux):
    x = np.ascontiguousarray(np.asarray(L0_attn_x, dtype=np.float32))
    B = x.shape[0]
    f32 = lambda a: np.ascontiguousarray(np.asarray(a, dtype=np.float32))
    k = np.arange(128)[:, None]
    i = np.arange(128)[None, :]
    mpr = np.where(k >= i, 0.0, NEG).astype(np.float32)
    msa = np.where(k <= i, 0.0, NEG).astype(np.float32)
    pack = np.zeros((128, PACK_N), np.float32)
    pack[:, PK_CLT:PK_CLT + 128] = (k >= i).astype(np.float32)
    pack[:, PK_CUT:PK_CUT + 128] = (k <= i).astype(np.float32)
    pack[:, PK_IDN:PK_IDN + 128] = np.eye(128, dtype=np.float32)
    pack[:, PK_MR:PK_MR + 256] = np.concatenate([mpr, msa], axis=1)
    daT = f32(np.asarray(L0_ffn_down_aux).T)
    daT_packed = np.ascontiguousarray(
        daT.reshape(8, 128, 100).transpose(1, 0, 2).reshape(128, 800))
    biases = np.concatenate(
        [f32(L0_attn_bg), f32(L0_attn_bv), f32(L0_attn_bo),
         f32(L0_ffn_b)]).reshape(1, 4 * D)
    shared = {
        "wgT": f32(np.asarray(L0_attn_Wg).T),
        "wvT": f32(np.asarray(L0_attn_Wv).T),
        "woT": f32(np.asarray(L0_attn_Wo).T),
        "wfT": f32(np.asarray(L0_ffn_W).T),
        "pack": np.ascontiguousarray(pack),
        "biases": np.ascontiguousarray(biases),
        "lu": f32(L0_ffn_lora_up), "ld": f32(L0_ffn_lora_down),
        "uaT": f32(np.asarray(L0_ffn_up_aux).T),
        "daT": daT_packed,
    }
    in_maps = []
    for c in range(B):
        m = {"x": np.ascontiguousarray(x[c])}
        m.update(shared)
        in_maps.append(m)
    nc = _get_nc()
    res = run_bass_kernel_spmd(nc, in_maps, core_ids=list(range(B)))
    out = np.stack([np.asarray(r["out"]) for r in res.results], axis=0)
    return out.astype(np.float32)


# version 30
# speedup vs baseline: 1.0538x; 1.0087x over previous
import numpy as np
import ml_dtypes
import concourse.bass as bass
import concourse.mybir as mybir
from concourse.bass_utils import run_bass_kernel_spmd

F32 = mybir.dt.float32
BF16 = mybir.dt.bfloat16
AF = mybir.ActivationFunctionType
ALU = mybir.AluOpType
AX = mybir.AxisListType

T = 2048
D = 1024
H = 16
HD = 64
NT = 16
NK = 8
NEG = -1.0e10
ACT_SET_NL_EXP = 6


class Ctr:
    def __init__(self, h):
        self.h = h
        self.n = 0


_HW = {}


def _wait(eng, ctr, val):
    assert val <= ctr.n, f"wait on future token {val} > {ctr.n}"
    if val <= 0:
        return
    key = (id(eng), id(ctr))
    if _HW.get(key, 0) >= val:
        return
    _HW[key] = val
    eng.wait_ge(ctr.h, val)


def fap(t, offset, dims):
    return bass.AP(t, offset, dims)


PK_CLT = 0
PK_CUT = 128
PK_IDN = 256
PK_MR = 384
PACK_N = 640


def build_nc():
    _HW.clear()
    nc = bass.Bass("TRN2", target_bir_lowering=False,
                   detect_race_conditions=False)

    x_hbm = nc.dram_tensor("x", [T, D], F32, kind="ExternalInput").ap()
    out_hbm = nc.dram_tensor("out", [T, D], F32, kind="ExternalOutput").ap()
    wgT_h = nc.dram_tensor("wgT", [D, D], F32, kind="ExternalInput").ap()
    wvT_h = nc.dram_tensor("wvT", [D, D], F32, kind="ExternalInput").ap()
    woT_h = nc.dram_tensor("woT", [D, D], F32, kind="ExternalInput").ap()
    wfT_h = nc.dram_tensor("wfT", [D, D], F32, kind="ExternalInput").ap()
    pack_h = nc.dram_tensor("pack", [128, PACK_N], F32,
                            kind="ExternalInput").ap()
    bias_h = nc.dram_tensor("biases", [1, 4 * D], F32,
                            kind="ExternalInput").ap()
    lu_h = nc.dram_tensor("lu", [50, 4], F32, kind="ExternalInput").ap()
    ld_h = nc.dram_tensor("ld", [4, 100], F32, kind="ExternalInput").ap()
    uaT_h = nc.dram_tensor("uaT", [50, D], F32, kind="ExternalInput").ap()
    daT_h = nc.dram_tensor("daT", [128, 800], F32, kind="ExternalInput").ap()

    A = nc.alloc_sbuf_tensor
    x_sb = A("x_sb", [128, NT, D], F32)
    gh_sb = A("gh_sb", [128, NT, D], BF16)
    V_sb = A("V_sb", [128, NT, H * 65], BF16)
    Wa = A("Wa", [128, NK, D], BF16)
    Wb = A("Wb", [128, NK, D], BF16)
    lnx_roll = A("lnx_roll", [128, 2, D], BF16)
    lnxT_win = A("lnxT_win", [128, NK, 2, 128], BF16, is_transpose_dest=True)
    ghT_win = A("ghT_win", [128, NK, 2, 128], BF16, is_transpose_dest=True)
    gtT_win = A("gtT_win", [128, NK, 2, 128], BF16, is_transpose_dest=True)
    gt_roll = A("gt_roll", [128, 2, D], BF16)
    raw_roll = A("raw_roll", [128, 2, D], BF16)
    rawT_win = A("rawT_win", [128, NK, 1, 128], BF16, is_transpose_dest=True)
    E_sb = A("E_sb", [128, H, 2, 128], BF16)
    b2_sb = A("b2_sb", [100, D], BF16)
    t1T_sb = A("t1T_sb", [100, 128], BF16)
    pack_sb = A("pack_sb", [128, PACK_N], BF16)
    bias_sb = A("bias_sb", [1, 4 * D], BF16)
    lnst = A("lnst", [128, 6, 15], F32)
    nsq_sb = A("nsq_sb", [128, 2, 16], F32)
    inv_sb = A("inv_sb", [128, 2, 16], F32)
    rs_sb = A("rs_sb", [128, H, 16], F32)
    f2_sb = rs_sb
    rq_bf = A("rq_bf", [128, 16], BF16)
    sir_bf = A("sir_bf", [1, 16], BF16)
    siv_sb = A("siv_sb", [128, 16], F32)
    sr2_sb = A("sr2_sb", [128, 16], F32)
    eps_sb = A("eps_sb", [128, 1], F32)
    zr_sb = lnst.ap()[:, 2, 0:8]
    pk = pack_sb.ap()
    clt_sb = pk[:, PK_CLT:PK_CLT + 128]
    cut_sb = pk[:, PK_CUT:PK_CUT + 128]
    idn_sb = pk[:, PK_IDN:PK_IDN + 128]
    mrhs_v = pk[:, PK_MR:PK_MR + 256]
    orow_bf = pk[0:1, PK_CUT:PK_CUT + 128]
    ocol_bf = pk[:, PK_CLT:PK_CLT + 1]
    b1_view = raw_roll.ap()[0:4, 0, :]
    uaT_view = rawT_win.ap().rearrange("p a b c -> p (a b c)")[0:50, 0:D]
    lu_view = gt_roll.ap()[0:50, 0, 0:4]
    ld_view = gt_roll.ap()[0:4, 0, 16:116]
    bb = bias_sb.ap()
    bg_view = bb[0:1, 0:D]
    bv_view = bb[0:1, D:2 * D]
    bo_bf = bb[0:1, 2 * D:3 * D]
    bff_bf = bb[0:1, 3 * D:4 * D]
    daT_views = [fap(E_sb, 100 * k, [[H * 2 * 128, 128], [1, 100]])
                 for k in range(NK)]
    daT_dst = fap(E_sb, 0, [[H * 2 * 128, 128], [1, 800]])
    assert nc.sbuf_bytes_remaining >= 0, nc.sbuf_bytes_remaining

    PA = nc.alloc_psum_tensor("PA", [128, 1024], F32)
    PB = nc.alloc_psum_tensor("PB", [128, 1024], F32)
    PC = nc.alloc_psum_tensor("PC", [128, 1024], F32)
    PD = nc.alloc_psum_tensor("PD", [128, 1024], F32)
    assert nc.psum_banks_remaining == 0

    import contextlib
    stack = contextlib.ExitStack()
    s_x = Ctr(stack.enter_context(nc.semaphore("s_x")))
    s_w1 = Ctr(stack.enter_context(nc.semaphore("s_w1")))
    s_w2 = Ctr(stack.enter_context(nc.semaphore("s_w2")))
    s_w3 = Ctr(stack.enter_context(nc.semaphore("s_w3")))
    s_w4 = Ctr(stack.enter_context(nc.semaphore("s_w4")))
    s_sm = Ctr(stack.enter_context(nc.semaphore("s_sm")))
    s_tp = Ctr(stack.enter_context(nc.semaphore("s_tp")))
    s_mm = Ctr(stack.enter_context(nc.semaphore("s_mm")))
    s_v = Ctr(stack.enter_context(nc.semaphore("s_v")))
    s_s = Ctr(stack.enter_context(nc.semaphore("s_s")))
    s_go = Ctr(stack.enter_context(nc.semaphore("s_go")))
    s_da = Ctr(stack.enter_context(nc.semaphore("s_da")))
    s_sp = Ctr(stack.enter_context(nc.semaphore("s_sp")))
    s_xc = [Ctr(stack.enter_context(nc.semaphore(f"s_xc{c}")))
            for c in range(8)]

    with stack:
        with nc.Block() as block:
            _emit(nc, block, locals())
    return nc


def _emit(nc, block, L):
    g = block.gpsimd
    t = block.tensor
    v = block.vector
    s = block.scalar
    y = block.sync

    (s_x, s_w1, s_w2, s_w3, s_w4, s_sm, s_tp, s_mm, s_v, s_s, s_go, s_da) = (
        L["s_x"], L["s_w1"], L["s_w2"], L["s_w3"], L["s_w4"], L["s_sm"],
        L["s_tp"], L["s_mm"], L["s_v"], L["s_s"], L["s_go"], L["s_da"])
    s_xc = L["s_xc"]
    s_sp = L["s_sp"]
    x_sb, gh_sb, V_sb, Wa, Wb = L["x_sb"], L["gh_sb"], L["V_sb"], L["Wa"], L["Wb"]
    lnx_roll, lnxT_win = L["lnx_roll"], L["lnxT_win"]
    ghT_win, gtT_win, gt_roll = L["ghT_win"], L["gtT_win"], L["gt_roll"]
    raw_roll, rawT_win, E_sb = L["raw_roll"], L["rawT_win"], L["E_sb"]
    b2_sb, t1T_sb = L["b2_sb"], L["t1T_sb"]
    bo_bf, bff_bf = L["bo_bf"], L["bff_bf"]
    clt_sb, cut_sb, idn_sb, mrhs_v = (L["clt_sb"], L["cut_sb"],
                                      L["idn_sb"], L["mrhs_v"])
    orow_bf, ocol_bf = L["orow_bf"], L["ocol_bf"]
    daT_views = L["daT_views"]
    lnst, nsq_sb, inv_sb = L["lnst"], L["nsq_sb"], L["inv_sb"]
    rs_sb, f2_sb = L["rs_sb"], L["f2_sb"]
    rq_bf, sir_bf, siv_sb, sr2_sb, zr_sb = (L["rq_bf"], L["sir_bf"],
                                            L["siv_sb"], L["sr2_sb"],
                                            L["zr_sb"])
    eps_sb = L["eps_sb"]
    b1_view, uaT_view, daT_dst = L["b1_view"], L["uaT_view"], L["daT_dst"]
    lu_view, ld_view, bg_view, bv_view = (L["lu_view"], L["ld_view"],
                                          L["bg_view"], L["bv_view"])
    PA, PB, PC, PD = L["PA"], L["PB"], L["PC"], L["PD"]
    x_hbm, out_hbm = L["x_hbm"], L["out_hbm"]
    pack_h, bias_h = L["pack_h"], L["bias_h"]
    pack_sb, bias_sb = L["pack_sb"], L["bias_sb"]
    sq_scr = gt_roll.ap()[:, 1, :]

    def bn_pass(ve, slot, src_lo, src_hi):
        ve.bn_stats(lnst.ap()[:, slot, 0:6], src_lo)
        ve.bn_stats(lnst.ap()[:, slot, 6:12], src_hi)
        ve.drain()
        return ve.bn_aggr(lnst.ap()[:, slot, 12:14], lnst.ap()[:, slot, 0:12])

    def rstd_pass(sc, slot):
        sc.activation(lnst.ap()[:, slot, 14:15], lnst.ap()[:, slot, 13:14],
                      AF.Ln, scale=1.0, bias=eps_sb.ap())
        sc.drain()
        return sc.activation(lnst.ap()[:, slot, 14:15],
                             lnst.ap()[:, slot, 14:15], AF.Exp, scale=-0.5)

    def apply_pass(ve, slot, dst, src):
        return ve.tensor_scalar(dst, src, lnst.ap()[:, slot, 12:13],
                                lnst.ap()[:, slot, 14:15],
                                ALU.subtract, ALU.mult)

    tok = {}

    def _g_loads(gp):
        for dst, src in ((pack_sb.ap(), pack_h), (bias_sb.ap(), bias_h)):
            gp.dma_start(dst, src,
                         max_dma_last_dim=8192).then_inc(s_sp.h, 16)
            s_sp.n += 16
        gp.dma_start(Wa.ap(), L["wgT_h"].rearrange("(k p) o -> p k o", p=128)
                     ).then_inc(s_w1.h, 16)
        s_w1.n += 16
        gp.dma_start(Wb.ap(), L["wvT_h"].rearrange("(k p) o -> p k o", p=128)
                     ).then_inc(s_w2.h, 16)
        s_w2.n += 16
        gp.wait_ge(s_tp.h, 16)
        xr = x_hbm.rearrange("(m p) f -> p m f", p=128)
        for c in (2, 3):
            gp.dma_start(x_sb.ap()[:, 2 * c:2 * c + 2, :],
                         xr[:, 2 * c:2 * c + 2, :]).then_inc(s_xc[c].h, 16)
            s_xc[c].n += 16
        for dst, src in ((uaT_view, L["uaT_h"]), (lu_view, L["lu_h"]),
                         (ld_view, L["ld_h"])):
            gp.dma_start(dst, src,
                         max_dma_last_dim=8192).then_inc(s_sm.h, 16)
            s_sm.n += 16
        for c in (4, 5, 6, 7):
            gp.dma_start(x_sb.ap()[:, 2 * c:2 * c + 2, :],
                         xr[:, 2 * c:2 * c + 2, :]).then_inc(s_xc[c].h, 16)
            s_xc[c].n += 16

    g(_g_loads)
    tok["sm"] = s_sm.n
    tok["sp"] = s_sp.n

    def _s_xload(sc):
        xr = x_hbm.rearrange("(m p) f -> p m f", p=128)
        for c in (0, 1):
            sc.dma_start(x_sb.ap()[:, 2 * c:2 * c + 2, :],
                         xr[:, 2 * c:2 * c + 2, :]).then_inc(s_xc[c].h, 16)
            s_xc[c].n += 16

    s(_s_xload)

    def _s_table(sc):
        sc.add_instruction(mybir.InstLoadActFuncSet(
            name=sc.bass.get_next_instruction_name(),
            act_func_set_id=ACT_SET_NL_EXP, ins=[], outs=[]))

    s(_s_table)

    def _v_ones(ve):
        ve.memset(fap(V_sb, 64, [[NT * H * 65, 128], [65 * H, NT], [65, H],
                                 [1, 1]]), 1.0)
        ve.memset(eps_sb.ap(), 1e-5).then_inc(s_v.h, 1)
        s_v.n += 1

    v(_v_ones)
    tok["vones"] = s_v.n

    tok_red1 = [0] * NT
    tok_ap1 = [0] * NT
    tok_ap1v = [0] * NT
    tok_tp1 = [0] * NT
    tok_gmm = [0] * NT
    tok_vmm = [0] * NT
    tok_gsq = [0] * NT
    tok_nsq = [0] * NT
    tok_inv = [0] * NT
    tok_ghev = [0] * NT
    tok_vev = [0] * NT
    tok_cmm = [0] * NT
    tok_dot = [0] * NT

    def kext(te, dst, bias_row, n):
        te.matmul(dst, orow_bf, bias_row[:, n * 512:(n + 1) * 512],
                  start=False, stop=True).then_inc(s_mm.h, 1)
        s_mm.n += 1

    def emit_ln1a(m):
        def _v_bn1(ve, m=m):
            _wait(ve, s_xc[m // 2], 16)
            bn_pass(ve, 2 + m % 2, x_sb.ap()[:, m, 0:512],
                    x_sb.ap()[:, m, 512:1024]).then_inc(s_v.h, 1)
            s_v.n += 1
            tok_red1[m] = s_v.n

        v(_v_bn1)

        def _s_rstd1(sc, m=m):
            _wait(sc, s_v, tok_red1[m])
            rstd_pass(sc, 2 + m % 2).then_inc(s_s.h, 1)
            s_s.n += 1
            tok_ap1[m] = s_s.n

        s(_s_rstd1)

    def emit_ln1b(m):
        def _v_ap1(ve, m=m):
            _wait(ve, s_s, tok_ap1[m])
            if m >= 2:
                _wait(ve, s_tp, tok_tp1[m - 2])
            apply_pass(ve, 2 + m % 2, lnx_roll.ap()[:, m % 2, :],
                       x_sb.ap()[:, m, :]).then_inc(s_v.h, 1)
            s_v.n += 1
            tok_ap1v[m] = s_v.n

        v(_v_ap1)

        def _y_tp1(sy, m=m):
            _wait(sy, s_v, tok_ap1v[m])
            if m >= 2:
                _wait(sy, s_mm, tok_vmm[m - 2])
            sy.dma_start_transpose(
                fap(lnxT_win, (m % 2) * 128,
                    [[NK * 256, 128], [256, NK], [1, 128]]),
                fap(lnx_roll, (m % 2) * D,
                    [[2 * D, 128], [128, NK], [1, 128]]),
            ).then_inc(s_tp.h, 16)
            s_tp.n += 16
            tok_tp1[m] = s_tp.n

        y(_y_tp1)

    emit_ln1a(0)
    emit_ln1b(0)

    for st in range(NT + 2):
        m2 = st + 1
        mp = st - 1
        qb = st - 2
        if 0 <= qb < NT:
            def _t_band(te, qb=qb):
                ps = PA if qb % 2 == 0 else PB
                _wait(te, s_s, tok_gsq[qb])
                _wait(te, s_v, tok_ghev[qb])
                for n in range(2):
                    dst = ps.ap()[:, n * 512:(n + 1) * 512]
                    first = True
                    if qb > 0:
                        te.matmul(dst, clt_sb,
                                  gh_sb.ap()[:, qb - 1, n * 512:(n + 1) * 512],
                                  start=True, stop=False)
                        first = False
                    te.matmul(dst, cut_sb,
                              gh_sb.ap()[:, qb, n * 512:(n + 1) * 512],
                              start=first, stop=True).then_inc(s_mm.h, 1)
                    s_mm.n += 1
                tok_cmm[qb] = s_mm.n

            t(_t_band)

        if 0 <= mp < NT:
            pg = PA if mp % 2 == 0 else PB
            pv = PC if mp % 2 == 0 else PD

            def _t_gv(te, mp=mp, pg=pg, pv=pv):
                _wait(te, s_tp, tok_tp1[mp])
                _wait(te, s_w1, 16)
                _wait(te, s_sp, tok["sp"])
                if mp >= 2:
                    _wait(te, s_v, tok_vev[mp - 2])
                for n in range(2):
                    dst = pg.ap()[:, n * 512:(n + 1) * 512]
                    for k in range(NK):
                        te.matmul(dst, lnxT_win.ap()[:, k, mp % 2, :],
                                  Wa.ap()[:, k, n * 512:(n + 1) * 512],
                                  start=(k == 0), stop=False)
                    kext(te, dst, bg_view, n)
                tok_gmm[mp] = s_mm.n
                _wait(te, s_w2, 16)
                for n in range(2):
                    dst = pv.ap()[:, n * 512:(n + 1) * 512]
                    for k in range(NK):
                        te.matmul(dst, lnxT_win.ap()[:, k, mp % 2, :],
                                  Wb.ap()[:, k, n * 512:(n + 1) * 512],
                                  start=(k == 0), stop=False)
                    kext(te, dst, bv_view, n)
                tok_vmm[mp] = s_mm.n

            t(_t_gv)

            def _s_gsq(sc, mp=mp, pg=pg):
                _wait(sc, s_mm, tok_gmm[mp])
                sc.activation(sq_scr, pg.ap(), AF.Square).then_inc(s_s.h, 1)
                s_s.n += 1
                tok_gsq[mp] = s_s.n

            s(_s_gsq)

        if m2 < NT:
            emit_ln1a(m2)

        if 0 <= mp < NT:
            def _v_nsq(ve, mp=mp):
                _wait(ve, s_s, tok_gsq[mp])
                if mp >= 2:
                    _wait(ve, s_s, tok_inv[mp - 2])
                ve.tensor_reduce(nsq_sb.ap()[:, mp % 2, :],
                                 gt_roll.ap()[:, 1, :].rearrange(
                                     "p (h d) -> p h d", h=16),
                                 AX.X, ALU.add).then_inc(s_v.h, 1)
                s_v.n += 1
                tok_nsq[mp] = s_v.n

            v(_v_nsq)

            def _s_inv(sc, mp=mp):
                _wait(sc, s_v, tok_nsq[mp])
                sc.activation(inv_sb.ap()[:, mp % 2, :],
                              nsq_sb.ap()[:, mp % 2, :], AF.Ln)
                sc.drain()
                sc.activation(inv_sb.ap()[:, mp % 2, :],
                              inv_sb.ap()[:, mp % 2, :], AF.Exp,
                              scale=-0.5).then_inc(s_s.h, 1)
                s_s.n += 1
                tok_inv[mp] = s_s.n

            s(_s_inv)

            def _v_ghev(ve, mp=mp, pg=pg):
                _wait(ve, s_s, tok_inv[mp])
                ve.tensor_tensor(
                    gh_sb.ap()[:, mp, :].rearrange("p (h d) -> p h d", h=16),
                    pg.ap().rearrange("p (h d) -> p h d", h=16),
                    fap(inv_sb, (mp % 2) * 16, [[32, 128], [1, 16], [0, 64]]),
                    ALU.mult).then_inc(s_v.h, 1)
                s_v.n += 1
                tok_ghev[mp] = s_v.n

            v(_v_ghev)

        if 0 <= qb < NT:
            def _v_dot(ve, dq=qb):
                ps = PA if dq % 2 == 0 else PB
                _wait(ve, s_mm, tok_cmm[dq])
                ve.tensor_tensor(raw_roll.ap()[:, 0, :], gh_sb.ap()[:, dq, :],
                                 ps.ap(), ALU.mult)
                ve.drain()
                ve.tensor_reduce(fap(rs_sb, dq, [[H * 16, 128], [16, H], [1, 1]]),
                                 raw_roll.ap()[:, 0, :].rearrange(
                                     "p (h d) -> p h d", h=16),
                                 AX.X, ALU.add).then_inc(s_v.h, 1)
                s_v.n += 1
                tok_dot[dq] = s_v.n

            v(_v_dot)

        if m2 < NT:
            emit_ln1b(m2)

        if 0 <= mp < NT:
            def _v_vev(ve, mp=mp, pv=pv):
                _wait(ve, s_mm, tok_vmm[mp])
                _wait(ve, s_v, tok["vones"])
                for n in range(2):
                    ve.tensor_copy(
                        fap(V_sb, mp * H * 65 + n * 8 * 65,
                            [[NT * H * 65, 128], [65, 8], [1, 64]]),
                        pv.ap()[:, n * 512:(n + 1) * 512]).then_inc(s_v.h, 1)
                    s_v.n += 1
                tok_vev[mp] = s_v.n

            v(_v_vev)

    def _g_w2(gp):
        _wait(gp, s_v, tok_vev[NT - 1])
        gp.dma_start(Wb.ap(), L["woT_h"].rearrange("(k p) o -> p k o", p=128)
                     ).then_inc(s_w3.h, 16)
        s_w3.n += 16
        gp.dma_start(Wa.ap(), L["wfT_h"].rearrange("(k p) o -> p k o", p=128)
                     ).then_inc(s_w4.h, 16)
        s_w4.n += 16

    g(_g_w2)

    def _t_l1(te):
        _wait(te, s_sm, tok["sm"])
        _wait(te, s_v, tok_vev[NT - 1])
        for n in range(2):
            te.matmul(PD.ap()[0:4, n * 512:(n + 1) * 512], lu_view,
                      uaT_view[:, n * 512:(n + 1) * 512], start=True,
                      stop=True).then_inc(s_mm.h, 1)
            s_mm.n += 1

    t(_t_l1)
    tok["b1"] = s_mm.n

    def _v_b1(ve):
        _wait(ve, s_mm, tok["b1"])
        ve.tensor_copy(b1_view, PD.ap()[0:4, :]).then_inc(s_v.h, 1)
        s_v.n += 1

    v(_v_b1)
    tok["b1e"] = s_v.n

    def _t_l2(te):
        _wait(te, s_v, tok["b1e"])
        for n in range(2):
            te.matmul(PD.ap()[0:100, n * 512:(n + 1) * 512], ld_view,
                      b1_view[:, n * 512:(n + 1) * 512], start=True,
                      stop=True).then_inc(s_mm.h, 1)
            s_mm.n += 1

    t(_t_l2)
    tok["b2"] = s_mm.n

    def _v_b2(ve):
        _wait(ve, s_mm, tok["b2"])
        ve.tensor_copy(b2_sb.ap(), PD.ap()[0:100, :]).then_inc(s_v.h, 1)
        s_v.n += 1

    v(_v_b2)
    tok["b2e"] = s_v.n


    def _v_fit1(ve):
        _wait(ve, s_v, tok_dot[NT - 1])
        ve.tensor_scalar(rs_sb.ap(), rs_sb.ap(), 1.0 / 4096.0, 1.0,
                         ALU.mult, ALU.add).then_inc(s_v.h, 1)
        s_v.n += 1

    v(_v_fit1)
    tok["fit1"] = s_v.n

    def _s_fit(sc):
        _wait(sc, s_v, tok["fit1"])
        sc.activation(rs_sb.ap(), rs_sb.ap(), AF.Ln)
        sc.drain()
        sc.activation(rs_sb.ap(), rs_sb.ap(), AF.Exp,
                      scale=-1.0).then_inc(s_s.h, 1)
        s_s.n += 1

    s(_s_fit)
    tok["fit2"] = s_s.n

    def _v_fit2(ve):
        _wait(ve, s_s, tok["fit2"])
        ve.tensor_reduce(siv_sb.ap(), rs_sb.ap(), AX.X,
                         ALU.add).then_inc(s_v.h, 1)
        s_v.n += 1

    v(_v_fit2)
    tok["siv"] = s_v.n

    def _s_rq(sc):
        _wait(sc, s_v, tok["siv"])
        sc.activation(rq_bf.ap(), siv_sb.ap(),
                      AF.Copy).then_inc(s_s.h, 1)
        s_s.n += 1

    s(_s_rq)
    tok["rq"] = s_s.n

    def _t_si(te):
        _wait(te, s_s, tok["rq"])
        _wait(te, s_v, tok_vev[14])
        te.matmul(PC.ap()[0:1, 0:16], ocol_bf, rq_bf.ap(), start=True,
                  stop=True).then_inc(s_mm.h, 1)
        s_mm.n += 1

    t(_t_si)
    tok["si"] = s_mm.n

    def _v_si(ve):
        _wait(ve, s_mm, tok["si"])
        ve.tensor_copy(sir_bf.ap(), PC.ap()[0:1, 0:16]).then_inc(s_v.h, 1)
        s_v.n += 1

    v(_v_si)
    tok["sir"] = s_v.n

    def _t_sib(te):
        _wait(te, s_v, tok["sir"])
        te.matmul(PC.ap()[:, 512:528], orow_bf, sir_bf.ap(), start=True,
                  stop=True).then_inc(s_mm.h, 1)
        s_mm.n += 1

    t(_t_sib)
    tok["sib"] = s_mm.n

    def _v_fcp(ve):
        _wait(ve, s_mm, tok["sib"])
        ve.tensor_copy(siv_sb.ap(), PC.ap()[:, 512:528]).then_inc(s_v.h, 1)
        s_v.n += 1

    v(_v_fcp)
    tok["fcp"] = s_v.n

    def _s_sr2(sc):
        _wait(sc, s_v, tok["fcp"])
        sc.activation(sr2_sb.ap(), siv_sb.ap(), AF.Ln, scale=2.0)
        sc.drain()
        sc.activation(sr2_sb.ap(), sr2_sb.ap(), AF.Exp,
                      scale=-1.0).then_inc(s_s.h, 1)
        s_s.n += 1

    s(_s_sr2)
    tok["sr2"] = s_s.n

    def _v_f2(ve):
        _wait(ve, s_s, tok["sr2"])
        ve.tensor_tensor(f2_sb.ap(), rs_sb.ap(),
                         fap(sr2_sb, 0, [[16, 128], [1, 16], [0, 16]]),
                         ALU.mult).then_inc(s_v.h, 1)
        s_v.n += 1

    v(_v_f2)
    tok["f2"] = s_v.n

    tok_ghT = [0] * NT
    tok_gt = [0] * NT
    tok_gtT = [0] * NT
    tok_smm = [[0] * 4 for _ in range(NT)]
    tok_exp = [[0] * 4 for _ in range(NT)]
    tok_av = [[0] * 4 for _ in range(NT)]
    tok_nrm = [[0] * 4 for _ in range(NT)]
    tok_rawT = [0] * NT
    tok_omm = [0] * NT
    tok_red2 = [0] * NT
    tok_ap2 = [0] * NT
    tok_ap2v = [0] * NT
    tok_ln2 = [0] * NT

    TP_DIMS = [[NK * 256, 128], [256, NK], [1, 128]]

    def ghT_slot(qb):
        s = qb % 3
        if s < 2:
            return fap(ghT_win, s * 128, TP_DIMS)
        return fap(lnxT_win, 0, TP_DIMS)

    def gtT_slot(qb):
        s = qb % 3
        if s < 2:
            return fap(gtT_win, s * 128, TP_DIMS)
        return fap(lnxT_win, 128, TP_DIMS)

    def emit_gt_and_T(qb):
        def _v_gt(ve, qb=qb):
            _wait(ve, s_v, tok["f2"])
            if qb >= 2:
                _wait(ve, s_tp, tok_gtT[qb - 2])
            ve.tensor_tensor(
                gt_roll.ap()[:, qb % 2, :].rearrange("p (h d) -> p h d", h=16),
                gh_sb.ap()[:, qb, :].rearrange("p (h d) -> p h d", h=16),
                fap(f2_sb, qb, [[H * 16, 128], [16, H], [0, 64]]),
                ALU.mult).then_inc(s_v.h, 1)
            s_v.n += 1
            tok_gt[qb] = s_v.n

        v(_v_gt)

        def _y_T(sy, qb=qb):
            if qb % 3 == 2:
                _wait(sy, s_mm, tok_vmm[NT - 1])
            if qb >= 3:
                _wait(sy, s_mm, tok_smm[qb - 2][3])
            sy.dma_start_transpose(
                ghT_slot(qb),
                fap(gh_sb, qb * D, [[NT * D, 128], [128, NK], [1, 128]]),
            ).then_inc(s_tp.h, 16)
            s_tp.n += 16
            tok_ghT[qb] = s_tp.n
            _wait(sy, s_v, tok_gt[qb])
            sy.dma_start_transpose(
                gtT_slot(qb),
                fap(gt_roll, (qb % 2) * D,
                    [[2 * D, 128], [128, NK], [1, 128]]),
            ).then_inc(s_tp.h, 16)
            s_tp.n += 16
            tok_gtT[qb] = s_tp.n

        y(_y_T)

    def emit_sc(qb, grp):
        def _t_sc(te, qb=qb, grp=grp):
            ps = PA if grp % 2 == 0 else PB
            _wait(te, s_tp, tok_ghT[qb])
            _wait(te, s_tp, tok_gtT[qb])
            if qb > 0:
                _wait(te, s_tp, tok_ghT[qb - 1])
            if grp >= 2:
                _wait(te, s_s, tok_exp[qb][grp - 2])
            elif qb > 0:
                _wait(te, s_s, tok_exp[qb - 1][grp + 2])
            else:
                _wait(te, s_v, tok_dot[NT - 1])
            hlist = [grp * 4, grp * 4 + 2, grp * 4 + 1, grp * 4 + 3]
            ghq = ghT_slot(qb)
            gtq = gtT_slot(qb)
            ghp = ghT_slot(qb - 1) if qb > 0 else None
            mms = []
            for i, h in enumerate(hlist):
                p0 = (h % 2) * 64
                lhs_same = ghq[p0:p0 + 64, h // 2, :]
                rhs_q = gtq[p0:p0 + 64, h // 2, :]
                if qb > 0:
                    mms.append((ps.ap()[:, i * 256:i * 256 + 128],
                                ghp[p0:p0 + 64, h // 2, :], rhs_q))
                mms.append((ps.ap()[:, i * 256 + 128:i * 256 + 256],
                            lhs_same, rhs_q))
            if qb > 0:
                for half in range(2):
                    mms.append((ps.ap()[:, half * 512:(half + 1) * 512],
                                idn_sb,
                                fap(pack_sb, PK_MR,
                                    [[PACK_N, 128], [0, 2], [1, 256]])))
            else:
                for i in range(4):
                    mms.append((ps.ap()[:, i * 256 + 128:i * 256 + 256],
                                idn_sb, mrhs_v[:, 128:256]))
            first = {0: True, 1: True}
            last = {}
            for j, (dst, _, _) in enumerate(mms):
                last[dst.offset // 512] = j
            for j, (dst, lhsT, rhs) in enumerate(mms):
                bank = dst.offset // 512
                bi = te.matmul(dst, lhsT, rhs, start=first[bank],
                               stop=(last[bank] == j))
                first[bank] = False
                if last[bank] == j:
                    bi.then_inc(s_mm.h, 1)
                    s_mm.n += 1
            tok_smm[qb][grp] = s_mm.n

        t(_t_sc)

    def emit_exp(qb, grp):
        def _s_exp(sc, qb=qb, grp=grp):
            ps = PA if grp % 2 == 0 else PB
            _wait(sc, s_mm, tok_smm[qb][grp])
            if qb == 0:
                sc.activation(
                    fap(E_sb, grp * 4 * 256 + 128,
                        [[H * 256, 128], [256, 4], [1, 128]]),
                    fap(ps, 128, [[1024, 128], [256, 4], [1, 128]]),
                    AF.Exp).then_inc(s_s.h, 1)
            else:
                sc.activation(
                    fap(E_sb, grp * 4 * 256, [[H * 256, 128], [1, 1024]]),
                    ps.ap(), AF.Exp).then_inc(s_s.h, 1)
            s_s.n += 1
            tok_exp[qb][grp] = s_s.n

        s(_s_exp)

    def emit_av(qb, grp):
        def _t_av(te, qb=qb, grp=grp):
            half = (grp % 2) * 512
            _wait(te, s_s, tok_exp[qb][grp])
            _wait(te, s_v, tok_vev[NT - 1])
            if grp >= 2:
                _wait(te, s_v, tok_nrm[qb][grp - 2])
            elif qb > 0:
                _wait(te, s_v, tok_nrm[qb - 1][grp + 2])
            elif grp == 0:
                _wait(te, s_v, tok["sir"])
            else:
                _wait(te, s_v, tok["f2"])
            n_mm = 4 * (2 if qb > 0 else 1)
            hlist = [grp * 4, grp * 4 + 2, grp * 4 + 1, grp * 4 + 3]
            j = 0
            for i, h in enumerate(hlist):
                dst = fap(PC, half + i * 65, [[1024, 128], [1, 65]])
                epos = grp * 4 + i
                if qb > 0:
                    te.matmul(dst, E_sb.ap()[:, epos, 0, :],
                              fap(V_sb, (qb - 1) * H * 65 + h * 65,
                                  [[NT * H * 65, 128], [1, 65]]),
                              start=(j == 0), stop=False)
                    j += 1
                bi = te.matmul(dst, E_sb.ap()[:, epos, 1, :],
                               fap(V_sb, qb * H * 65 + h * 65,
                                   [[NT * H * 65, 128], [1, 65]]),
                               start=(j == 0), stop=(j == n_mm - 1))
                j += 1
                if j == n_mm:
                    bi.then_inc(s_mm.h, 1)
                    s_mm.n += 1
            tok_av[qb][grp] = s_mm.n

        t(_t_av)

    tok_zr = [[0] * 4 for _ in range(NT)]

    def emit_zr(qb, pair):
        def _s_zr(sc, qb=qb, pair=pair):
            _wait(sc, s_mm, tok_av[qb][2 * pair + 1])
            if qb > 0:
                _wait(sc, s_v, tok_nrm[qb - 1][3])
            z = fap(lnst, (2 + pair) * 15, [[90, 128], [1, 8]])
            sc.activation(z, fap(PC, 64, [[1024, 128], [512, 2], [65, 4],
                                          [1, 1]]), AF.Ln)
            sc.drain()
            sc.activation(z, z, AF.Exp, scale=-1.0).then_inc(s_s.h, 1)
            s_s.n += 1
            tok_zr[qb][pair] = s_s.n

        s(_s_zr)

    def emit_nrm(qb, grp):
        def _v_nrm(ve, qb=qb, grp=grp):
            half = (grp % 2) * 512
            _wait(ve, s_s, tok_zr[qb][grp // 2])
            if qb >= 2 and grp == 0:
                _wait(ve, s_tp, tok_rawT[qb - 2])
            ve.tensor_tensor(
                fap(raw_roll, (qb % 2) * D + grp * 256,
                    [[2 * D, 128], [64, 2], [128, 2], [1, 64]]),
                fap(PC, half, [[1024, 128], [130, 2], [65, 2], [1, 64]]),
                fap(lnst, (2 + grp // 2) * 15 + (grp % 2) * 4,
                    [[90, 128], [2, 2], [1, 2], [0, 64]]),
                ALU.mult).then_inc(s_v.h, 1)
            s_v.n += 1
            tok_nrm[qb][grp] = s_v.n

        v(_v_nrm)

    def emit_rawT(qb):
        def _s_rawT(sc, qb=qb):
            _wait(sc, s_v, tok_nrm[qb][3])
            if qb >= 1:
                _wait(sc, s_mm, tok_omm[qb - 1])
            sc.dma_start_transpose(
                fap(rawT_win, 0, [[NK * 128, 128], [128, NK], [1, 128]]),
                fap(raw_roll, (qb % 2) * D, [[2 * D, 128], [128, NK], [1, 128]]),
            ).then_inc(s_tp.h, 16)
            s_tp.n += 16
            tok_rawT[qb] = s_tp.n

        s(_s_rawT)

    def emit_o(qb):
        def _t_o(te, qb=qb):
            _wait(te, s_tp, tok_rawT[qb])
            _wait(te, s_w3, 16)
            if qb == 0:
                _wait(te, s_v, tok["b2e"])
                _wait(te, s_v, tok_vev[15])
            else:
                _wait(te, s_v, tok_ap2v[qb - 1])
            for n in range(2):
                dst = PD.ap()[:, n * 512:(n + 1) * 512]
                for k in range(NK):
                    te.matmul(dst, rawT_win.ap()[:, k, 0, :],
                              Wb.ap()[:, k, n * 512:(n + 1) * 512],
                              start=(k == 0), stop=False)
                kext(te, dst, bo_bf, n)
            tok_omm[qb] = s_mm.n

        t(_t_o)

    def emit_ln2a(qb):
        def _v_bn2(ve, qb=qb):
            _wait(ve, s_mm, tok_omm[qb])
            bn_pass(ve, 4 + qb % 2, PD.ap()[:, 0:512],
                    PD.ap()[:, 512:1024]).then_inc(s_v.h, 1)
            s_v.n += 1
            tok_red2[qb] = s_v.n

        v(_v_bn2)

    def emit_ln2b(qb):
        def _s_r2(sc, qb=qb):
            _wait(sc, s_v, tok_red2[qb])
            rstd_pass(sc, 4 + qb % 2).then_inc(s_s.h, 1)
            s_s.n += 1
            tok_ap2[qb] = s_s.n

        s(_s_r2)

        def _v_ap2(ve, qb=qb):
            _wait(ve, s_s, tok_ap2[qb])
            apply_pass(ve, 4 + qb % 2, lnx_roll.ap()[:, qb % 2, :],
                       PD.ap()).then_inc(s_v.h, 1)
            s_v.n += 1
            tok_ap2v[qb] = s_v.n

        v(_v_ap2)

        def _v_add2(ve, qb=qb):
            ve.drain()
            ve.tensor_tensor(x_sb.ap()[:, qb, :], x_sb.ap()[:, qb, :],
                             lnx_roll.ap()[:, qb % 2, :],
                             ALU.add).then_inc(s_v.h, 1)
            s_v.n += 1
            tok_ln2[qb] = s_v.n

        v(_v_add2)

    tok_red3 = [0] * NT
    tok_ap3 = [0] * NT
    tok_ap3v = [0] * NT
    tok_tp2 = [0] * NT
    tok_nm3 = [0] * NT

    def emit_ln3a(m):
        def _v_bn3(ve, m=m):
            _wait(ve, s_v, tok_ln2[m])
            if m >= 2:
                _wait(ve, s_s, tok_ap3v[m - 2])
            bn_pass(ve, m % 2, x_sb.ap()[:, m, 0:512],
                    x_sb.ap()[:, m, 512:1024]).then_inc(s_v.h, 1)
            s_v.n += 1
            tok_red3[m] = s_v.n

        v(_v_bn3)

        def _s_rstd3(sc, m=m):
            _wait(sc, s_v, tok_red3[m])
            rstd_pass(sc, m % 2).then_inc(s_s.h, 1)
            s_s.n += 1
            tok_ap3[m] = s_s.n

        s(_s_rstd3)

        def _v_nm3(ve, m=m):
            _wait(ve, s_s, tok_ap3[m])
            ve.tensor_scalar(lnst.ap()[:, m % 2, 10:11],
                             lnst.ap()[:, m % 2, 12:13],
                             lnst.ap()[:, m % 2, 14:15], -1.0,
                             ALU.mult, ALU.mult).then_inc(s_v.h, 1)
            s_v.n += 1
            tok_nm3[m] = s_v.n

        v(_v_nm3)

    emit_gt_and_T(0)
    emit_gt_and_T(1)
    emit_gt_and_T(2)
    for qb in range(NT):
        emit_sc(qb, 0)
        emit_exp(qb, 0)
        emit_sc(qb, 1)
        emit_exp(qb, 1)
        if qb >= 2:
            emit_ln2b(qb - 2)
        emit_av(qb, 0)
        emit_sc(qb, 2)
        emit_exp(qb, 2)
        emit_av(qb, 1)
        emit_sc(qb, 3)
        emit_exp(qb, 3)
        emit_zr(qb, 0)
        emit_nrm(qb, 0)
        emit_nrm(qb, 1)
        if qb >= 1:
            emit_o(qb - 1)
        emit_av(qb, 2)
        emit_av(qb, 3)
        emit_zr(qb, 1)
        emit_nrm(qb, 2)
        emit_nrm(qb, 3)
        if qb >= 1:
            emit_ln2a(qb - 1)
        if 3 <= qb + 2 < NT:
            emit_gt_and_T(qb + 2)
        emit_rawT(qb)
        if qb == NT - 2:
            emit_ln3a(0)
            emit_ln3a(1)
    emit_ln2b(NT - 2)
    emit_o(NT - 1)
    emit_ln2a(NT - 1)
    emit_ln2b(NT - 1)

    def _g_daT(gp):
        _wait(gp, s_mm, tok_av[NT - 1][0])
        gp.dma_start(daT_dst, L["daT_h"]).then_inc(s_da.h, 16)
        s_da.n += 16

    g(_g_daT)

    tok_t1 = [0] * NT
    tok_t1e = [0] * NT
    tok_ymm = [0] * NT
    tok_red4 = [0] * NT
    tok_ap4 = [0] * NT
    tok_x2 = [0] * NT
    tok_ap4v = [0] * NT
    tok_nm4 = [0] * NT

    def emit_ln3b(m):
            def _s_ap3(sc, m=m):
                _wait(sc, s_v, tok_nm3[m])
                if m >= 2:
                    _wait(sc, s_tp, tok_tp2[m - 2])
                else:
                    _wait(sc, s_v, tok_ln2[14 + m])
                sc.activation(lnx_roll.ap()[:, m % 2, :], x_sb.ap()[:, m, :],
                              AF.Identity,
                              bias=lnst.ap()[:, m % 2, 10:11],
                              scale=lnst.ap()[:, m % 2, 14:15]
                              ).then_inc(s_s.h, 1)
                s_s.n += 1
                tok_ap3v[m] = s_s.n

            s(_s_ap3)

            def _y_tp2(sy, m=m):
                _wait(sy, s_s, tok_ap3v[m])
                if m < 2:
                    _wait(sy, s_mm, tok_smm[NT - 1][3])
                if m >= 2:
                    _wait(sy, s_mm, tok_ymm[m - 2])
                sy.dma_start_transpose(
                    fap(lnxT_win, (m % 2) * 128,
                        [[NK * 256, 128], [256, NK], [1, 128]]),
                    fap(lnx_roll, (m % 2) * D,
                        [[2 * D, 128], [128, NK], [1, 128]]),
                ).then_inc(s_tp.h, 16)
                s_tp.n += 16
                tok_tp2[m] = s_tp.n

            y(_y_tp2)

    emit_ln3b(0)

    for st in range(NT + 2):
        m3 = st + 1
        mp = st - 1
        q2 = st - 2
        if 0 <= mp < NT:
            py = PA if mp % 2 == 0 else PB

            def _t_t1(te, mp=mp):
                _wait(te, s_tp, tok_tp2[mp])
                _wait(te, s_da, 16)
                if mp == 0:
                    _wait(te, s_v, tok_nrm[NT - 1][2])
                else:
                    _wait(te, s_v, tok_t1e[mp - 1])
                for k in range(NK):
                    bi = te.matmul(PC.ap()[0:100, 0:128], daT_views[k],
                                   lnxT_win.ap()[:, k, mp % 2, :],
                                   start=(k == 0), stop=(k == NK - 1))
                bi.then_inc(s_mm.h, 1)
                s_mm.n += 1
                tok_t1[mp] = s_mm.n

            t(_t_t1)

            def _v_t1e(ve, mp=mp):
                _wait(ve, s_mm, tok_t1[mp])
                ve.tensor_copy(t1T_sb.ap(),
                               PC.ap()[0:100, 0:128]).then_inc(s_v.h, 1)
                s_v.n += 1
                tok_t1e[mp] = s_v.n

            v(_v_t1e)

            def _t_y(te, mp=mp, py=py):
                _wait(te, s_w4, 16)
                if mp < 2:
                    _wait(te, s_s, tok_exp[NT - 1][2 + mp])
                if mp >= 2:
                    _wait(te, s_s, tok_ap4v[mp - 2])
                for n in range(2):
                    dst = py.ap()[:, n * 512:(n + 1) * 512]
                    for k in range(NK):
                        te.matmul(dst, lnxT_win.ap()[:, k, mp % 2, :],
                                  Wa.ap()[:, k, n * 512:(n + 1) * 512],
                                  start=(k == 0), stop=False)
                    if n == 0:
                        _wait(te, s_v, tok_t1e[mp])
                    te.matmul(dst, t1T_sb.ap(),
                              b2_sb.ap()[:, n * 512:(n + 1) * 512],
                              start=False, stop=False)
                    kext(te, dst, bff_bf, n)
                tok_ymm[mp] = s_mm.n

            t(_t_y)

        if m3 < NT:
            if m3 >= 2:
                emit_ln3a(m3)
            emit_ln3b(m3)

        if 0 <= q2 < NT:
            pq = PA if q2 % 2 == 0 else PB

            def _v_bn4(ve, q2=q2, pq=pq):
                _wait(ve, s_mm, tok_ymm[q2])
                if q2 < 2:
                    _wait(ve, s_s, tok_zr[NT - 1][1])
                else:
                    _wait(ve, s_s, tok_ap4v[q2 - 2])
                bn_pass(ve, 2 + q2 % 2, pq.ap()[:, 0:512],
                        pq.ap()[:, 512:1024]).then_inc(s_v.h, 1)
                s_v.n += 1
                tok_red4[q2] = s_v.n

            v(_v_bn4)

            def _s_rstd4(sc, q2=q2):
                _wait(sc, s_v, tok_red4[q2])
                rstd_pass(sc, 2 + q2 % 2).then_inc(s_s.h, 1)
                s_s.n += 1
                tok_ap4[q2] = s_s.n

            s(_s_rstd4)

            def _v_nm4(ve, q2=q2):
                _wait(ve, s_s, tok_ap4[q2])
                sl = 2 + q2 % 2
                ve.tensor_scalar(lnst.ap()[:, sl, 10:11],
                                 lnst.ap()[:, sl, 12:13],
                                 lnst.ap()[:, sl, 14:15], -1.0,
                                 ALU.mult, ALU.mult).then_inc(s_v.h, 1)
                s_v.n += 1
                tok_nm4[q2] = s_v.n

            v(_v_nm4)

            def _s_ap4(sc, q2=q2, pq=pq):
                _wait(sc, s_v, tok_nm4[q2])
                sl = 2 + q2 % 2
                sc.activation(raw_roll.ap()[:, q2 % 2, :], pq.ap(),
                              AF.Identity,
                              bias=lnst.ap()[:, sl, 10:11],
                              scale=lnst.ap()[:, sl, 14:15]
                              ).then_inc(s_s.h, 1)
                s_s.n += 1
                tok_ap4v[q2] = s_s.n

            s(_s_ap4)

            def _v_x2(ve, q2=q2):
                _wait(ve, s_s, tok_ap4v[q2])
                ve.tensor_tensor(x_sb.ap()[:, q2, :], x_sb.ap()[:, q2, :],
                                 raw_roll.ap()[:, q2 % 2, :],
                                 ALU.add).then_inc(s_v.h, 1)
                s_v.n += 1
                tok_x2[q2] = s_v.n

            v(_v_x2)

            def _st(eng, q2=q2):
                _wait(eng, s_v, tok_x2[q2])
                eng.dma_start(
                    out_hbm.rearrange("(m p) f -> p m f", p=128)[:, q2, :],
                    x_sb.ap()[:, q2, :]).then_inc(s_go.h, 16)
                s_go.n += 16

            (g if q2 % 2 == 0 else y)(_st)

    def _g_final(gp):
        gp.wait_ge(s_go.h, 16 * NT)

    g(_g_final)


_BUILT = None


def _get_nc():
    global _BUILT
    if _BUILT is None:
        _BUILT = build_nc()
    return _BUILT


def kernel(L0_attn_x, L0_attn_Wg, L0_attn_bg, L0_attn_Wv, L0_attn_bv,
           L0_attn_Wo, L0_attn_bo, L0_ffn_W, L0_ffn_b, L0_ffn_lora_down,
           L0_ffn_lora_up, L0_ffn_down_aux, L0_ffn_up_aux):
    x = np.ascontiguousarray(np.asarray(L0_attn_x, dtype=np.float32))
    B = x.shape[0]
    f32 = lambda a: np.ascontiguousarray(np.asarray(a, dtype=np.float32))
    k = np.arange(128)[:, None]
    i = np.arange(128)[None, :]
    mpr = np.where(k >= i, 0.0, NEG).astype(np.float32)
    msa = np.where(k <= i, 0.0, NEG).astype(np.float32)
    pack = np.zeros((128, PACK_N), np.float32)
    pack[:, PK_CLT:PK_CLT + 128] = (k >= i).astype(np.float32)
    pack[:, PK_CUT:PK_CUT + 128] = (k <= i).astype(np.float32)
    pack[:, PK_IDN:PK_IDN + 128] = np.eye(128, dtype=np.float32)
    pack[:, PK_MR:PK_MR + 256] = np.concatenate([mpr, msa], axis=1)
    daT = f32(np.asarray(L0_ffn_down_aux).T)
    daT_packed = np.ascontiguousarray(
        daT.reshape(8, 128, 100).transpose(1, 0, 2).reshape(128, 800))
    biases = np.concatenate(
        [f32(L0_attn_bg), f32(L0_attn_bv), f32(L0_attn_bo),
         f32(L0_ffn_b)]).reshape(1, 4 * D)
    shared = {
        "wgT": f32(np.asarray(L0_attn_Wg).T),
        "wvT": f32(np.asarray(L0_attn_Wv).T),
        "woT": f32(np.asarray(L0_attn_Wo).T),
        "wfT": f32(np.asarray(L0_ffn_W).T),
        "pack": np.ascontiguousarray(pack),
        "biases": np.ascontiguousarray(biases),
        "lu": f32(L0_ffn_lora_up), "ld": f32(L0_ffn_lora_down),
        "uaT": f32(np.asarray(L0_ffn_up_aux).T),
        "daT": daT_packed,
    }
    in_maps = []
    for c in range(B):
        m = {"x": np.ascontiguousarray(x[c])}
        m.update(shared)
        in_maps.append(m)
    nc = _get_nc()
    res = run_bass_kernel_spmd(nc, in_maps, core_ids=list(range(B)))
    out = np.stack([np.asarray(r["out"]) for r in res.results], axis=0)
    return out.astype(np.float32)
